# version 13
# baseline (speedup 1.0000x reference)
import numpy as np
from contextlib import ExitStack

import concourse.bass as bass
import concourse.mybir as mybir
from concourse.bass_utils import run_bass_kernel_spmd

F32 = mybir.dt.float32
F16 = mybir.dt.float16
AF = mybir.ActivationFunctionType
ALU = mybir.AluOpType
AX = mybir.AxisListType.X

T_LOC = 2048
NT = 16
C = 1024
NJ = 8
HID = 2048
NE0 = 8
NE1 = 16
NH = 16
HD = 64
S_FULL = 8192
LN_EPS = 1e-5
XN_W = T_LOC + 2

ENGINES = ("sync", "gpsimd", "tensor", "vector", "scalar")
GROUPS = [[0, 1, 2, 3], [4, 5, 6, 7]]


def _tname(x):
    k = None
    if isinstance(x, tuple):
        x, k = x
    t = getattr(x, "tensor", x)
    return (getattr(t, "name", str(t)), k)


class Sem:
    __slots__ = ("h", "n", "name")

    def __init__(self, h, name):
        self.h, self.n, self.name = h, 0, name


class Prog:
    def __init__(self, nc, ctx):
        self.nc = nc
        self.ctx = ctx
        self.script = {e: [] for e in ENGINES}
        self.esem = {e: self.sem("sE_" + e) for e in ENGINES}
        self.dsem = {}
        self.wr = {}
        self.rd = {}
        self.pending = {e: [] for e in ENGINES}
        self.waited = {e: {} for e in ENGINES}
        self.all_sems = list(self.esem.values())

    def sem(self, name):
        h = self.ctx.enter_context(self.nc.semaphore(name))
        s = Sem(h, name)
        if hasattr(self, "all_sems"):
            self.all_sems.append(s)
        return s

    def dma_sem(self, key):
        if key not in self.dsem:
            self.dsem[key] = self.sem("q_" + str(key)[:12] + str(len(self.dsem)))
        return self.dsem[key]

    def op(self, eng, build, reads=(), writes=(), inc=True, dma=False, q=None,
           inc_n=None, extra=()):
        rkeys = [_tname(x) for x in reads]
        wkeys = [_tname(x) for x in writes]
        if q is not None:
            sem = q
        elif dma:
            nm0, k0 = wkeys[0]
            sem = self.dma_sem((nm0, k0) if nm0 in ("x1_d", "g1_d") else nm0)
        else:
            sem = self.esem[eng]
        if inc_n is None:
            inc_n = 16 if dma else 1
        waits = {}

        def add_wait(s, v):
            if v is None:
                raise RuntimeError(f"unresolved dep for {eng} op; producer must inc")
            if s.name.startswith("q_"):
                v = s.n
            if v <= 0:
                return
            if s is not self.esem[eng] and self.waited[eng].get(s.name, 0) >= v:
                return
            if s.name in waits:
                waits[s.name] = (s, max(waits[s.name][1], v))
            else:
                waits[s.name] = (s, v)

        def hit_keys(d, k):
            if not d:
                return []
            if k is None:
                return list(d.keys())
            return [kk for kk in (k, None) if kk in d]

        pe_skip = (eng == "tensor")

        def dep(s, v):
            if not (pe_skip and s is self.esem["tensor"]):
                add_wait(s, v)

        for name, k in rkeys:
            d = self.wr.get(name)
            for kk in hit_keys(d, k):
                dep(*d[kk])
        for name, k in wkeys:
            for peng, plist in self.pending.items():
                if peng == eng or (pe_skip and peng == "tensor"):
                    continue
                for _, pn, pk in plist:
                    if pn == name and (k is None or pk is None or pk == k):
                        raise RuntimeError(
                            f"write {name}[{k}] on {eng} conflicts with pending op on {peng}")
            d = self.wr.get(name)
            for kk in hit_keys(d, k):
                dep(*d[kk])
            rdn = self.rd.get(name)
            for kk in hit_keys(rdn, k):
                for s, v in rdn[kk].values():
                    dep(s, v)
        for s, v in extra:
            add_wait(s, v)

        if inc:
            sem.n += inc_n
            tgt = sem.n
        else:
            tgt = None
        wl = tuple(waits.values())
        for k2, (s, v) in waits.items():
            self.waited[eng][k2] = max(self.waited[eng].get(k2, 0), v)

        def f(e, build=build, wl=wl, sem=sem, tgt=tgt, inc=inc, inc_n=inc_n):
            for s, v in wl:
                e.wait_ge(s.h, v)
            ins = build(e)
            if inc:
                ins.then_inc(sem.h, inc_n)

        self.script[eng].append(f)

        def rec_read(name, k, s, v):
            self.rd.setdefault(name, {}).setdefault(k, {})[s.name] = (s, v)

        def rec_write(name, k, s, v):
            rdn = self.rd.get(name)
            if rdn is not None:
                if k is None:
                    self.rd.pop(name, None)
                else:
                    rdn.pop(k, None)
                    rdn.pop(None, None)
            if k is None:
                self.wr[name] = {None: (s, v)}
            else:
                dd = self.wr.setdefault(name, {})
                dd.pop(None, None)
                dd[k] = (s, v)

        if inc:
            if not dma and q is None:
                for kind, name, k in self.pending[eng]:
                    (rec_read if kind == "r" else rec_write)(name, k, sem, tgt)
                self.pending[eng] = []
            for name, k in rkeys:
                rec_read(name, k, sem, tgt)
            for name, k in wkeys:
                rec_write(name, k, sem, tgt)
        else:
            for name, k in rkeys:
                self.pending[eng].append(("r", name, k))
            for name, k in wkeys:
                self.pending[eng].append(("w", name, k))
                if k is None:
                    self.wr[name] = {None: (sem, None)}
                else:
                    dd = self.wr.setdefault(name, {})
                    dd.pop(None, None)
                    dd[k] = (sem, None)
        return tgt

    def wait_all(self, eng, sems=None):
        sems = sems if sems is not None else self.all_sems
        wl = []
        for s in sems:
            if s.n > 0 and self.waited[eng].get(s.name, 0) < s.n:
                wl.append((s, s.n))
                self.waited[eng][s.name] = s.n
        if not wl:
            return

        def f(e, wl=tuple(wl)):
            for s, v in wl:
                e.wait_ge(s.h, v)

        self.script[eng].append(f)

    def flush(self):
        for e in ENGINES:
            fns = self.script[e]
            if fns:
                getattr(self.blk, e)(lambda engobj, fns=fns: [f(engobj) for f in fns])
                self.script[e] = []


DBG_SHAPES = {
    "xnT": ([128, NJ * XN_W], F16),
    "srwkvT": ([128, NJ * T_LOC], F16),
    "s0": ([T_LOC, C], F16),
    "g0pre": ([128, NT * C], F16),
    "x1": ([T_LOC, C], F32),
    "QT": ([128, NJ * T_LOC], F16),
    "KT": ([128, NJ * T_LOC], F16),
    "VT": ([128, NJ * T_LOC], F16),
    "xn1T": ([128, NJ * T_LOC], F16),
    "ctxT": ([128, NJ * T_LOC], F16),
    "s1": ([128, NT * C], F16),
    "eku": ([128, NJ * T_LOC], F16),
    "Sab": ([128, 16], F32),
    "ainbin": ([128, 16], F32),
    "cw": ([128, NT * NE0], F32),
    "scoresT": ([128, NE1 * T_LOC], F16),
    "selT": ([16, T_LOC], F16),
}


class _St:
    def __init__(self, nc, ctx, p, d, dbg):
        self.nc, self.ctx, self.p, self.d, self.dbg = nc, ctx, p, d, dbg
        self.cc_sems = [p.sem(f"sCC{i}") for i in range(3)]

    def sb(self, name, shape, dt=F32, scope=None, side=None):
        return (scope or self.ctx).enter_context(
            self.nc.sbuf_tensor(name, shape, dt, side=side))

    def ps(self, name, shape, dt=F32, scope=None):
        return (scope or self.ctx).enter_context(self.nc.psum_tensor(name, shape, dt))

    def barrier(self):
        sems = [s for s in self.p.all_sems if not s.name.startswith("sCC")]
        for e in ENGINES:
            assert not self.p.pending[e], f"dangling non-inc ops on {e}"
            self.p.wait_all(e, sems)
        self.p.flush()

    def dbg_store(self, name, src_ap):
        if name in self.dbg:
            self.p.op("sync", lambda e, ap=self.dbg[name], s=src_ap: e.dma_start(out=ap, in_=s),
                      reads=[src_ap], writes=[self.dbg[name]], dma=True)



CONST_VALS = [LN_EPS, -1.0, 1.0 / S_FULL, 0.0, 1.0, 0.125]


def emit_setup(st):
    p, d = st.p, st.d
    st.ident = st.sb("ident", [128, 128], F16)
    st.csts = st.sb("csts", [128, 8], F32)
    st.idblk = st.sb("idblk", [128, 64], F16)
    p.op("gpsimd", lambda e: e.dma_start(out=st.ident[:], in_=d["idn"]),
         reads=[d["idn"]], writes=[st.ident], dma=True)
    p.op("sync", lambda e: e.dma_start(out=st.csts[:, 0:len(CONST_VALS)], in_=d["cst"]),
         reads=[d["cst"]], writes=[st.csts], dma=True)
    p.op("gpsimd", lambda e: e.dma_start(out=st.idblk[:], in_=d["idn_blk"]),
         reads=[d["idn_blk"]], writes=[st.idblk], dma=True)
    for i, v in enumerate(CONST_VALS):
        st.nc.const_aps.register(F32, v, st.csts[:, i:i + 1])
    st.barrier()


def emit_l0_alloc(st, scope):
    p, d = st.p, st.d
    st.iota = st.sb("iota", [128, T_LOC], F16, scope=scope)
    st.iotar = st.sb("iotar", [128, T_LOC], F16, scope=scope)
    p.op("gpsimd", lambda e: e.dma_start(out=st.iota[:], in_=d["iota_t"]),
         reads=[d["iota_t"]], writes=[st.iota], dma=True)
    p.op("gpsimd", lambda e: e.dma_start(out=st.iotar[:], in_=d["iotar_t"]),
         reads=[d["iotar_t"]], writes=[st.iotar], dma=True)
    st.xnT = st.sb("xnT", [128, NJ * XN_W], F16, scope=scope)
    st.rstd0 = st.sb("rstd0", [128, 17], F32, scope=scope)
    st.nmr0 = st.sb("nmr0", [128, 17], F32, scope=scope)


def _ln_stats(st, sc, tag, src_fn, n_tiles, rstd, nmr):
    p = st.p
    sums = st.sb(tag + "_sums", [128, 17], F32, scope=sc)
    sqs = st.sb(tag + "_sqs", [128, 17], F32, scope=sc)
    xt = [st.sb(f"{tag}_x{i}", [128, C], F32, scope=sc) for i in range(4)]
    sqscr = [st.sb(f"{tag}_q{i}", [128, C], F32, scope=sc) for i in range(2)]
    p.op("vector", lambda e: e.memset(sums[:], 0.0), writes=[sums])
    p.op("vector", lambda e: e.memset(sqs[:], 0.0), writes=[sqs])
    for i in range(n_tiles):
        b = i % 4
        src, nr = src_fn(i)
        p.op("sync", lambda e, s=src, dd=xt[b], nr=nr: e.dma_start(out=dd[0:nr, :], in_=s),
             reads=[(src, i)], writes=[xt[b]], dma=True)
        p.op("vector", lambda e, b=b, i=i, nr=nr: e.reduce_sum(
            sums[0:nr, i:i + 1], xt[b][0:nr, :], axis=AX),
            reads=[xt[b]], writes=[sums])
        b2 = i % 2
        p.op("scalar", lambda e, b=b, b2=b2, i=i, nr=nr: e.activation(
            sqscr[b2][0:nr, :], xt[b][0:nr, :], AF.Square, accum_out=sqs[0:nr, i:i + 1]),
            reads=[xt[b]], writes=[sqscr[b2], sqs])
    n = n_tiles
    mean = st.sb(tag + "_mean", [128, 17], F32, scope=sc)
    var = st.sb(tag + "_var", [128, 17], F32, scope=sc)
    p.op("vector", lambda e, n=n: e.tensor_scalar(mean[:, 0:n], sums[:, 0:n], 1.0 / C, None, ALU.mult),
         reads=[sums], writes=[mean])
    p.op("vector", lambda e, n=n: e.scalar_tensor_tensor(var[:, 0:n], mean[:, 0:n], -1.0, mean[:, 0:n], ALU.mult, ALU.mult),
         reads=[mean], writes=[var])
    p.op("vector", lambda e, n=n: e.scalar_tensor_tensor(var[:, 0:n], sqs[:, 0:n], 1.0 / C, var[:, 0:n], ALU.mult, ALU.add),
         reads=[sqs, var], writes=[var])
    p.op("scalar", lambda e, n=n: e.activation(var[:, 0:n], var[:, 0:n], AF.Sqrt, bias=LN_EPS),
         reads=[var], writes=[var])
    p.op("vector", lambda e, n=n: e.reciprocal(rstd[:, 0:n], var[:, 0:n]),
         reads=[var], writes=[rstd])
    p.op("vector", lambda e, n=n: e.scalar_tensor_tensor(nmr[:, 0:n], mean[:, 0:n], -1.0, rstd[:, 0:n], ALU.mult, ALU.mult),
         reads=[mean, rstd], writes=[nmr])


def emit_ln0(st):
    p, d = st.p, st.d
    with ExitStack() as sc:
        def src(i):
            if i < 16:
                return d["x_main"][128 * i:128 * (i + 1), :], 128
            return d["x_halo"], 2

        _ln_stats(st, sc, "ln0", src, 17, st.rstd0, st.nmr0)
        xt = [st.sb(f"p0_x{i}", [128, C], F32, scope=sc) for i in range(4)]
        xn16 = [st.sb(f"p0_n{i}", [128, C], F16, scope=sc) for i in range(2)]
        pst = [st.ps(f"p0_t{i}", [128, 8 * 128], F16, scope=sc) for i in range(2)]
        xnr = st.xnT.ap().rearrange("p (j t) -> p j t", j=NJ)
        for i in range(17):
            b = i % 4
            sap, nr = src(i)
            p.op("sync", lambda e, s=sap, dd=xt[b], nr=nr: e.dma_start(out=dd[0:nr, :], in_=s),
                 reads=[sap], writes=[xt[b]], dma=True)
            b2 = i % 2
            p.op("vector", lambda e, b=b, b2=b2, i=i, nr=nr: e.tensor_scalar(
                xn16[b2][0:nr, :], xt[b][0:nr, :], st.rstd0[0:nr, i:i + 1], st.nmr0[0:nr, i:i + 1],
                ALU.mult, ALU.add),
                reads=[xt[b], st.rstd0, st.nmr0], writes=[xn16[b2]])
            for j in range(NJ):
                p.op("tensor", lambda e, b2=b2, j=j, nr=nr: e.transpose(
                    pst[b2][:, 128 * j:128 * j + nr], xn16[b2][0:nr, 128 * j:128 * (j + 1)],
                    st.ident[0:nr, 0:nr]),
                    reads=[xn16[b2], st.ident], writes=[pst[b2]], inc=(j == NJ - 1))
            if i < 16:
                dst = xnr[:, :, 1 + 128 * i: 1 + 128 * (i + 1)]
                srcap = pst[b2].ap().rearrange("p (j t) -> p j t", j=NJ)
            else:
                dst = xnr[:, :, 0:2050:2049]
                srcap = pst[b2].ap().rearrange("p (j t) -> p j t", j=NJ)[:, :, 0:2]
            p.op("vector", lambda e, dst=dst, srcap=srcap, b2=b2: e.tensor_copy(dst, srcap),
                 reads=[pst[b2]], writes=[st.xnT])
        st.dbg_store("xnT", st.xnT[:])
    st.barrier()


def _load_wT(st, bufs, wname, j, slot):
    p, d = st.p, st.d
    w16, pswt, wT = bufs
    p.op("gpsimd", lambda e, wname=wname, j=j, slot=slot: e.dma_start(
        out=w16[slot][:], in_=d[wname][128 * j:128 * (j + 1), :]),
        reads=[d[wname]], writes=[w16[slot]], dma=True)
    for j2 in range(NJ):
        p.op("tensor", lambda e, slot=slot, j2=j2: e.transpose(
            pswt[slot][:, 128 * j2:128 * (j2 + 1)], w16[slot][:, 128 * j2:128 * (j2 + 1)], st.ident[:]),
            reads=[w16[slot], st.ident], writes=[pswt[slot]], inc=(j2 == NJ - 1))
    p.op("vector", lambda e, slot=slot: e.tensor_copy(wT[slot][:], pswt[slot][:]),
         reads=[pswt[slot]], writes=[wT[slot]])
    return wT[slot]


def emit_l0_attn(st):
    p, d = st.p, st.d
    xn = st.xnT.ap().rearrange("p (j t) -> p j t", j=NJ)
    with ExitStack() as sc:
        sb = lambda nm, sh, dt=F32: st.sb(nm, sh, dt, scope=sc)
        vecs = sb("l0_vecs", [128, 40], F32)
        for i, nm in enumerate(["mixk_fm", "mixv_fm", "mixr_fm", "sd_fm", "sf_fm"]):
            p.op("sync", lambda e, nm=nm, i=i: e.dma_start(out=vecs[:, 8 * i:8 * (i + 1)], in_=d[nm]),
                 reads=[d[nm]], writes=[vecs], dma=True)
        mixk, mixv, mixr = vecs[:, 0:8], vecs[:, 8:16], vecs[:, 16:24]
        sdf, sff = vecs[:, 24:32], vecs[:, 32:40]
        aux = sb("l0_aux", [128, 64], F32)
        w8, u8, ew = aux[:, 0:8], aux[:, 8:16], aux[:, 16:24]
        expu, expnu = aux[:, 24:32], aux[:, 32:40]
        ain, binn = aux[:, 40:48], aux[:, 48:56]
        p.op("vector", lambda e: e.tensor_scalar(w8, sdf, 1.0 / S_FULL, None, ALU.mult),
             reads=[vecs], writes=[aux])
        p.op("vector", lambda e: e.tensor_scalar(u8, sff, 1.0 / S_FULL, None, ALU.mult),
             reads=[vecs, aux], writes=[aux])
        p.op("scalar", lambda e: e.activation(ew, w8, AF.Exp), reads=[aux], writes=[aux])
        p.op("scalar", lambda e: e.activation(expu, u8, AF.Exp), reads=[aux], writes=[aux])
        p.op("scalar", lambda e: e.activation(expnu, u8, AF.Exp, scale=-1.0),
             reads=[aux], writes=[aux])

        eku = sb("ekuT", [128, NJ * T_LOC], F16)
        vT = sb("vTbuf", [128, NJ * T_LOC], F16)
        srT = sb("srT", [128, NJ * T_LOC], F16)
        Sab = sb("Sab", [128, 16], F32)
        ekur = eku.ap().rearrange("p (j t) -> p j t", j=NJ)
        vTr = vT.ap().rearrange("p (j t) -> p j t", j=NJ)
        srr = srT.ap().rearrange("p (j t) -> p j t", j=NJ)
        wbufs = ([sb(f"l0_w16_{i}", [128, C], F16) for i in range(2)],
                 [st.ps(f"l0_pswt{i}", [128, NJ * 128], F16, scope=sc) for i in range(2)],
                 [sb(f"l0_wT{i}", [128, NJ * 128], F16) for i in range(2)])
        psk = st.ps("l0_psk", [128, 512], F32, scope=sc)
        psv = st.ps("l0_psv", [128, 512], F32, scope=sc)

        with ExitStack() as sc2:
            sb2 = lambda nm, sh, dt=F32: st.sb(nm, sh, dt, scope=sc2)
            mixT = st.sb("mixT", [128, NJ * T_LOC], F16, scope=sc2)
            mxr = mixT.ap().rearrange("p (j t) -> p j t", j=NJ)
            dscr = st.sb("l0_dscr", [128, T_LOC], F16, scope=sc2)

            def emit_mix(mixcol, dstT, dsc):
                dstr = dstT.ap().rearrange("p (j t) -> p j t", j=NJ)
                for j in range(NJ):
                    main = xn[:, j, 1:1 + T_LOC]
                    xx = xn[:, j, 0:T_LOC] if j < 2 else (xn[:, j, 2:2 + T_LOC] if j < 4 else main)
                    if j < 4:
                        p.op("vector", lambda e, main=main, xx=xx, dsc=dsc: e.tensor_sub(dsc[:], main, xx),
                             reads=[st.xnT], writes=[dsc])
                        p.op("vector", lambda e, j=j, xx=xx, mixcol=mixcol, dstr=dstr, dsc=dsc, dstT=dstT: e.scalar_tensor_tensor(
                            dstr[:, j, :], dsc[:], mixcol[:, j:j + 1], xx, ALU.mult, ALU.add),
                            reads=[dsc, vecs, st.xnT], writes=[dstT])
                    else:
                        p.op("vector", lambda e, j=j, main=main, dstr=dstr, dstT=dstT: e.tensor_copy(dstr[:, j, :], main),
                             reads=[st.xnT], writes=[dstT])

            def mm8(psum, wTbuf, rhsr, tc, extra_reads):
                for cj in range(8):
                    p.op("tensor", lambda e, cj=cj, tc=tc, psum=psum, wTbuf=wTbuf, rhsr=rhsr: e.matmul(
                        psum[:, :], wTbuf[:, 128 * cj:128 * (cj + 1)],
                        rhsr[:, cj, 512 * tc:512 * (tc + 1)], start=(cj == 0), stop=(cj == 7)),
                        reads=[wTbuf] + extra_reads, writes=[psum], inc=(cj == 7))

            emit_mix(mixk, mixT, dscr)
            wts = {0: _load_wT(st, wbufs, "Wk", 0, 0)}
            for j in range(NJ):
                wTk = wts.pop(j)
                if j + 1 < NJ:
                    wts[j + 1] = _load_wT(st, wbufs, "Wk", j + 1, (j + 1) % 2)
                for tc in range(4):
                    mm8(psk, wTk, mxr, tc, [mixT])
                    p.op("scalar", lambda e, j=j, tc=tc: e.activation(
                        ekur[:, j, 512 * tc:512 * (tc + 1)], psk[:, :], AF.Exp, bias=u8[:, j:j + 1]),
                        reads=[psk, aux], writes=[eku])
            dscr2 = st.sb("l0_dscr2", [128, T_LOC], F16, scope=sc2)
            emit_mix(mixv, srT, dscr2)
            srr_mix = srT.ap().rearrange("p (j t) -> p j t", j=NJ)
            wts = {0: _load_wT(st, wbufs, "Wv", 0, 0)}
            for j in range(NJ):
                wTv = wts.pop(j)
                if j + 1 < NJ:
                    wts[j + 1] = _load_wT(st, wbufs, "Wv", j + 1, (j + 1) % 2)
                for tc in range(4):
                    mm8(psv, wTv, srr_mix, tc, [srT])
                    p.op("vector", lambda e, j=j, tc=tc: e.tensor_copy(
                        vTr[:, j, 512 * tc:512 * (tc + 1)], psv[:, :]),
                        reads=[psv], writes=[vT])
            ekt = sb2("l0_ek", [128, T_LOC], F16)
            ppt = sb2("l0_pp", [128, T_LOC], F16)
            dmat = sb2("l0_dm16", [128, T_LOC], F16)
            junk = dscr
            for j in range(NJ):
                p.op("vector", lambda e, j=j: e.tensor_scalar(
                    ekt[:], ekur[:, j, :], expnu[:, j:j + 1], None, ALU.mult),
                    reads=[eku, aux], writes=[ekt])
                p.op("vector", lambda e, j=j: e.tensor_mul(ppt[:], ekt[:], vTr[:, j, :]),
                     reads=[ekt, vT], writes=[ppt])
                p.op("scalar", lambda e, j=j: e.activation(dmat[:], st.iotar[:], AF.Exp, scale=w8[:, j:j + 1]),
                     reads=[st.iotar, aux], writes=[dmat])
                p.op("vector", lambda e, j=j: e.scalar_tensor_tensor(
                    junk[:], ppt[:], 1.0, dmat[:], ALU.mult, ALU.mult, accum_out=Sab[:, j:j + 1]),
                    reads=[ppt, dmat], writes=[junk, Sab])
                p.op("vector", lambda e, j=j: e.scalar_tensor_tensor(
                    junk[:], ekt[:], 1.0, dmat[:], ALU.mult, ALU.mult, accum_out=Sab[:, 8 + j:9 + j]),
                    reads=[ekt, dmat], writes=[junk, Sab])
            st.dbg_store("eku", eku[:])
            st.dbg_store("Sab", Sab[:])

            p.op("sync", lambda e: e.dma_start(out=d["ccS_in"], in_=Sab[:]),
                 reads=[Sab], writes=[d["ccS_in"]], dma=True)
            p.op("gpsimd", lambda e: e.collective_compute(
                "AllGather", ALU.bypass, replica_groups=GROUPS,
                ins=[d["ccS_in"].opt()], outs=[d["ccS_out"].opt()]),
                reads=[d["ccS_in"]], writes=[d["ccS_out"]], q=st.cc_sems[0], inc_n=1)
            Sgf = sb2("Sgf", [128, 64], F32)
            for r in range(4):
                p.op("gpsimd", lambda e, r=r: e.dma_start(
                    out=Sgf[:, 16 * r:16 * (r + 1)], in_=d["ccS_out"][128 * r:128 * (r + 1), :]),
                    reads=[d["ccS_out"]], writes=[Sgf], dma=True)
            qrow = sb2("qrow", [1, 8], F16)
            p.op("gpsimd", lambda e: e.dma_start(out=qrow[0:1, 0:4], in_=d["qmask"]),
                 reads=[d["qmask"]], writes=[qrow], dma=True)
            p.op("gpsimd", lambda e: e.dma_start(out=qrow[0:1, 4:8], in_=d["qcoef"]),
                 reads=[d["qcoef"]], writes=[qrow], dma=True)
            ones128 = sb2("ones128", [1, 128], F16)
            p.op("vector", lambda e: e.memset(ones128[:], 1.0), writes=[ones128])
            psq8 = st.ps("l0_psq8", [128, 8], F32, scope=sc2)
            p.op("tensor", lambda e: e.matmul(psq8[:, :], ones128[0:1, :], qrow[0:1, :],
                                              start=True, stop=True),
                 reads=[ones128, qrow], writes=[psq8])
            qmb = aux[:, 56:60]
            qcb = aux[:, 60:64]
            p.op("vector", lambda e: e.tensor_copy(aux[:, 56:64], psq8[:, :]),
                 reads=[psq8], writes=[aux])
            sc8 = sb2("l0_sc8", [128, 8], F32)
            tm8 = sb2("l0_tm8", [128, 8], F32)
            p.op("vector", lambda e: e.memset(aux[:, 40:56], 0.0), writes=[aux])
            for r in range(4):
                p.op("vector", lambda e, r=r: e.tensor_scalar(
                    sc8[:], w8, qcb[:, r:r + 1], None, ALU.mult),
                    reads=[aux], writes=[sc8])
                p.op("scalar", lambda e: e.activation(sc8[:], sc8[:], AF.Exp),
                     reads=[sc8], writes=[sc8])
                p.op("vector", lambda e, r=r: e.tensor_scalar(
                    sc8[:], sc8[:], qmb[:, r:r + 1], None, ALU.mult),
                    reads=[sc8, aux], writes=[sc8])
                p.op("vector", lambda e, r=r: e.tensor_mul(tm8[:], Sgf[:, 16 * r:16 * r + 8], sc8[:]),
                     reads=[Sgf, sc8], writes=[tm8])
                p.op("vector", lambda e: e.tensor_add(ain, ain, tm8[:]),
                     reads=[tm8, aux], writes=[aux])
                p.op("vector", lambda e, r=r: e.tensor_mul(tm8[:], Sgf[:, 16 * r + 8:16 * r + 16], sc8[:]),
                     reads=[Sgf, sc8], writes=[tm8])
                p.op("vector", lambda e: e.tensor_add(binn, binn, tm8[:]),
                     reads=[tm8, aux], writes=[aux])
            st.dbg_store("ainbin", aux[:, 40:56])

            emit_mix(mixr, mixT, dscr)
            wts = {0: _load_wT(st, wbufs, "Wr", 0, 0)}
            for j in range(NJ):
                wTr = wts.pop(j)
                if j + 1 < NJ:
                    wts[j + 1] = _load_wT(st, wbufs, "Wr", j + 1, (j + 1) % 2)
                for tc in range(4):
                    mm8(psk, wTr, mxr, tc, [mixT])
                    p.op("scalar", lambda e, j=j, tc=tc: e.activation(
                        srr[:, j, 512 * tc:512 * (tc + 1)], psk[:, :], AF.Sigmoid),
                        reads=[psk], writes=[srT])

        st.barrier()
        ekt = sb("l02_ek", [128, T_LOC], F16)
        ppt = sb("l02_pp", [128, T_LOC], F16)
        dmat = sb("l02_dm", [128, T_LOC], F32)
        Abuf = sb("Abuf", [128, T_LOC + 1], F32)
        Bbuf = sb("Bbuf", [128, T_LOC + 1], F32)
        ewb = sb("ewb", [128, T_LOC], F32)
        numb = sb("numb", [128, T_LOC], F32)
        denb = sb("denb", [128, T_LOC], F32)
        p.op("vector", lambda e: e.memset(Abuf[:, 0:1], 0.0), writes=[Abuf])
        p.op("vector", lambda e: e.memset(Bbuf[:, 0:1], 0.0), reads=[Abuf], writes=[Bbuf])
        for j in range(NJ):
            p.op("vector", lambda e, j=j: e.tensor_scalar(
                ekt[:], ekur[:, j, :], expnu[:, j:j + 1], None, ALU.mult),
                reads=[eku, aux], writes=[ekt])
            p.op("vector", lambda e, j=j: e.tensor_mul(ppt[:], ekt[:], vTr[:, j, :]),
                 reads=[ekt, vT], writes=[ppt])
            p.op("vector", lambda e, j=j: e.tensor_scalar(
                ewb[:], st.iota[:], 0.0, ew[:, j:j + 1], ALU.mult, ALU.add),
                reads=[st.iota, aux], writes=[ewb])
            p.op("vector", lambda e: e.tensor_tensor_scan(
                Abuf[:, 1:], ewb[:], ppt[:], 0.0, ALU.mult, ALU.add),
                reads=[ewb, ppt], writes=[Abuf])
            p.op("vector", lambda e: e.tensor_tensor_scan(
                Bbuf[:, 1:], ewb[:], ekt[:], 0.0, ALU.mult, ALU.add),
                reads=[ewb, ekt], writes=[Bbuf])
            p.op("scalar", lambda e, j=j: e.activation(dmat[:], st.iota[:], AF.Exp, scale=w8[:, j:j + 1]),
                 reads=[st.iota, aux], writes=[dmat])
            p.op("vector", lambda e, j=j: e.scalar_tensor_tensor(
                numb[:], dmat[:], ain[:, j:j + 1], Abuf[:, 0:T_LOC], ALU.mult, ALU.add),
                reads=[dmat, aux, Abuf], writes=[numb])
            p.op("vector", lambda e, j=j: e.scalar_tensor_tensor(
                numb[:], ppt[:], expu[:, j:j + 1], numb[:], ALU.mult, ALU.add),
                reads=[ppt, aux, numb], writes=[numb])
            p.op("vector", lambda e, j=j: e.scalar_tensor_tensor(
                denb[:], dmat[:], binn[:, j:j + 1], Bbuf[:, 0:T_LOC], ALU.mult, ALU.add),
                reads=[dmat, aux, Bbuf], writes=[denb])
            p.op("vector", lambda e, j=j: e.scalar_tensor_tensor(
                denb[:], ekt[:], expu[:, j:j + 1], denb[:], ALU.mult, ALU.add),
                reads=[ekt, aux, denb], writes=[denb])
            p.op("scalar", lambda e: e.activation(dmat[:], denb[:], AF.Ln),
                 reads=[denb], writes=[dmat])
            p.op("scalar", lambda e: e.activation(dmat[:], dmat[:], AF.Exp, scale=-1.0),
                 reads=[dmat], writes=[dmat])
            p.op("vector", lambda e: e.tensor_mul(numb[:], numb[:], dmat[:]),
                 reads=[numb, dmat], writes=[numb])
            p.op("vector", lambda e, j=j: e.tensor_mul(vTr[:, j, :], numb[:], srr[:, j, :]),
                 reads=[numb, srT], writes=[vT])
        st.dbg_store("srwkvT", vT[:])

        s0T = eku
        s0r = s0T.ap().rearrange("p (j t) -> p j t", j=NJ)
        wts = {0: _load_wT(st, wbufs, "Wo", 0, 0)}
        for j in range(NJ):
            wTo = wts.pop(j)
            if j + 1 < NJ:
                wts[j + 1] = _load_wT(st, wbufs, "Wo", j + 1, (j + 1) % 2)
            for tc in range(4):
                for cj in range(8):
                    p.op("tensor", lambda e, cj=cj, tc=tc, wTo=wTo: e.matmul(
                        psv[:, :], wTo[:, 128 * cj:128 * (cj + 1)],
                        vTr[:, cj, 512 * tc:512 * (tc + 1)], start=(cj == 0), stop=(cj == 7)),
                        reads=[wTo, vT], writes=[psv], inc=(cj == 7))
                p.op("vector", lambda e, j=j, tc=tc: e.tensor_copy(
                    s0r[:, j, 512 * tc:512 * (tc + 1)], psv[:, :]),
                    reads=[psv], writes=[s0T])
        pst2 = st.ps("l0_pst2", [128, 1024], F16, scope=sc)
        s0tile = [sb(f"s0t{i}", [128, C], F16) for i in range(2)]
        for i in range(NT):
            b = i % 2
            for j in range(NJ):
                p.op("tensor", lambda e, i=i, j=j: e.transpose(
                    pst2[:, 128 * j:128 * (j + 1)], s0r[:, j, 128 * i:128 * (i + 1)], st.ident[:]),
                    reads=[s0T, st.ident], writes=[pst2], inc=(j == NJ - 1))
            p.op("vector", lambda e, i=i, b=b: e.tensor_copy(s0tile[b][:], pst2[:]),
                 reads=[pst2], writes=[s0tile[b]])
            p.op("sync", lambda e, i=i, b=b: e.dma_start(
                out=d["s0_d"][128 * i:128 * (i + 1), :], in_=s0tile[b][:]),
                reads=[s0tile[b]], writes=[(d["s0_d"], i)], dma=True)
        st.dbg_store("s0", d["s0_d"])
    st.barrier()



def emit_moe(st):
    p, d = st.p, st.d
    xnr = st.xnT.ap().rearrange("p (j t) -> p j t", j=NJ)

    def xns(j, t0, t1):
        return xnr[:, j, 1 + t0:1 + t1]

    with ExitStack() as sc:
        sb = lambda nm, sh, dt=F32: st.sb(nm, sh, dt, scope=sc)
        gws = sb("gws16", [128, 64], F16)
        p.op("gpsimd", lambda e: e.dma_start(out=gws[:], in_=d["gws"]),
             reads=[d["gws"]], writes=[gws], dma=True)
        psl = st.ps("moe_psl", [128, 8], F32, scope=sc)
        cw = sb("cw", [128, NT * NE0], F32)
        Escr = sb("moe_E", [128, 8], F32)
        m1s = sb("moe_m1", [128, NT], F32)
        m2s = sb("moe_m2", [128, NT], F32)
        zs = sb("moe_z", [128, NT], F32)
        nmax = sb("moe_nm", [128, 1], F32)
        msk = sb("moe_msk", [128, 8], F32)
        gm = sb("moe_gm", [128, 8], F32)
        den = sb("moe_den", [128, NT], F32)
        cwT = sb("cwT", [8, T_LOC], F16)
        cw16 = sb("cw16", [128, 8], F16)
        pscT = st.ps("moe_pscT", [8, 128], F16, scope=sc)
        for tt in range(NT):
            for cj in range(NJ):
                p.op("tensor", lambda e, tt=tt, cj=cj: e.matmul(
                    psl[:, :], xns(cj, 128 * tt, 128 * (tt + 1)), gws[:, 8 * cj:8 * (cj + 1)],
                    start=(cj == 0), stop=(cj == 7)),
                    reads=[st.xnT, gws], writes=[psl], inc=(cj == 7))
            p.op("vector", lambda e: e.reduce_max(nmax[:, 0:1], psl[:, :], axis=AX, negate=True),
                 reads=[psl], writes=[nmax])
            p.op("scalar", lambda e: e.activation(Escr[:, :], psl[:, :], AF.Exp, bias=nmax[:, 0:1]),
                 reads=[psl, nmax], writes=[Escr])
            p.op("vector", lambda e, tt=tt: e.reduce_sum(zs[:, tt:tt + 1], Escr[:, :], axis=AX),
                 reads=[Escr], writes=[zs])
            p.op("vector", lambda e, tt=tt: e.reduce_max(m1s[:, tt:tt + 1], Escr[:, :], axis=AX),
                 reads=[Escr], writes=[m1s])
            p.op("vector", lambda e, tt=tt: e.tensor_scalar(
                msk[:, :], Escr[:, :], m1s[:, tt:tt + 1], None, ALU.is_ge),
                reads=[Escr, m1s], writes=[msk])
            p.op("vector", lambda e: e.scalar_tensor_tensor(
                gm[:, :], msk[:, :], -1e30, Escr[:, :], ALU.mult, ALU.add),
                reads=[msk, Escr], writes=[gm])
            p.op("vector", lambda e, tt=tt: e.reduce_max(m2s[:, tt:tt + 1], gm[:, :], axis=AX),
                 reads=[gm], writes=[m2s])
            p.op("vector", lambda e, tt=tt: e.tensor_scalar(
                msk[:, :], Escr[:, :], m2s[:, tt:tt + 1], None, ALU.is_ge),
                reads=[Escr, m2s], writes=[msk])
            p.op("vector", lambda e, tt=tt: e.tensor_mul(
                cw[:, NE0 * tt:NE0 * (tt + 1)], Escr[:, :], msk[:, :]),
                reads=[Escr, msk], writes=[cw])
        p.op("vector", lambda e: e.tensor_add(den[:, :], m1s[:, :], m2s[:, :]),
             reads=[m1s, m2s], writes=[den])
        p.op("vector", lambda e: e.scalar_tensor_tensor(den[:, :], zs[:, :], 1e-8, den[:, :], ALU.mult, ALU.add),
             reads=[zs, den], writes=[den])
        p.op("vector", lambda e: e.reciprocal(den[:, :], den[:, :]), reads=[den], writes=[den])
        for tt in range(NT):
            p.op("vector", lambda e, tt=tt: e.tensor_scalar(
                cw[:, NE0 * tt:NE0 * (tt + 1)], cw[:, NE0 * tt:NE0 * (tt + 1)], den[:, tt:tt + 1],
                None, ALU.mult),
                reads=[cw, den], writes=[cw])
            p.op("vector", lambda e, tt=tt: e.tensor_copy(cw16[:, :], cw[:, NE0 * tt:NE0 * (tt + 1)]),
                 reads=[cw], writes=[cw16])
            p.op("tensor", lambda e: e.transpose(pscT[:, :], cw16[:, :], st.ident[:]),
                 reads=[cw16, st.ident], writes=[pscT])
            p.op("vector", lambda e, tt=tt: e.tensor_copy(cwT[:, 128 * tt:128 * (tt + 1)], pscT[0:8, :]),
                 reads=[pscT], writes=[cwT])
        st.dbg_store("cw", cw[:])

        g0 = sb("g0acc", [128, NT * C], F16)
        b2s = sb("b2s", [8, C], F16)
        p.op("gpsimd", lambda e: e.dma_start(out=b2s[:], in_=d["b2"]),
             reads=[d["b2"]], writes=[b2s], dma=True)
        psb = st.ps("moe_psb", [128, 1024], F32, scope=sc)
        xts = [sb(f"moe_x{i}", [128, C], F32) for i in range(2)]
        for tt in range(NT):
            b = tt % 2
            for h in range(2):
                p.op("tensor", lambda e, tt=tt, h=h: e.matmul(
                    psb[:, 512 * h:512 * (h + 1)], cwT[:, 128 * tt:128 * (tt + 1)],
                    b2s[:, 512 * h:512 * (h + 1)], start=True, stop=True),
                    reads=[cwT, b2s], writes=[psb], inc=(h == 1))
            p.op("sync", lambda e, tt=tt, b=b: e.dma_start(
                out=xts[b][:], in_=d["x_main"][128 * tt:128 * (tt + 1), :]),
                reads=[d["x_main"]], writes=[xts[b]], dma=True)
            p.op("vector", lambda e, tt=tt, b=b: e.tensor_scalar(
                xts[b][:], xts[b][:], st.rstd0[:, tt:tt + 1], st.nmr0[:, tt:tt + 1], ALU.mult, ALU.add),
                reads=[xts[b], st.rstd0, st.nmr0], writes=[xts[b]])
            p.op("vector", lambda e, tt=tt, b=b: e.tensor_add(
                g0[:, C * tt:C * (tt + 1)], xts[b][:], psb[:, :]),
                reads=[xts[b], psb], writes=[g0])

        W1s = sb("W1s", [128, NJ * HID], F16)
        W2s = sb("W2s", [128, 16 * C], F16)
        hT = sb("moe_hT", [128, 16 * 512], F16)
        b1s = sb("b1s", [128, 128], F32)
        ysc = sb("moe_ysc", [128, 1024], F16)
        p.op("sync", lambda e: e.dma_start(out=b1s[:], in_=d["b1_fm"]),
             reads=[d["b1_fm"]], writes=[b1s], dma=True)
        psh = [st.ps(f"moe_psh{i}", [128, 512], F32, scope=sc) for i in range(2)]
        psy = st.ps("moe_psy", [128, 1024], F32, scope=sc)
        w1r = W1s.ap().rearrange("p (j h) -> p j h", j=NJ)

        for e_ in range(NE0):
            p.op("gpsimd", lambda e, e_=e_: e.dma_start(
                out=W1s.ap().rearrange("p (j h) -> p j h", j=NJ),
                in_=d["W1"][e_].rearrange("(j p) h -> p j h", p=128)),
                reads=[d["W1"]], writes=[W1s], dma=True)
            p.op("gpsimd", lambda e, e_=e_: e.dma_start(
                out=W2s.ap().rearrange("p (h o) -> p h o", h=16),
                in_=d["W2"][e_].rearrange("(h p) o -> p h o", p=128)),
                reads=[d["W2"]], writes=[W2s], dma=True)
            for tc in range(4):
                for ht in range(16):
                    pb = ht % 2
                    for cj in range(NJ):
                        p.op("tensor", lambda e, tc=tc, ht=ht, cj=cj, pb=pb: e.matmul(
                            psh[pb][:, :], w1r[:, cj, 128 * ht:128 * (ht + 1)],
                            xns(cj, 512 * tc, 512 * (tc + 1)),
                            start=(cj == 0), stop=(cj == 7)),
                            reads=[W1s, st.xnT], writes=[psh[pb]], inc=(cj == 7))
                    p.op("scalar", lambda e, ht=ht, pb=pb, e_=e_: e.activation(
                        hT[:, 512 * ht:512 * (ht + 1)], psh[pb][:, :], AF.Relu,
                        bias=b1s[:, 16 * e_ + ht:16 * e_ + ht + 1]),
                        reads=[psh[pb], b1s], writes=[hT])
                for tt2 in range(4):
                    tt = 4 * tc + tt2
                    for ht in range(16):
                        for hh in range(2):
                            p.op("tensor", lambda e, tt2=tt2, ht=ht, hh=hh: e.matmul(
                                psy[:, 512 * hh:512 * (hh + 1)],
                                hT[:, 512 * ht + 128 * tt2:512 * ht + 128 * (tt2 + 1)],
                                W2s[:, C * ht + 512 * hh:C * ht + 512 * (hh + 1)],
                                start=(ht == 0), stop=(ht == 15)),
                                reads=[hT, W2s], writes=[psy],
                                inc=(ht == 15 and hh == 1))
                    p.op("scalar", lambda e, tt=tt, e_=e_: e.activation(
                        ysc[:, :], psy[:, :], AF.Copy, scale=cw[:, NE0 * tt + e_:NE0 * tt + e_ + 1]),
                        reads=[psy, cw], writes=[ysc])
                    p.op("vector", lambda e, tt=tt: e.tensor_add(
                        g0[:, C * tt:C * (tt + 1)], g0[:, C * tt:C * (tt + 1)], ysc[:, :]),
                        reads=[ysc, g0], writes=[g0])
        st.dbg_store("g0pre", g0[:])

        sg = sb("moe_sg", [128, C], F16)
        s0tl = [sb(f"moe_s0{i}", [128, C], F16) for i in range(2)]
        for tt in range(NT):
            b = tt % 2
            p.op("sync", lambda e, tt=tt, b=b: e.dma_start(
                out=s0tl[b][:], in_=d["s0_d"][128 * tt:128 * (tt + 1), :]),
                reads=[(d["s0_d"], tt)], writes=[s0tl[b]], dma=True)
            p.op("scalar", lambda e, tt=tt: e.activation(
                g0[:, C * tt:C * (tt + 1)], g0[:, C * tt:C * (tt + 1)], AF.Silu),
                reads=[g0], writes=[g0])
            p.op("vector", lambda e, tt=tt, b=b: e.tensor_mul(
                sg[:, :], s0tl[b][:], g0[:, C * tt:C * (tt + 1)]),
                reads=[s0tl[b], g0], writes=[sg])
            p.op("sync", lambda e, tt=tt, b=b: e.dma_start(
                out=xts[b][:], in_=d["x_main"][128 * tt:128 * (tt + 1), :]),
                reads=[d["x_main"]], writes=[xts[b]], dma=True)
            p.op("vector", lambda e, tt=tt, b=b: e.tensor_add(xts[b][:], xts[b][:], sg[:, :]),
                 reads=[sg, xts[b]], writes=[xts[b]])
            p.op("sync", lambda e, tt=tt, b=b: e.dma_start(
                out=d["x1_d"][128 * tt:128 * (tt + 1), :], in_=xts[b][:]),
                reads=[xts[b]], writes=[(d["x1_d"], tt)], dma=True)
        st.dbg_store("x1", d["x1_d"])
    st.barrier()



def emit_l1_qkv(st, scope_l1, scope_q, scope_x):
    p, d = st.p, st.d
    st.rstd1 = st.sb("rstd1", [128, 16], F32, scope=scope_l1)
    st.nmr1 = st.sb("nmr1", [128, 16], F32, scope=scope_l1)
    st.ctxT = st.sb("ctxT", [128, NJ * T_LOC], F16, scope=scope_l1, side="right")
    st.scoresT = st.sb("scoresT", [128, NE1 * T_LOC], F16, scope=scope_l1, side="right")
    st.xn1T = st.sb("xn1T", [128, NJ * T_LOC], F16, scope=scope_x)
    st.ka = [st.sb(f"ka{e_}", [128, NJ * 128], F16, scope=scope_x) for e_ in range(NE1)]
    st.es16 = st.sb("es16", [16, C], F16, scope=scope_x)
    st.sel1h = st.sb("p6_1h", [16, NE1 * 128], F16, scope=scope_x)
    xnr1 = st.xn1T.ap().rearrange("p (j t) -> p j t", j=NJ)
    with ExitStack() as sc:
        sb = lambda nm, sh, dt=F32: st.sb(nm, sh, dt, scope=sc)

        def src(i):
            return d["x1_d"][128 * i:128 * (i + 1), :], 128

        with ExitStack() as sc2:
            sb2 = lambda nm, sh, dt=F32: st.sb(nm, sh, dt, scope=sc2)
            with ExitStack() as sc_stats:
                _ln_stats(st, sc_stats, "ln1", src, 16, st.rstd1, st.nmr1)
                st.barrier()
            xt = [sb2(f"p4_x{i}", [128, C], F32) for i in range(4)]
            xn16 = [sb2(f"p4_n{i}", [128, C], F16) for i in range(2)]
            pst = [st.ps(f"p4_t{i}", [128, 8 * 128], F16, scope=sc2) for i in range(2)]
            for i in range(NT):
                b = i % 4
                p.op("sync", lambda e, i=i, b=b: e.dma_start(
                    out=xt[b][:], in_=d["x1_d"][128 * i:128 * (i + 1), :]),
                    reads=[(d["x1_d"], i)], writes=[xt[b]], dma=True)
                b2 = i % 2
                p.op("vector", lambda e, b=b, b2=b2, i=i: e.tensor_scalar(
                    xn16[b2][:], xt[b][:], st.rstd1[:, i:i + 1], st.nmr1[:, i:i + 1], ALU.mult, ALU.add),
                    reads=[xt[b], st.rstd1, st.nmr1], writes=[xn16[b2]])
                for j in range(NJ):
                    p.op("tensor", lambda e, b2=b2, j=j: e.transpose(
                        pst[b2][:, 128 * j:128 * (j + 1)], xn16[b2][:, 128 * j:128 * (j + 1)], st.ident[:]),
                        reads=[xn16[b2], st.ident], writes=[pst[b2]], inc=(j == NJ - 1))
                p.op("vector", lambda e, b2=b2, i=i: e.tensor_copy(
                    xnr1[:, :, 128 * i:128 * (i + 1)], pst[b2].ap().rearrange("p (j t) -> p j t", j=NJ)),
                    reads=[pst[b2]], writes=[st.xn1T])
            st.dbg_store("xn1T", st.xn1T[:])
            st.barrier()

        kvtile = [sb(f"p4_kv{i}", [128, 512], F16) for i in range(2)]
        bqs = sb("bqs", [128, 24], F32)
        for i, nm in enumerate(["bq_fm", "bk_fm", "bv_fm"]):
            p.op("sync", lambda e, nm=nm, i=i: e.dma_start(out=bqs[:, 8 * i:8 * (i + 1)], in_=d[nm]),
                 reads=[d[nm]], writes=[bqs], dma=True)
        wsb = [sb(f"p4_w{i}", [128, NJ * 128], F16) for i in range(2)]
        wall = sb("p4_wall", [128, 16 * NJ * 128], F16)
        psq = [st.ps(f"p4_q{i}", [128, 512], F32, scope=sc) for i in range(2)]

        def prefetch(j):
            for nm, bi in (("wvR", j), ("wqR", 8 + j)):
                p.op("gpsimd", lambda e, nm=nm, bi=bi, j=j: e.dma_start(
                    out=wall[:, 1024 * bi:1024 * (bi + 1)], in_=d[nm][j]),
                    reads=[d[nm]], writes=[(wall, bi)], dma=True)
            for e_ in (2 * j, 2 * j + 1):
                p.op("gpsimd", lambda e, e_=e_: e.dma_start(
                    out=st.ka[e_][:], in_=d["keysR"][e_]),
                    reads=[d["keysR"]], writes=[st.ka[e_]], dma=True,
                    q=p.dma_sem("ka_shared"))
            if j == 0:
                p.op("gpsimd", lambda e: e.dma_start(out=st.es16[:], in_=d["esel"]),
                     reads=[d["esel"]], writes=[st.es16], dma=True)
                p.op("gpsimd", lambda e: e.dma_start(out=st.sel1h[:], in_=d["sel1h"]),
                     reads=[d["sel1h"]], writes=[st.sel1h], dma=True)

        def proj(wname, dstT, dst_dram, bcol, j, slot):
            if wname == "wk":
                p.op("gpsimd", lambda e, wname=wname, j=j, slot=slot: e.dma_start(
                    out=wsb[slot][:], in_=d[wname + "R"][j]),
                    reads=[d[wname + "R"]], writes=[wsb[slot]], dma=True)
                wsrc, wread = wsb[slot], wsb[slot]
            else:
                bi = j if wname == "wv" else 8 + j
                wsrc, wread = wall[:, 1024 * bi:1024 * (bi + 1)], (wall, bi)
            dstr = dstT.ap().rearrange("p (j t) -> p j t", j=NJ) if dstT is not None else None
            for tc in range(4):
                pb = tc % 2
                for cj in range(NJ):
                    p.op("tensor", lambda e, tc=tc, cj=cj, pb=pb, wsrc=wsrc, wread=wread: e.matmul(
                        psq[pb][:, :], wsrc[:, 128 * cj:128 * (cj + 1)],
                        xnr1[:, cj, 512 * tc:512 * (tc + 1)], start=(cj == 0), stop=(cj == 7)),
                        reads=[wread, st.xn1T], writes=[psq[pb]], inc=(cj == 7))
                if dstT is not None:
                    p.op("vector", lambda e, j=j, tc=tc, pb=pb, bcol=bcol, dstr=dstr, dstT=dstT: e.tensor_scalar(
                        dstr[:, j, 512 * tc:512 * (tc + 1)], psq[pb][:, :],
                        bqs[:, bcol * 8 + j:bcol * 8 + j + 1], None, ALU.add),
                        reads=[psq[pb], bqs], writes=[dstT])
                else:
                    kb = tc % 2
                    p.op("vector", lambda e, j=j, tc=tc, pb=pb, kb=kb, bcol=bcol: e.tensor_scalar(
                        kvtile[kb][:, :], psq[pb][:, :],
                        bqs[:, bcol * 8 + j:bcol * 8 + j + 1], None, ALU.add),
                        reads=[psq[pb], bqs], writes=[kvtile[kb]])
                    p.op("sync", lambda e, j=j, tc=tc, kb=kb, dst_dram=dst_dram: e.dma_start(
                        out=dst_dram[128 * j:128 * (j + 1), 512 * tc:512 * (tc + 1)],
                        in_=kvtile[kb][:, :]),
                        reads=[kvtile[kb]], writes=[dst_dram], dma=True)

        for j in range(NJ):
            proj("wk", None, d["ccK_in"], 1, j, j % 2)
            prefetch(j)
        p.op("gpsimd", lambda e: e.collective_compute(
            "AllGather", ALU.bypass, replica_groups=GROUPS,
            ins=[d["ccK_in"].opt()], outs=[d["ccK_out"].opt()]),
            reads=[d["ccK_in"]], writes=[d["ccK_out"]], q=st.cc_sems[1], inc_n=1)
        for j in range(NJ):
            proj("wv", None, d["ccV_in"], 2, j, j % 2)
        p.op("gpsimd", lambda e: e.collective_compute(
            "AllGather", ALU.bypass, replica_groups=GROUPS,
            ins=[d["ccV_in"].opt()], outs=[d["ccV_out"].opt()]),
            reads=[d["ccV_in"]], writes=[d["ccV_out"]], q=st.cc_sems[2], inc_n=1)
        for j in range(NJ):
            proj("wq", None, d["q_d"], 0, j, j % 2)
    st.barrier()



def emit_l1_attn(st, scope_q):
    p, d = st.p, st.d
    ctr = st.ctxT.ap().rearrange("p (j t) -> p j t", j=NJ)
    with ExitStack() as sc:
        sb = lambda nm, sh, dt=F32: st.sb(nm, sh, dt, scope=sc)
        KVH = [sb(f"a_kvh{i}", [128, S_FULL], F16) for i in range(4)]
        VH = [sb(f"a_vh{i}", [128, 64 * 65], F16) for i in range(4)]
        QR = [sb(f"a_qr{i}", [128, T_LOC], F16) for i in range(2)]
        PT = [sb(f"a_pt{i}", [128, 1024], F16) for i in range(2)]
        zrow = sb("a_z", [1, 512], F16)
        ups = sb("a_u", [64, 512], F16)
        ones1 = sb("a_ones", [1, 64], F16)
        psS = [st.ps(f"a_psS{i}", [128, 1024], F32, scope=sc) for i in range(2)]
        psC = [st.ps(f"a_psC{i}", [65, 512], F32, scope=sc) for i in range(2)]
        psT = [st.ps(f"a_psT{i}", [128, 512], F16, scope=sc) for i in range(2)]
        p.op("vector", lambda e: e.memset(ones1[:], 1.0), writes=[ones1])

        def qr_load(m):
            s = m % 2
            p.op("sync", lambda e, m=m, s=s: e.dma_start(
                out=QR[s][:], in_=d["q_d"][128 * m:128 * (m + 1), :]),
                reads=[d["q_d"]], writes=[QR[s]], dma=True)

        def kv_load(h):
            ss = h % 4
            prow = 64 * (h % 2)
            vrow = 64 - prow
            KH_h = KVH[ss][prow:prow + 64, :]
            VHT_h = KVH[ss][vrow:vrow + 64, :]
            for r in range(4):
                p.op("sync", lambda e, h=h, r=r, KH_h=KH_h, ss=ss: e.dma_start(
                    out=KH_h[:, T_LOC * r:T_LOC * (r + 1)],
                    in_=d["ccK_out"][C * r + HD * h:C * r + HD * (h + 1), :]),
                    reads=[d["ccK_out"]], writes=[KVH[ss]], dma=True)
                p.op("sync", lambda e, h=h, r=r, VHT_h=VHT_h, ss=ss: e.dma_start(
                    out=VHT_h[:, T_LOC * r:T_LOC * (r + 1)],
                    in_=d["ccV_out"][C * r + HD * h:C * r + HD * (h + 1), :]),
                    reads=[d["ccV_out"]], writes=[KVH[ss]], dma=True)

        def vh_build(h):
            ss = h % 4
            vrow = 64 - 64 * (h % 2)
            VHT_h = KVH[ss][vrow:vrow + 64, :]
            p.op("vector", lambda e, ss=ss: e.memset(VH[ss][:], 1.0),
                 reads=[(VH[ss], "t")], writes=[VH[ss]])
            for g2 in range(16):
                tb = psT[g2 % 2]
                for kb in range(4):
                    p.op("tensor", lambda e, ss=ss, g2=g2, kb=kb, VHT_h=VHT_h, vrow=vrow, tb=tb: e.transpose(
                        tb[:, 128 * kb:128 * kb + 64],
                        VHT_h[:, 128 * (4 * g2 + kb):128 * (4 * g2 + kb + 1)],
                        st.idblk[vrow:vrow + 64, :]),
                        reads=[KVH[ss], st.idblk], writes=[tb], inc=(kb == 3))
                p.op("vector", lambda e, ss=ss, g2=g2, tb=tb: e.tensor_copy(
                    VH[ss].ap().rearrange("p (kb x) -> p kb x", x=65)[:, 4 * g2:4 * (g2 + 1), 0:64],
                    tb.ap().rearrange("p (kb x) -> p kb x", x=128)[:, :, 0:64]),
                    reads=[tb, VH[ss]], writes=[(VH[ss], "t")])

        NG = 64

        def emit_qk(m, qt, gi):
            s = m % 2
            ss0, ss1 = (2 * m) % 4, (2 * m + 1) % 4
            pss = psS[gi % 2]
            p.op("tensor", lambda e, s=s, ss0=ss0, qt=qt, gi=gi, pss=pss: e.matmul(
                pss[:, 0:512],
                KVH[ss0][0:64, 128 * gi:128 * (gi + 1)],
                QR[s][0:64, 512 * qt:512 * (qt + 1)],
                start=True, stop=True, tile_position=(0, 0)),
                reads=[KVH[ss0], QR[s]], writes=[pss], inc=False)
            p.op("tensor", lambda e, s=s, ss1=ss1, qt=qt, gi=gi, pss=pss: e.matmul(
                pss[:, 512:1024],
                KVH[ss1][64:128, 128 * gi:128 * (gi + 1)],
                QR[s][64:128, 512 * qt:512 * (qt + 1)],
                start=True, stop=True, tile_position=(64, 0)),
                reads=[KVH[ss1], QR[s]], writes=[pss], inc=True)

        def emit_exp(gi):
            pb = gi % 2
            pss = psS[pb]
            p.op("scalar", lambda e, pb=pb, pss=pss: e.activation(
                PT[pb][:, :], pss[:, :], AF.Exp, scale=0.125),
                reads=[pss], writes=[PT[pb]])

        def emit_av(m, gi):
            pb = gi % 2
            last = (gi == NG - 1)
            for hh in range(2):
                ss = (2 * m + hh) % 4
                p.op("tensor", lambda e, ss=ss, pb=pb, gi=gi, hh=hh, last=last: e.matmul(
                    psC[hh][:, :], VH[ss][:, 65 * gi:65 * (gi + 1)],
                    PT[pb][:, 512 * hh:512 * (hh + 1)],
                    start=(gi == 0), stop=last),
                    reads=[VH[ss], (VH[ss], "t"), PT[pb]], writes=[psC[hh]],
                    inc=(last and hh == 1))

        qr_load(0)
        kv_load(0)
        vh_build(0)
        kv_load(1)
        vh_build(1)
        for m in range(NH // 2):
            h2, h3 = 2 * m + 2, 2 * m + 3
            if h2 < NH:
                kv_load(h2)
                kv_load(h3)
            if m + 1 < NH // 2:
                qr_load(m + 1)
            for qt in range(4):
                emit_qk(m, qt, 0)
                for gi in range(NG):
                    emit_exp(gi)
                    if gi >= 1:
                        emit_av(m, gi - 1)
                    if gi + 1 < NG:
                        emit_qk(m, qt, gi + 1)
                emit_av(m, NG - 1)
                for hh in range(2):
                    prow = 64 * hh
                    p.op("vector", lambda e, hh=hh: e.tensor_scalar(
                        ups[:, :], psC[hh][0:64, :], 1.0 / 4096.0, None, ALU.mult),
                        reads=[psC[hh]], writes=[ups])
                    p.op("vector", lambda e, hh=hh: e.reciprocal(psS[0][0:1, 0:512], psC[hh][64:65, :]),
                         reads=[psC[hh]], writes=[psS[0]])
                    p.op("vector", lambda e: e.tensor_scalar(
                        zrow[0:1, :], psS[0][0:1, 0:512], 4096.0, None, ALU.mult),
                        reads=[psS[0]], writes=[zrow])
                    p.op("tensor", lambda e, hh=hh: e.matmul(
                        psC[hh][0:64, :], ones1[0:1, :], zrow[0:1, :],
                        start=True, stop=True),
                        reads=[ones1, zrow], writes=[psC[hh]])
                    p.op("vector", lambda e, m=m, qt=qt, hh=hh, prow=prow: e.tensor_mul(
                        ctr[prow:prow + 64, m, 512 * qt:512 * (qt + 1)],
                        ups[:, :], psC[hh][0:64, :]),
                        reads=[ups, psC[hh]], writes=[st.ctxT])
                if qt == 1 and h2 < NH:
                    vh_build(h2)
                if qt == 2 and h2 < NH:
                    vh_build(h3)
        st.dbg_store("ctxT", st.ctxT[:])
    st.barrier()


def emit_l1_ffn_scores(st, scope):
    p, d = st.p, st.d
    xnr1 = st.xn1T.ap().rearrange("p (j t) -> p j t", j=NJ)
    scr = st.scoresT.ap().rearrange("p (E t) -> p E t", E=NE1)
    if True:
        sc = scope
        sb = lambda nm, sh, dt=F32: st.sb(nm, sh, dt, scope=sc)
        es16 = st.es16
        eselT = sb("eselT", [128, NJ * 16], F16)
        psE = st.ps("p6_psE", [128, 16], F16, scope=sc)
        for j in range(NJ):
            p.op("tensor", lambda e, j=j: e.transpose(
                psE[:, :], es16[:, 128 * j:128 * (j + 1)], st.ident[0:16, 0:16]),
                reads=[es16, st.ident], writes=[psE])
            p.op("vector", lambda e, j=j: e.tensor_copy(eselT[:, 16 * j:16 * (j + 1)], psE[:, :]),
                 reads=[psE], writes=[eselT])
        psSel = st.ps("p6_psSel", [16, 512], F32, scope=sc)
        selT = sb("selT", [16, T_LOC], F16)
        for tc in range(4):
            for cj in range(NJ):
                p.op("tensor", lambda e, tc=tc, cj=cj: e.matmul(
                    psSel[:, :], eselT[:, 16 * cj:16 * (cj + 1)],
                    xnr1[:, cj, 512 * tc:512 * (tc + 1)], start=(cj == 0), stop=(cj == 7)),
                    reads=[eselT, st.xn1T], writes=[psSel], inc=(cj == 7))
            p.op("scalar", lambda e, tc=tc: e.activation(
                selT[:, 512 * tc:512 * (tc + 1)], psSel[:, :], AF.Sigmoid),
                reads=[psSel], writes=[selT])

        sel1h = st.sel1h
        relscr = sb("p6_rel", [128, 512], F16)
        psSc = [st.ps(f"p6_psSc{i}", [128, 512], F32, scope=sc) for i in range(2)]
        psBc = st.ps("p6_psBc", [128, 512], F32, scope=sc)
        for e_ in range(NE1):
            for tc in range(4):
                pb = tc % 2
                for cj in range(NJ):
                    p.op("tensor", lambda e, tc=tc, cj=cj, e_=e_, pb=pb: e.matmul(
                        psSc[pb][:, :], st.ka[e_][:, 128 * cj:128 * (cj + 1)],
                        xnr1[:, cj, 512 * tc:512 * (tc + 1)], start=(cj == 0), stop=(cj == 7)),
                        reads=[st.ka[e_], st.xn1T], writes=[psSc[pb]], inc=(cj == 7))
                p.op("scalar", lambda e, tc=tc, pb=pb: e.activation(
                    relscr[:, :], psSc[pb][:, :], AF.Relu),
                    reads=[psSc[pb]], writes=[relscr])
                p.op("tensor", lambda e, e_=e_, tc=tc: e.matmul(
                    psBc[:, :], sel1h[:, 128 * e_:128 * (e_ + 1)],
                    selT[0:16, 512 * tc:512 * (tc + 1)],
                    start=True, stop=True),
                    reads=[sel1h, selT], writes=[psBc])
                p.op("vector", lambda e, e_=e_, tc=tc: e.tensor_mul(
                    scr[:, e_, 512 * tc:512 * (tc + 1)], relscr[:, :], psBc[:, :]),
                    reads=[relscr, psBc], writes=[st.scoresT])
        st.dbg_store("scoresT", st.scoresT[:])
        st.dbg_store("selT", selT[:])


def emit_tail(st, scope_l1):
    p, d = st.p, st.d
    ctr = st.ctxT.ap().rearrange("p (j t) -> p j t", j=NJ)
    scr = st.scoresT.ap().rearrange("p (E t) -> p E t", E=NE1)
    with ExitStack() as sc:
        sb = lambda nm, sh, dt=F32: st.sb(nm, sh, dt, scope=sc)
        wos = sb("wos", [128, NJ * C], F16)
        p.op("gpsimd", lambda e: e.dma_start(
            out=wos.ap().rearrange("p (j o) -> p j o", j=NJ),
            in_=d["wo1"].rearrange("(j p) o -> p j o", p=128)),
            reads=[d["wo1"]], writes=[wos], dma=True)
        valsb = sb("t_vals", [128, NE1 * C], F16)
        for e_ in range(NE1):
            p.op("gpsimd", lambda e, e_=e_: e.dma_start(
                out=valsb[:, C * e_:C * (e_ + 1)], in_=d["values"][e_]),
                reads=[d["values"]], writes=[valsb], dma=True)
        Bo = sb("Bo", [128, C], F32)
        Gt = sb("Gt", [128, C], F32)
        Bt = sb("Bt", [128, C], F32)
        for nm, t in [("Bo_t", Bo), ("Gam_t", Gt), ("Bet_t", Bt)]:
            p.op("sync", lambda e, nm=nm, t=t: e.dma_start(out=t[:], in_=d[nm]),
                 reads=[d[nm]], writes=[t], dma=True)
        pso = [st.ps(f"t_pso{i}", [128, 1024], F32, scope=sc) for i in range(2)]
        psYs = [st.ps(f"t_psY{i}", [128, 1024], F32, scope=sc) for i in range(2)]
        xt = [sb(f"t_x{i}", [128, C], F32) for i in range(4)]
        lnin = sb("t_ln", [128, C], F32)
        sums = sb("t_s", [128, NT], F32)
        sqs = sb("t_q", [128, NT], F32)
        sqscr = sb("t_qs", [128, C], F32)
        g1 = [sb(f"t_g1{i}", [128, C], F16) for i in range(2)]
        s1 = sb("t_s1", [128, C], F16)
        sg = sb("t_sg", [128, C], F16)
        wor = wos.ap().rearrange("p (j o) -> p j o", j=NJ)
        for tt in range(NT):
            pb = tt % 2
            xb = tt % 4
            for hh in range(2):
                for cj in range(NJ):
                    p.op("tensor", lambda e, tt=tt, hh=hh, cj=cj, pb=pb: e.matmul(
                        pso[pb][:, 512 * hh:512 * (hh + 1)], ctr[:, cj, 128 * tt:128 * (tt + 1)],
                        wor[:, cj, 512 * hh:512 * (hh + 1)], start=(cj == 0), stop=(cj == 7)),
                        reads=[st.ctxT, wos], writes=[pso[pb]], inc=(cj == 7 and hh == 1))
            psY = psYs[tt % 2]
            for e_ in range(NE1):
                for hh in range(2):
                    p.op("tensor", lambda e, tt=tt, e_=e_, hh=hh, psY=psY: e.matmul(
                        psY[:, 512 * hh:512 * (hh + 1)],
                        scr[:, e_, 128 * tt:128 * (tt + 1)],
                        valsb[:, C * e_ + 512 * hh:C * e_ + 512 * (hh + 1)],
                        start=(e_ == 0), stop=(e_ == NE1 - 1)),
                        reads=[st.scoresT, valsb], writes=[psY],
                        inc=(e_ == NE1 - 1 and hh == 1))
            p.op("sync", lambda e, tt=tt, xb=xb: e.dma_start(
                out=xt[xb][:], in_=d["x1_d"][128 * tt:128 * (tt + 1), :]),
                reads=[(d["x1_d"], tt)], writes=[xt[xb]], dma=True)
            p.op("vector", lambda e, tt=tt, xb=xb: e.tensor_scalar(
                lnin[:], xt[xb][:], st.rstd1[:, tt:tt + 1], st.nmr1[:, tt:tt + 1], ALU.mult, ALU.add),
                reads=[xt[xb], st.rstd1, st.nmr1], writes=[lnin])
            p.op("vector", lambda e, tt=tt, pb=pb: e.tensor_add(lnin[:], lnin[:], pso[pb][:, :]),
                 reads=[lnin, pso[pb]], writes=[lnin])
            p.op("vector", lambda e: e.tensor_add(lnin[:], lnin[:], Bo[:]),
                 reads=[lnin, Bo], writes=[lnin])
            p.op("vector", lambda e, tt=tt: e.reduce_sum(sums[:, tt:tt + 1], lnin[:], axis=AX),
                 reads=[lnin], writes=[sums])
            p.op("scalar", lambda e, tt=tt: e.activation(sqscr[:], lnin[:], AF.Square,
                                                         accum_out=sqs[:, tt:tt + 1]),
                 reads=[lnin], writes=[sqscr, sqs])
            p.op("vector", lambda e, tt=tt: e.tensor_scalar(
                sums[:, tt:tt + 1], sums[:, tt:tt + 1], 1.0 / C, None, ALU.mult),
                reads=[sums], writes=[sums])
            p.op("vector", lambda e, tt=tt: e.scalar_tensor_tensor(
                sqscr[:, 0:1], sums[:, tt:tt + 1], -1.0, sums[:, tt:tt + 1], ALU.mult, ALU.mult),
                reads=[sums], writes=[sqscr])
            p.op("vector", lambda e, tt=tt: e.scalar_tensor_tensor(
                sqscr[:, 0:1], sqs[:, tt:tt + 1], 1.0 / C, sqscr[:, 0:1], ALU.mult, ALU.add),
                reads=[sqs, sqscr], writes=[sqscr])
            p.op("scalar", lambda e: e.activation(sqscr[:, 0:1], sqscr[:, 0:1], AF.Sqrt, bias=LN_EPS),
                 reads=[sqscr], writes=[sqscr])
            p.op("vector", lambda e: e.reciprocal(sqscr[:, 1:2], sqscr[:, 0:1]),
                 reads=[sqscr], writes=[sqscr])
            p.op("vector", lambda e, tt=tt: e.scalar_tensor_tensor(
                sqscr[:, 2:3], sums[:, tt:tt + 1], -1.0, sqscr[:, 1:2], ALU.mult, ALU.mult),
                reads=[sums, sqscr], writes=[sqscr])
            p.op("vector", lambda e: e.tensor_scalar(
                lnin[:], lnin[:], sqscr[:, 1:2], sqscr[:, 2:3], ALU.mult, ALU.add),
                reads=[lnin, sqscr], writes=[lnin])
            p.op("vector", lambda e: e.tensor_mul(lnin[:], lnin[:], Gt[:]),
                 reads=[lnin, Gt], writes=[lnin])
            p.op("vector", lambda e: e.tensor_add(s1[:], lnin[:], Bt[:]),
                 reads=[lnin, Bt], writes=[s1])
            gb = tt % 2
            p.op("scalar", lambda e, gb=gb, psY=psY: e.activation(g1[gb][:, :], psY[:, :], AF.Silu),
                 reads=[psY], writes=[g1[gb]])
            p.op("vector", lambda e, gb=gb: e.tensor_mul(sg[:, :], s1[:], g1[gb][:, :]),
                 reads=[s1, g1[gb]], writes=[sg])
            p.op("vector", lambda e, xb=xb: e.tensor_add(xt[xb][:], xt[xb][:], sg[:, :]),
                 reads=[sg, xt[xb]], writes=[xt[xb]])
            p.op("sync", lambda e, tt=tt, xb=xb: e.dma_start(
                out=d["out"][128 * tt:128 * (tt + 1), :], in_=xt[xb][:]),
                reads=[xt[xb]], writes=[(d["out"], tt)], dma=True)
        st.dbg_store("s1", s1[:])
    st.p.wait_all("sync")



ALL_PHASES = ("p0", "p1", "p2", "p4", "p5", "p5b", "p6")


def build_program(phases=ALL_PHASES, dbg=()):
    nc = bass.Bass("TRN2", target_bir_lowering=False)
    d = {}

    def din(name, shape, dt=F32):
        d[name] = nc.dram_tensor(name, shape, dt, kind="ExternalInput").ap()

    def dint(name, shape, dt=F32):
        d[name] = nc.dram_tensor(name, shape, dt, kind="Internal").ap()

    din("x_main", [T_LOC, C]); din("x_halo", [2, C])
    din("idn", [128, 128]); din("iota_t", [128, T_LOC]); din("iotar_t", [128, T_LOC])
    din("cst", [128, 6]); din("idn_blk", [128, 64])
    if "p1" in phases:
        for nm in ("Wk", "Wv", "Wr", "Wo"):
            din(nm, [C, C])
        for nm in ("mixk_fm", "mixv_fm", "mixr_fm", "sd_fm", "sf_fm"):
            din(nm, [128, 8])
        din("sd_row", [1, C]); din("qmask", [1, 4]); din("qcoef", [1, 4])
        dint("ccS_in", [128, 16]); dint("ccS_out", [512, 16]); dint("s0_d", [T_LOC, C], F16)
    if "p2" in phases:
        din("gws", [128, 64]); din("W1", [NE0, C, HID]); din("b1_fm", [128, 128])
        din("W2", [NE0, HID, C]); din("b2", [NE0, C])
        dint("x1_d", [T_LOC, C])
    if "p4" in phases:
        din("wqR", [NJ, 128, NJ * 128]); din("wkR", [NJ, 128, NJ * 128]); din("wvR", [NJ, 128, NJ * 128])
        din("bq_fm", [128, 8]); din("bk_fm", [128, 8]); din("bv_fm", [128, 8])
        dint("ccK_in", [C, T_LOC], F16); dint("ccK_out", [4 * C, T_LOC], F16)
        dint("ccV_in", [C, T_LOC], F16); dint("ccV_out", [4 * C, T_LOC], F16)
        dint("q_d", [C, T_LOC], F16)
    if "p5b" in phases:
        din("wo1", [C, C]); din("Bo_t", [128, C]); din("Gam_t", [128, C]); din("Bet_t", [128, C])
    if "p6" in phases:
        din("esel", [NE1, C]); din("keysR", [NE1, 128, NJ * 128]); din("values", [NE1, 128, C])
        din("sel1h", [NE1, NE1 * 128])
        dint("g1_d", [T_LOC, C], F16)
    d["out"] = nc.dram_tensor("out", [T_LOC, C], F32, kind="ExternalOutput").ap()

    dbg_aps = {}
    for name in dbg:
        shape, dt = DBG_SHAPES[name]
        dbg_aps[name] = nc.dram_tensor("dbg_" + name, shape, dt, kind="ExternalOutput").ap()

    with ExitStack() as ctx:
        p = Prog(nc, ctx)
        st = _St(nc, ctx, p, d, dbg_aps)
        p.blk = ctx.enter_context(nc.Block())
        emit_setup(st)
        with ExitStack() as scope_l0:
            emit_l0_alloc(st, scope_l0)
            if "p0" in phases:
                emit_ln0(st)
            if "p1" in phases:
                emit_l0_attn(st)
            if "p2" in phases:
                emit_moe(st)
        with ExitStack() as scope_l1:
            with ExitStack() as scope_q:
                with ExitStack() as scope_x:
                    if "p4" in phases:
                        emit_l1_qkv(st, scope_l1, scope_q, scope_x)
                    if "p6" in phases:
                        emit_l1_ffn_scores(st, scope_x)
                    st.barrier()
                if "p5" in phases:
                    emit_l1_attn(st, scope_q)
            if "p5b" in phases:
                emit_tail(st, scope_l1)
                p.flush()
        if "p6" not in phases:
            zo = st.sb("zo", [128, C], F32)
            p.op("vector", lambda e: e.memset(zo[:], 0.0), writes=[zo])
            for i in range(NT):
                p.op("sync", lambda e, i=i: e.dma_start(
                    out=d["out"][128 * i:128 * (i + 1), :], in_=zo[:]),
                    reads=[zo], writes=[d["out"]], dma=True)
            p.wait_all("sync")
        p.flush()
    return nc



def _fm(vec):
    return np.ascontiguousarray(np.asarray(vec, np.float32).reshape(8, 128).T)


def make_in_maps(inputs, phases=ALL_PHASES):
    ii = {k: np.asarray(v) for k, v in inputs.items()}
    x = ii["L0_attn_x"]
    maps = []
    iota = np.tile(np.arange(T_LOC, dtype=np.float32), (128, 1))
    iotar = (T_LOC - 1.0) - iota
    idn = np.eye(128, dtype=np.float32)
    for g in range(8):
        b, q = g // 4, g % 4
        t0 = T_LOC * q
        m = {}
        m["x_main"] = x[b, t0:t0 + T_LOC]
        halo = np.zeros((2, C), np.float32)
        if q > 0:
            halo[0] = x[b, t0 - 1]
        if q < 3:
            halo[1] = x[b, t0 + T_LOC]
        m["x_halo"] = halo
        m["idn"] = idn
        m["iota_t"] = iota
        m["iotar_t"] = iotar
        m["cst"] = np.tile(np.array([[1e-5, -1.0, 1.0 / S_FULL, 0.0, 1.0, 0.125]], np.float32), (128, 1))
        m["idn_blk"] = np.vstack([np.eye(64, dtype=np.float32)] * 2)
        if "p1" in phases:
            m["Wk"] = ii["L0_attn_Wk"]; m["Wv"] = ii["L0_attn_Wv"]
            m["Wr"] = ii["L0_attn_Wr"]; m["Wo"] = ii["L0_attn_Wo"]
            m["mixk_fm"] = _fm(ii["L0_attn_mix_k"]); m["mixv_fm"] = _fm(ii["L0_attn_mix_v"])
            m["mixr_fm"] = _fm(ii["L0_attn_mix_r"])
            m["sd_fm"] = _fm(ii["L0_attn_spatial_decay"])
            m["sf_fm"] = _fm(ii["L0_attn_spatial_first"])
            m["sd_row"] = ii["L0_attn_spatial_decay"].reshape(1, C)
            qmask = np.zeros((1, 4), np.float32); qmask[0, :q] = 1.0
            qcoef = np.zeros((1, 4), np.float32)
            for r in range(q):
                qcoef[0, r] = float(T_LOC * (q - 1 - r))
            m["qmask"] = qmask; m["qcoef"] = qcoef
        if "p2" in phases:
            gw = ii["L0_ffn_gate_W"]
            m["gws"] = gw.reshape(8, 128, 8).transpose(1, 0, 2).reshape(128, 64)
            m["W1"] = ii["L0_ffn_W1"]; m["W2"] = ii["L0_ffn_W2"]
            m["b1_fm"] = ii["L0_ffn_b1"].reshape(8, 16, 128).transpose(2, 0, 1).reshape(128, 128)
            m["b2"] = ii["L0_ffn_b2"]
        if "p4" in phases:
            def _wR(w):
                return np.ascontiguousarray(
                    np.asarray(w, np.float32).reshape(8, 128, 8, 128).transpose(2, 1, 0, 3).reshape(8, 128, 1024))
            m["wqR"] = _wR(ii["L1_attn_wq"]); m["wkR"] = _wR(ii["L1_attn_wk"]); m["wvR"] = _wR(ii["L1_attn_wv"])
            m["bq_fm"] = _fm(ii["L1_attn_bq"]); m["bk_fm"] = _fm(ii["L1_attn_bk"])
            m["bv_fm"] = _fm(ii["L1_attn_bv"])
        if "p5b" in phases:
            m["wo1"] = ii["L1_attn_wo"]
            m["Bo_t"] = np.tile(np.asarray(ii["L1_attn_bo"], np.float32).reshape(1, C), (128, 1))
            m["Gam_t"] = np.tile(np.asarray(ii["L1_attn_gamma"], np.float32).reshape(1, C), (128, 1))
            m["Bet_t"] = np.tile(np.asarray(ii["L1_attn_beta"], np.float32).reshape(1, C), (128, 1))
        if "p6" in phases:
            m["esel"] = ii["L1_ffn_expert_sel"]
            m["keysR"] = ii["L1_ffn_keys"].reshape(NE1, 8, 128, 128).transpose(0, 2, 1, 3).reshape(NE1, 128, 1024)
            m["values"] = ii["L1_ffn_values"]
            m["sel1h"] = np.repeat(np.eye(NE1, dtype=np.float32), 128, axis=1)
        maps.append({k: np.ascontiguousarray(np.asarray(v, np.float32)) for k, v in m.items()})
    return maps


def kernel(**inputs):
    nc = build_program(ALL_PHASES)
    in_maps = make_in_maps(inputs, ALL_PHASES)
    res = run_bass_kernel_spmd(nc, in_maps, core_ids=list(range(8)))
    out = np.zeros((2, S_FULL, C), np.float32)
    for g in range(8):
        b, q = g // 4, g % 4
        out[b, T_LOC * q:T_LOC * (q + 1)] = res.results[g]["out"]
    return out


# version 14
# speedup vs baseline: 1.1222x; 1.1222x over previous
import numpy as np
from contextlib import ExitStack

import concourse.bass as bass
import concourse.mybir as mybir
from concourse.bass_utils import run_bass_kernel_spmd

F32 = mybir.dt.float32
F16 = mybir.dt.float16
AF = mybir.ActivationFunctionType
ALU = mybir.AluOpType
AX = mybir.AxisListType.X

T_LOC = 2048
NT = 16
C = 1024
NJ = 8
HID = 2048
NE0 = 8
NE1 = 16
NH = 16
HD = 64
S_FULL = 8192
LN_EPS = 1e-5
XN_W = T_LOC + 2

ENGINES = ("sync", "gpsimd", "tensor", "vector", "scalar")
GROUPS = [[0, 1, 2, 3], [4, 5, 6, 7]]


def _tname(x):
    k = None
    if isinstance(x, tuple):
        x, k = x
    t = getattr(x, "tensor", x)
    return (getattr(t, "name", str(t)), k)


class Sem:
    __slots__ = ("h", "n", "name")

    def __init__(self, h, name):
        self.h, self.n, self.name = h, 0, name


class Prog:
    def __init__(self, nc, ctx):
        self.nc = nc
        self.ctx = ctx
        self.script = {e: [] for e in ENGINES}
        self.esem = {e: self.sem("sE_" + e) for e in ENGINES}
        self.dsem = {}
        self.wr = {}
        self.rd = {}
        self.pending = {e: [] for e in ENGINES}
        self.waited = {e: {} for e in ENGINES}
        self.all_sems = list(self.esem.values())

    def sem(self, name):
        h = self.ctx.enter_context(self.nc.semaphore(name))
        s = Sem(h, name)
        if hasattr(self, "all_sems"):
            self.all_sems.append(s)
        return s

    def dma_sem(self, key):
        if key not in self.dsem:
            self.dsem[key] = self.sem("q_" + str(key)[:12] + str(len(self.dsem)))
        return self.dsem[key]

    def op(self, eng, build, reads=(), writes=(), inc=True, dma=False, q=None,
           inc_n=None, extra=()):
        rkeys = [_tname(x) for x in reads]
        wkeys = [_tname(x) for x in writes]
        if q is not None:
            sem = q
        elif dma:
            nm0, k0 = wkeys[0]
            sem = self.dma_sem((nm0, k0) if nm0 in ("x1_d", "g1_d") else nm0)
        else:
            sem = self.esem[eng]
        if inc_n is None:
            inc_n = 16 if dma else 1
        waits = {}

        def add_wait(s, v):
            if v is None:
                raise RuntimeError(f"unresolved dep for {eng} op; producer must inc")
            if s.name.startswith("q_"):
                v = s.n
            if v <= 0:
                return
            if s is not self.esem[eng] and self.waited[eng].get(s.name, 0) >= v:
                return
            if s.name in waits:
                waits[s.name] = (s, max(waits[s.name][1], v))
            else:
                waits[s.name] = (s, v)

        def hit_keys(d, k):
            if not d:
                return []
            if k is None:
                return list(d.keys())
            return [kk for kk in (k, None) if kk in d]

        pe_skip = (eng == "tensor")

        def dep(s, v):
            if not (pe_skip and s is self.esem["tensor"]):
                add_wait(s, v)

        for name, k in rkeys:
            d = self.wr.get(name)
            for kk in hit_keys(d, k):
                dep(*d[kk])
        for name, k in wkeys:
            for peng, plist in self.pending.items():
                if peng == eng or (pe_skip and peng == "tensor"):
                    continue
                for _, pn, pk in plist:
                    if pn == name and (k is None or pk is None or pk == k):
                        raise RuntimeError(
                            f"write {name}[{k}] on {eng} conflicts with pending op on {peng}")
            d = self.wr.get(name)
            for kk in hit_keys(d, k):
                dep(*d[kk])
            rdn = self.rd.get(name)
            for kk in hit_keys(rdn, k):
                for s, v in rdn[kk].values():
                    dep(s, v)
        for s, v in extra:
            add_wait(s, v)

        if inc:
            sem.n += inc_n
            tgt = sem.n
        else:
            tgt = None
        wl = tuple(waits.values())
        for k2, (s, v) in waits.items():
            self.waited[eng][k2] = max(self.waited[eng].get(k2, 0), v)

        def f(e, build=build, wl=wl, sem=sem, tgt=tgt, inc=inc, inc_n=inc_n):
            for s, v in wl:
                e.wait_ge(s.h, v)
            ins = build(e)
            if inc:
                ins.then_inc(sem.h, inc_n)

        self.script[eng].append(f)

        def rec_read(name, k, s, v):
            self.rd.setdefault(name, {}).setdefault(k, {})[s.name] = (s, v)

        def rec_write(name, k, s, v):
            rdn = self.rd.get(name)
            if rdn is not None:
                if k is None:
                    self.rd.pop(name, None)
                else:
                    rdn.pop(k, None)
                    rdn.pop(None, None)
            if k is None:
                self.wr[name] = {None: (s, v)}
            else:
                dd = self.wr.setdefault(name, {})
                dd.pop(None, None)
                dd[k] = (s, v)

        if inc:
            if not dma and q is None:
                for kind, name, k in self.pending[eng]:
                    (rec_read if kind == "r" else rec_write)(name, k, sem, tgt)
                self.pending[eng] = []
            for name, k in rkeys:
                rec_read(name, k, sem, tgt)
            for name, k in wkeys:
                rec_write(name, k, sem, tgt)
        else:
            for name, k in rkeys:
                self.pending[eng].append(("r", name, k))
            for name, k in wkeys:
                self.pending[eng].append(("w", name, k))
                if k is None:
                    self.wr[name] = {None: (sem, None)}
                else:
                    dd = self.wr.setdefault(name, {})
                    dd.pop(None, None)
                    dd[k] = (sem, None)
        return tgt

    def wait_all(self, eng, sems=None):
        sems = sems if sems is not None else self.all_sems
        wl = []
        for s in sems:
            if s.n > 0 and self.waited[eng].get(s.name, 0) < s.n:
                wl.append((s, s.n))
                self.waited[eng][s.name] = s.n
        if not wl:
            return

        def f(e, wl=tuple(wl)):
            for s, v in wl:
                e.wait_ge(s.h, v)

        self.script[eng].append(f)

    def flush(self):
        for e in ENGINES:
            fns = self.script[e]
            if fns:
                getattr(self.blk, e)(lambda engobj, fns=fns: [f(engobj) for f in fns])
                self.script[e] = []


DBG_SHAPES = {
    "xnT": ([128, NJ * XN_W], F16),
    "srwkvT": ([128, NJ * T_LOC], F16),
    "s0": ([T_LOC, C], F16),
    "g0pre": ([128, NT * C], F16),
    "x1": ([T_LOC, C], F32),
    "QT": ([128, NJ * T_LOC], F16),
    "KT": ([128, NJ * T_LOC], F16),
    "VT": ([128, NJ * T_LOC], F16),
    "xn1T": ([128, NJ * T_LOC], F16),
    "ctxT": ([128, NJ * T_LOC], F16),
    "s1": ([128, NT * C], F16),
    "eku": ([128, NJ * T_LOC], F16),
    "Sab": ([128, 16], F32),
    "ainbin": ([128, 16], F32),
    "cw": ([128, NT * NE0], F32),
}


class _St:
    def __init__(self, nc, ctx, p, d, dbg):
        self.nc, self.ctx, self.p, self.d, self.dbg = nc, ctx, p, d, dbg
        self.cc_sems = [p.sem(f"sCC{i}") for i in range(3)]

    def sb(self, name, shape, dt=F32, scope=None, side=None):
        return (scope or self.ctx).enter_context(
            self.nc.sbuf_tensor(name, shape, dt, side=side))

    def ps(self, name, shape, dt=F32, scope=None):
        return (scope or self.ctx).enter_context(self.nc.psum_tensor(name, shape, dt))

    def barrier(self):
        sems = [s for s in self.p.all_sems if not s.name.startswith("sCC")]
        for e in ENGINES:
            assert not self.p.pending[e], f"dangling non-inc ops on {e}"
            self.p.wait_all(e, sems)
        self.p.flush()

    def dbg_store(self, name, src_ap):
        if name in self.dbg:
            self.p.op("sync", lambda e, ap=self.dbg[name], s=src_ap: e.dma_start(out=ap, in_=s),
                      reads=[src_ap], writes=[self.dbg[name]], dma=True)



CONST_VALS = [LN_EPS, -1.0, 1.0 / S_FULL, 0.0, 1.0, 0.125]


def emit_setup(st):
    p, d = st.p, st.d
    st.ident = st.sb("ident", [128, 128], F16)
    st.csts = st.sb("csts", [128, 8], F32)
    st.idblk = st.sb("idblk", [128, 64], F16)
    p.op("gpsimd", lambda e: e.dma_start(out=st.ident[:], in_=d["idn"]),
         reads=[d["idn"]], writes=[st.ident], dma=True)
    p.op("sync", lambda e: e.dma_start(out=st.csts[:, 0:len(CONST_VALS)], in_=d["cst"]),
         reads=[d["cst"]], writes=[st.csts], dma=True)
    p.op("gpsimd", lambda e: e.dma_start(out=st.idblk[:], in_=d["idn_blk"]),
         reads=[d["idn_blk"]], writes=[st.idblk], dma=True)
    for i, v in enumerate(CONST_VALS):
        st.nc.const_aps.register(F32, v, st.csts[:, i:i + 1])
    st.barrier()


def emit_l0_alloc(st, scope):
    p, d = st.p, st.d
    st.iota = st.sb("iota", [128, T_LOC], F16, scope=scope)
    st.iotar = st.sb("iotar", [128, T_LOC], F16, scope=scope)
    p.op("gpsimd", lambda e: e.dma_start(out=st.iota[:], in_=d["iota_t"]),
         reads=[d["iota_t"]], writes=[st.iota], dma=True)
    p.op("gpsimd", lambda e: e.dma_start(out=st.iotar[:], in_=d["iotar_t"]),
         reads=[d["iotar_t"]], writes=[st.iotar], dma=True)
    st.xnT = st.sb("xnT", [128, NJ * XN_W], F16, scope=scope)
    st.rstd0 = st.sb("rstd0", [128, 17], F32, scope=scope)
    st.nmr0 = st.sb("nmr0", [128, 17], F32, scope=scope)


def _ln_stats(st, sc, tag, src_fn, n_tiles, rstd, nmr):
    p = st.p
    sums = st.sb(tag + "_sums", [128, 17], F32, scope=sc)
    sqs = st.sb(tag + "_sqs", [128, 17], F32, scope=sc)
    xt = [st.sb(f"{tag}_x{i}", [128, C], F32, scope=sc) for i in range(4)]
    sqscr = [st.sb(f"{tag}_q{i}", [128, C], F32, scope=sc) for i in range(2)]
    p.op("vector", lambda e: e.memset(sums[:], 0.0), writes=[sums])
    p.op("vector", lambda e: e.memset(sqs[:], 0.0), writes=[sqs])
    for i in range(n_tiles):
        b = i % 4
        src, nr = src_fn(i)
        p.op("sync", lambda e, s=src, dd=xt[b], nr=nr: e.dma_start(out=dd[0:nr, :], in_=s),
             reads=[(src, i)], writes=[xt[b]], dma=True)
        p.op("vector", lambda e, b=b, i=i, nr=nr: e.reduce_sum(
            sums[0:nr, i:i + 1], xt[b][0:nr, :], axis=AX),
            reads=[xt[b]], writes=[sums])
        b2 = i % 2
        p.op("scalar", lambda e, b=b, b2=b2, i=i, nr=nr: e.activation(
            sqscr[b2][0:nr, :], xt[b][0:nr, :], AF.Square, accum_out=sqs[0:nr, i:i + 1]),
            reads=[xt[b]], writes=[sqscr[b2], sqs])
    n = n_tiles
    mean = st.sb(tag + "_mean", [128, 17], F32, scope=sc)
    var = st.sb(tag + "_var", [128, 17], F32, scope=sc)
    p.op("vector", lambda e, n=n: e.tensor_scalar(mean[:, 0:n], sums[:, 0:n], 1.0 / C, None, ALU.mult),
         reads=[sums], writes=[mean])
    p.op("vector", lambda e, n=n: e.scalar_tensor_tensor(var[:, 0:n], mean[:, 0:n], -1.0, mean[:, 0:n], ALU.mult, ALU.mult),
         reads=[mean], writes=[var])
    p.op("vector", lambda e, n=n: e.scalar_tensor_tensor(var[:, 0:n], sqs[:, 0:n], 1.0 / C, var[:, 0:n], ALU.mult, ALU.add),
         reads=[sqs, var], writes=[var])
    p.op("scalar", lambda e, n=n: e.activation(var[:, 0:n], var[:, 0:n], AF.Sqrt, bias=LN_EPS),
         reads=[var], writes=[var])
    p.op("vector", lambda e, n=n: e.reciprocal(rstd[:, 0:n], var[:, 0:n]),
         reads=[var], writes=[rstd])
    p.op("vector", lambda e, n=n: e.scalar_tensor_tensor(nmr[:, 0:n], mean[:, 0:n], -1.0, rstd[:, 0:n], ALU.mult, ALU.mult),
         reads=[mean, rstd], writes=[nmr])


def emit_ln0(st):
    p, d = st.p, st.d
    with ExitStack() as sc:
        def src(i):
            if i < 16:
                return d["x_main"][128 * i:128 * (i + 1), :], 128
            return d["x_halo"], 2

        _ln_stats(st, sc, "ln0", src, 17, st.rstd0, st.nmr0)
        xt = [st.sb(f"p0_x{i}", [128, C], F32, scope=sc) for i in range(4)]
        xn16 = [st.sb(f"p0_n{i}", [128, C], F16, scope=sc) for i in range(2)]
        pst = [st.ps(f"p0_t{i}", [128, 8 * 128], F16, scope=sc) for i in range(2)]
        xnr = st.xnT.ap().rearrange("p (j t) -> p j t", j=NJ)
        for i in range(17):
            b = i % 4
            sap, nr = src(i)
            p.op("sync", lambda e, s=sap, dd=xt[b], nr=nr: e.dma_start(out=dd[0:nr, :], in_=s),
                 reads=[sap], writes=[xt[b]], dma=True)
            b2 = i % 2
            p.op("vector", lambda e, b=b, b2=b2, i=i, nr=nr: e.tensor_scalar(
                xn16[b2][0:nr, :], xt[b][0:nr, :], st.rstd0[0:nr, i:i + 1], st.nmr0[0:nr, i:i + 1],
                ALU.mult, ALU.add),
                reads=[xt[b], st.rstd0, st.nmr0], writes=[xn16[b2]])
            for j in range(NJ):
                p.op("tensor", lambda e, b2=b2, j=j, nr=nr: e.transpose(
                    pst[b2][:, 128 * j:128 * j + nr], xn16[b2][0:nr, 128 * j:128 * (j + 1)],
                    st.ident[0:nr, 0:nr]),
                    reads=[xn16[b2], st.ident], writes=[pst[b2]], inc=(j == NJ - 1))
            if i < 16:
                dst = xnr[:, :, 1 + 128 * i: 1 + 128 * (i + 1)]
                srcap = pst[b2].ap().rearrange("p (j t) -> p j t", j=NJ)
            else:
                dst = xnr[:, :, 0:2050:2049]
                srcap = pst[b2].ap().rearrange("p (j t) -> p j t", j=NJ)[:, :, 0:2]
            p.op("vector", lambda e, dst=dst, srcap=srcap, b2=b2: e.tensor_copy(dst, srcap),
                 reads=[pst[b2]], writes=[st.xnT])
        st.dbg_store("xnT", st.xnT[:])
    st.barrier()


def _load_wT(st, bufs, wname, j, slot):
    p, d = st.p, st.d
    w16, pswt, wT = bufs
    p.op("gpsimd", lambda e, wname=wname, j=j, slot=slot: e.dma_start(
        out=w16[slot][:], in_=d[wname][128 * j:128 * (j + 1), :]),
        reads=[d[wname]], writes=[w16[slot]], dma=True)
    for j2 in range(NJ):
        p.op("tensor", lambda e, slot=slot, j2=j2: e.transpose(
            pswt[slot][:, 128 * j2:128 * (j2 + 1)], w16[slot][:, 128 * j2:128 * (j2 + 1)], st.ident[:]),
            reads=[w16[slot], st.ident], writes=[pswt[slot]], inc=(j2 == NJ - 1))
    p.op("vector", lambda e, slot=slot: e.tensor_copy(wT[slot][:], pswt[slot][:]),
         reads=[pswt[slot]], writes=[wT[slot]])
    return wT[slot]


def emit_l0_attn(st):
    p, d = st.p, st.d
    xn = st.xnT.ap().rearrange("p (j t) -> p j t", j=NJ)
    with ExitStack() as sc:
        sb = lambda nm, sh, dt=F32: st.sb(nm, sh, dt, scope=sc)
        vecs = sb("l0_vecs", [128, 40], F32)
        for i, nm in enumerate(["mixk_fm", "mixv_fm", "mixr_fm", "sd_fm", "sf_fm"]):
            p.op("sync", lambda e, nm=nm, i=i: e.dma_start(out=vecs[:, 8 * i:8 * (i + 1)], in_=d[nm]),
                 reads=[d[nm]], writes=[vecs], dma=True)
        mixk, mixv, mixr = vecs[:, 0:8], vecs[:, 8:16], vecs[:, 16:24]
        sdf, sff = vecs[:, 24:32], vecs[:, 32:40]
        aux = sb("l0_aux", [128, 64], F32)
        w8, u8, ew = aux[:, 0:8], aux[:, 8:16], aux[:, 16:24]
        expu, expnu = aux[:, 24:32], aux[:, 32:40]
        ain, binn = aux[:, 40:48], aux[:, 48:56]
        p.op("vector", lambda e: e.tensor_scalar(w8, sdf, 1.0 / S_FULL, None, ALU.mult),
             reads=[vecs], writes=[aux])
        p.op("vector", lambda e: e.tensor_scalar(u8, sff, 1.0 / S_FULL, None, ALU.mult),
             reads=[vecs, aux], writes=[aux])
        p.op("scalar", lambda e: e.activation(ew, w8, AF.Exp), reads=[aux], writes=[aux])
        p.op("scalar", lambda e: e.activation(expu, u8, AF.Exp), reads=[aux], writes=[aux])
        p.op("scalar", lambda e: e.activation(expnu, u8, AF.Exp, scale=-1.0),
             reads=[aux], writes=[aux])

        eku = sb("ekuT", [128, NJ * T_LOC], F16)
        vT = sb("vTbuf", [128, NJ * T_LOC], F16)
        srT = sb("srT", [128, NJ * T_LOC], F16)
        Sab = sb("Sab", [128, 16], F32)
        ekur = eku.ap().rearrange("p (j t) -> p j t", j=NJ)
        vTr = vT.ap().rearrange("p (j t) -> p j t", j=NJ)
        srr = srT.ap().rearrange("p (j t) -> p j t", j=NJ)
        wbufs = ([sb(f"l0_w16_{i}", [128, C], F16) for i in range(2)],
                 [st.ps(f"l0_pswt{i}", [128, NJ * 128], F16, scope=sc) for i in range(2)],
                 [sb(f"l0_wT{i}", [128, NJ * 128], F16) for i in range(2)])
        psk = st.ps("l0_psk", [128, 512], F32, scope=sc)
        psv = st.ps("l0_psv", [128, 512], F32, scope=sc)

        with ExitStack() as sc2:
            sb2 = lambda nm, sh, dt=F32: st.sb(nm, sh, dt, scope=sc2)
            mixT = st.sb("mixT", [128, NJ * T_LOC], F16, scope=sc2)
            mxr = mixT.ap().rearrange("p (j t) -> p j t", j=NJ)
            dscr = st.sb("l0_dscr", [128, T_LOC], F16, scope=sc2)

            def emit_mix(mixcol, dstT, dsc):
                dstr = dstT.ap().rearrange("p (j t) -> p j t", j=NJ)
                for j in range(NJ):
                    main = xn[:, j, 1:1 + T_LOC]
                    xx = xn[:, j, 0:T_LOC] if j < 2 else (xn[:, j, 2:2 + T_LOC] if j < 4 else main)
                    if j < 4:
                        p.op("vector", lambda e, main=main, xx=xx, dsc=dsc: e.tensor_sub(dsc[:], main, xx),
                             reads=[st.xnT], writes=[dsc])
                        p.op("vector", lambda e, j=j, xx=xx, mixcol=mixcol, dstr=dstr, dsc=dsc, dstT=dstT: e.scalar_tensor_tensor(
                            dstr[:, j, :], dsc[:], mixcol[:, j:j + 1], xx, ALU.mult, ALU.add),
                            reads=[dsc, vecs, st.xnT], writes=[dstT])
                    else:
                        p.op("vector", lambda e, j=j, main=main, dstr=dstr, dstT=dstT: e.tensor_copy(dstr[:, j, :], main),
                             reads=[st.xnT], writes=[dstT])

            def mm8(psum, wTbuf, rhsr, tc, extra_reads):
                for cj in range(8):
                    p.op("tensor", lambda e, cj=cj, tc=tc, psum=psum, wTbuf=wTbuf, rhsr=rhsr: e.matmul(
                        psum[:, :], wTbuf[:, 128 * cj:128 * (cj + 1)],
                        rhsr[:, cj, 512 * tc:512 * (tc + 1)], start=(cj == 0), stop=(cj == 7)),
                        reads=[wTbuf] + extra_reads, writes=[psum], inc=(cj == 7))

            emit_mix(mixk, mixT, dscr)
            wts = {0: _load_wT(st, wbufs, "Wk", 0, 0)}
            for j in range(NJ):
                wTk = wts.pop(j)
                if j + 1 < NJ:
                    wts[j + 1] = _load_wT(st, wbufs, "Wk", j + 1, (j + 1) % 2)
                for tc in range(4):
                    mm8(psk, wTk, mxr, tc, [mixT])
                    p.op("scalar", lambda e, j=j, tc=tc: e.activation(
                        ekur[:, j, 512 * tc:512 * (tc + 1)], psk[:, :], AF.Exp, bias=u8[:, j:j + 1]),
                        reads=[psk, aux], writes=[eku])
            dscr2 = st.sb("l0_dscr2", [128, T_LOC], F16, scope=sc2)
            emit_mix(mixv, srT, dscr2)
            srr_mix = srT.ap().rearrange("p (j t) -> p j t", j=NJ)
            wts = {0: _load_wT(st, wbufs, "Wv", 0, 0)}
            for j in range(NJ):
                wTv = wts.pop(j)
                if j + 1 < NJ:
                    wts[j + 1] = _load_wT(st, wbufs, "Wv", j + 1, (j + 1) % 2)
                for tc in range(4):
                    mm8(psv, wTv, srr_mix, tc, [srT])
                    p.op("vector", lambda e, j=j, tc=tc: e.tensor_copy(
                        vTr[:, j, 512 * tc:512 * (tc + 1)], psv[:, :]),
                        reads=[psv], writes=[vT])
            ekt = sb2("l0_ek", [128, T_LOC], F16)
            ppt = sb2("l0_pp", [128, T_LOC], F16)
            dmat = sb2("l0_dm16", [128, T_LOC], F16)
            junk = dscr
            for j in range(NJ):
                p.op("vector", lambda e, j=j: e.tensor_scalar(
                    ekt[:], ekur[:, j, :], expnu[:, j:j + 1], None, ALU.mult),
                    reads=[eku, aux], writes=[ekt])
                p.op("vector", lambda e, j=j: e.tensor_mul(ppt[:], ekt[:], vTr[:, j, :]),
                     reads=[ekt, vT], writes=[ppt])
                p.op("scalar", lambda e, j=j: e.activation(dmat[:], st.iotar[:], AF.Exp, scale=w8[:, j:j + 1]),
                     reads=[st.iotar, aux], writes=[dmat])
                p.op("vector", lambda e, j=j: e.scalar_tensor_tensor(
                    junk[:], ppt[:], 1.0, dmat[:], ALU.mult, ALU.mult, accum_out=Sab[:, j:j + 1]),
                    reads=[ppt, dmat], writes=[junk, Sab])
                p.op("vector", lambda e, j=j: e.scalar_tensor_tensor(
                    junk[:], ekt[:], 1.0, dmat[:], ALU.mult, ALU.mult, accum_out=Sab[:, 8 + j:9 + j]),
                    reads=[ekt, dmat], writes=[junk, Sab])
            st.dbg_store("eku", eku[:])
            st.dbg_store("Sab", Sab[:])

            p.op("sync", lambda e: e.dma_start(out=d["ccS_in"], in_=Sab[:]),
                 reads=[Sab], writes=[d["ccS_in"]], dma=True)
            p.op("gpsimd", lambda e: e.collective_compute(
                "AllGather", ALU.bypass, replica_groups=GROUPS,
                ins=[d["ccS_in"].opt()], outs=[d["ccS_out"].opt()]),
                reads=[d["ccS_in"]], writes=[d["ccS_out"]], q=st.cc_sems[0], inc_n=1)
            Sgf = sb2("Sgf", [128, 64], F32)
            for r in range(4):
                p.op("gpsimd", lambda e, r=r: e.dma_start(
                    out=Sgf[:, 16 * r:16 * (r + 1)], in_=d["ccS_out"][128 * r:128 * (r + 1), :]),
                    reads=[d["ccS_out"]], writes=[Sgf], dma=True)
            qrow = sb2("qrow", [1, 8], F16)
            p.op("gpsimd", lambda e: e.dma_start(out=qrow[0:1, 0:4], in_=d["qmask"]),
                 reads=[d["qmask"]], writes=[qrow], dma=True)
            p.op("gpsimd", lambda e: e.dma_start(out=qrow[0:1, 4:8], in_=d["qcoef"]),
                 reads=[d["qcoef"]], writes=[qrow], dma=True)
            ones128 = sb2("ones128", [1, 128], F16)
            p.op("vector", lambda e: e.memset(ones128[:], 1.0), writes=[ones128])
            psq8 = st.ps("l0_psq8", [128, 8], F32, scope=sc2)
            p.op("tensor", lambda e: e.matmul(psq8[:, :], ones128[0:1, :], qrow[0:1, :],
                                              start=True, stop=True),
                 reads=[ones128, qrow], writes=[psq8])
            qmb = aux[:, 56:60]
            qcb = aux[:, 60:64]
            p.op("vector", lambda e: e.tensor_copy(aux[:, 56:64], psq8[:, :]),
                 reads=[psq8], writes=[aux])
            sc8 = sb2("l0_sc8", [128, 8], F32)
            tm8 = sb2("l0_tm8", [128, 8], F32)
            p.op("vector", lambda e: e.memset(aux[:, 40:56], 0.0), writes=[aux])
            for r in range(4):
                p.op("vector", lambda e, r=r: e.tensor_scalar(
                    sc8[:], w8, qcb[:, r:r + 1], None, ALU.mult),
                    reads=[aux], writes=[sc8])
                p.op("scalar", lambda e: e.activation(sc8[:], sc8[:], AF.Exp),
                     reads=[sc8], writes=[sc8])
                p.op("vector", lambda e, r=r: e.tensor_scalar(
                    sc8[:], sc8[:], qmb[:, r:r + 1], None, ALU.mult),
                    reads=[sc8, aux], writes=[sc8])
                p.op("vector", lambda e, r=r: e.tensor_mul(tm8[:], Sgf[:, 16 * r:16 * r + 8], sc8[:]),
                     reads=[Sgf, sc8], writes=[tm8])
                p.op("vector", lambda e: e.tensor_add(ain, ain, tm8[:]),
                     reads=[tm8, aux], writes=[aux])
                p.op("vector", lambda e, r=r: e.tensor_mul(tm8[:], Sgf[:, 16 * r + 8:16 * r + 16], sc8[:]),
                     reads=[Sgf, sc8], writes=[tm8])
                p.op("vector", lambda e: e.tensor_add(binn, binn, tm8[:]),
                     reads=[tm8, aux], writes=[aux])
            st.dbg_store("ainbin", aux[:, 40:56])

            emit_mix(mixr, mixT, dscr)
            wts = {0: _load_wT(st, wbufs, "Wr", 0, 0)}
            for j in range(NJ):
                wTr = wts.pop(j)
                if j + 1 < NJ:
                    wts[j + 1] = _load_wT(st, wbufs, "Wr", j + 1, (j + 1) % 2)
                for tc in range(4):
                    mm8(psk, wTr, mxr, tc, [mixT])
                    p.op("scalar", lambda e, j=j, tc=tc: e.activation(
                        srr[:, j, 512 * tc:512 * (tc + 1)], psk[:, :], AF.Sigmoid),
                        reads=[psk], writes=[srT])

        st.barrier()
        ekt = sb("l02_ek", [128, T_LOC], F16)
        ppt = sb("l02_pp", [128, T_LOC], F16)
        dmat = sb("l02_dm", [128, T_LOC], F32)
        Abuf = sb("Abuf", [128, T_LOC + 1], F32)
        Bbuf = sb("Bbuf", [128, T_LOC + 1], F32)
        ewb = sb("ewb", [128, T_LOC], F32)
        numb = sb("numb", [128, T_LOC], F32)
        denb = sb("denb", [128, T_LOC], F32)
        p.op("vector", lambda e: e.memset(Abuf[:, 0:1], 0.0), writes=[Abuf])
        p.op("vector", lambda e: e.memset(Bbuf[:, 0:1], 0.0), reads=[Abuf], writes=[Bbuf])
        for j in range(NJ):
            p.op("vector", lambda e, j=j: e.tensor_scalar(
                ekt[:], ekur[:, j, :], expnu[:, j:j + 1], None, ALU.mult),
                reads=[eku, aux], writes=[ekt])
            p.op("vector", lambda e, j=j: e.tensor_mul(ppt[:], ekt[:], vTr[:, j, :]),
                 reads=[ekt, vT], writes=[ppt])
            p.op("vector", lambda e, j=j: e.tensor_scalar(
                ewb[:], st.iota[:], 0.0, ew[:, j:j + 1], ALU.mult, ALU.add),
                reads=[st.iota, aux], writes=[ewb])
            p.op("vector", lambda e: e.tensor_tensor_scan(
                Abuf[:, 1:], ewb[:], ppt[:], 0.0, ALU.mult, ALU.add),
                reads=[ewb, ppt], writes=[Abuf])
            p.op("vector", lambda e: e.tensor_tensor_scan(
                Bbuf[:, 1:], ewb[:], ekt[:], 0.0, ALU.mult, ALU.add),
                reads=[ewb, ekt], writes=[Bbuf])
            p.op("scalar", lambda e, j=j: e.activation(dmat[:], st.iota[:], AF.Exp, scale=w8[:, j:j + 1]),
                 reads=[st.iota, aux], writes=[dmat])
            p.op("vector", lambda e, j=j: e.scalar_tensor_tensor(
                numb[:], dmat[:], ain[:, j:j + 1], Abuf[:, 0:T_LOC], ALU.mult, ALU.add),
                reads=[dmat, aux, Abuf], writes=[numb])
            p.op("vector", lambda e, j=j: e.scalar_tensor_tensor(
                numb[:], ppt[:], expu[:, j:j + 1], numb[:], ALU.mult, ALU.add),
                reads=[ppt, aux, numb], writes=[numb])
            p.op("vector", lambda e, j=j: e.scalar_tensor_tensor(
                denb[:], dmat[:], binn[:, j:j + 1], Bbuf[:, 0:T_LOC], ALU.mult, ALU.add),
                reads=[dmat, aux, Bbuf], writes=[denb])
            p.op("vector", lambda e, j=j: e.scalar_tensor_tensor(
                denb[:], ekt[:], expu[:, j:j + 1], denb[:], ALU.mult, ALU.add),
                reads=[ekt, aux, denb], writes=[denb])
            p.op("scalar", lambda e: e.activation(dmat[:], denb[:], AF.Ln),
                 reads=[denb], writes=[dmat])
            p.op("scalar", lambda e: e.activation(dmat[:], dmat[:], AF.Exp, scale=-1.0),
                 reads=[dmat], writes=[dmat])
            p.op("vector", lambda e: e.tensor_mul(numb[:], numb[:], dmat[:]),
                 reads=[numb, dmat], writes=[numb])
            p.op("vector", lambda e, j=j: e.tensor_mul(vTr[:, j, :], numb[:], srr[:, j, :]),
                 reads=[numb, srT], writes=[vT])
        st.dbg_store("srwkvT", vT[:])

        s0T = eku
        s0r = s0T.ap().rearrange("p (j t) -> p j t", j=NJ)
        wts = {0: _load_wT(st, wbufs, "Wo", 0, 0)}
        for j in range(NJ):
            wTo = wts.pop(j)
            if j + 1 < NJ:
                wts[j + 1] = _load_wT(st, wbufs, "Wo", j + 1, (j + 1) % 2)
            for tc in range(4):
                for cj in range(8):
                    p.op("tensor", lambda e, cj=cj, tc=tc, wTo=wTo: e.matmul(
                        psv[:, :], wTo[:, 128 * cj:128 * (cj + 1)],
                        vTr[:, cj, 512 * tc:512 * (tc + 1)], start=(cj == 0), stop=(cj == 7)),
                        reads=[wTo, vT], writes=[psv], inc=(cj == 7))
                p.op("vector", lambda e, j=j, tc=tc: e.tensor_copy(
                    s0r[:, j, 512 * tc:512 * (tc + 1)], psv[:, :]),
                    reads=[psv], writes=[s0T])
        pst2 = st.ps("l0_pst2", [128, 1024], F16, scope=sc)
        s0tile = [sb(f"s0t{i}", [128, C], F16) for i in range(2)]
        for i in range(NT):
            b = i % 2
            for j in range(NJ):
                p.op("tensor", lambda e, i=i, j=j: e.transpose(
                    pst2[:, 128 * j:128 * (j + 1)], s0r[:, j, 128 * i:128 * (i + 1)], st.ident[:]),
                    reads=[s0T, st.ident], writes=[pst2], inc=(j == NJ - 1))
            p.op("vector", lambda e, i=i, b=b: e.tensor_copy(s0tile[b][:], pst2[:]),
                 reads=[pst2], writes=[s0tile[b]])
            p.op("sync", lambda e, i=i, b=b: e.dma_start(
                out=d["s0_d"][128 * i:128 * (i + 1), :], in_=s0tile[b][:]),
                reads=[s0tile[b]], writes=[(d["s0_d"], i)], dma=True)
        st.dbg_store("s0", d["s0_d"])
    st.barrier()



def emit_moe(st):
    p, d = st.p, st.d
    xnr = st.xnT.ap().rearrange("p (j t) -> p j t", j=NJ)

    def xns(j, t0, t1):
        return xnr[:, j, 1 + t0:1 + t1]

    with ExitStack() as sc:
        sb = lambda nm, sh, dt=F32: st.sb(nm, sh, dt, scope=sc)
        gws = sb("gws16", [128, 64], F16)
        p.op("gpsimd", lambda e: e.dma_start(out=gws[:], in_=d["gws"]),
             reads=[d["gws"]], writes=[gws], dma=True)
        psl = st.ps("moe_psl", [128, 8], F32, scope=sc)
        cw = sb("cw", [128, NT * NE0], F32)
        Escr = sb("moe_E", [128, 8], F32)
        m1s = sb("moe_m1", [128, NT], F32)
        m2s = sb("moe_m2", [128, NT], F32)
        zs = sb("moe_z", [128, NT], F32)
        nmax = sb("moe_nm", [128, 1], F32)
        msk = sb("moe_msk", [128, 8], F32)
        gm = sb("moe_gm", [128, 8], F32)
        den = sb("moe_den", [128, NT], F32)
        cwT = sb("cwT", [8, T_LOC], F16)
        cw16 = sb("cw16", [128, 8], F16)
        pscT = st.ps("moe_pscT", [8, 128], F16, scope=sc)
        for tt in range(NT):
            for cj in range(NJ):
                p.op("tensor", lambda e, tt=tt, cj=cj: e.matmul(
                    psl[:, :], xns(cj, 128 * tt, 128 * (tt + 1)), gws[:, 8 * cj:8 * (cj + 1)],
                    start=(cj == 0), stop=(cj == 7)),
                    reads=[st.xnT, gws], writes=[psl], inc=(cj == 7))
            p.op("vector", lambda e: e.reduce_max(nmax[:, 0:1], psl[:, :], axis=AX, negate=True),
                 reads=[psl], writes=[nmax])
            p.op("scalar", lambda e: e.activation(Escr[:, :], psl[:, :], AF.Exp, bias=nmax[:, 0:1]),
                 reads=[psl, nmax], writes=[Escr])
            p.op("vector", lambda e, tt=tt: e.reduce_sum(zs[:, tt:tt + 1], Escr[:, :], axis=AX),
                 reads=[Escr], writes=[zs])
            p.op("vector", lambda e, tt=tt: e.reduce_max(m1s[:, tt:tt + 1], Escr[:, :], axis=AX),
                 reads=[Escr], writes=[m1s])
            p.op("vector", lambda e, tt=tt: e.tensor_scalar(
                msk[:, :], Escr[:, :], m1s[:, tt:tt + 1], None, ALU.is_ge),
                reads=[Escr, m1s], writes=[msk])
            p.op("vector", lambda e: e.scalar_tensor_tensor(
                gm[:, :], msk[:, :], -1e30, Escr[:, :], ALU.mult, ALU.add),
                reads=[msk, Escr], writes=[gm])
            p.op("vector", lambda e, tt=tt: e.reduce_max(m2s[:, tt:tt + 1], gm[:, :], axis=AX),
                 reads=[gm], writes=[m2s])
            p.op("vector", lambda e, tt=tt: e.tensor_scalar(
                msk[:, :], Escr[:, :], m2s[:, tt:tt + 1], None, ALU.is_ge),
                reads=[Escr, m2s], writes=[msk])
            p.op("vector", lambda e, tt=tt: e.tensor_mul(
                cw[:, NE0 * tt:NE0 * (tt + 1)], Escr[:, :], msk[:, :]),
                reads=[Escr, msk], writes=[cw])
        p.op("vector", lambda e: e.tensor_add(den[:, :], m1s[:, :], m2s[:, :]),
             reads=[m1s, m2s], writes=[den])
        p.op("vector", lambda e: e.scalar_tensor_tensor(den[:, :], zs[:, :], 1e-8, den[:, :], ALU.mult, ALU.add),
             reads=[zs, den], writes=[den])
        p.op("vector", lambda e: e.reciprocal(den[:, :], den[:, :]), reads=[den], writes=[den])
        for tt in range(NT):
            p.op("vector", lambda e, tt=tt: e.tensor_scalar(
                cw[:, NE0 * tt:NE0 * (tt + 1)], cw[:, NE0 * tt:NE0 * (tt + 1)], den[:, tt:tt + 1],
                None, ALU.mult),
                reads=[cw, den], writes=[cw])
            p.op("vector", lambda e, tt=tt: e.tensor_copy(cw16[:, :], cw[:, NE0 * tt:NE0 * (tt + 1)]),
                 reads=[cw], writes=[cw16])
            p.op("tensor", lambda e: e.transpose(pscT[:, :], cw16[:, :], st.ident[:]),
                 reads=[cw16, st.ident], writes=[pscT])
            p.op("vector", lambda e, tt=tt: e.tensor_copy(cwT[:, 128 * tt:128 * (tt + 1)], pscT[0:8, :]),
                 reads=[pscT], writes=[cwT])
        st.dbg_store("cw", cw[:])

        g0 = sb("g0acc", [128, NT * C], F16)
        b2s = sb("b2s", [8, C], F16)
        p.op("gpsimd", lambda e: e.dma_start(out=b2s[:], in_=d["b2"]),
             reads=[d["b2"]], writes=[b2s], dma=True)
        psb = st.ps("moe_psb", [128, 1024], F32, scope=sc)
        xts = [sb(f"moe_x{i}", [128, C], F32) for i in range(2)]
        for tt in range(NT):
            b = tt % 2
            for h in range(2):
                p.op("tensor", lambda e, tt=tt, h=h: e.matmul(
                    psb[:, 512 * h:512 * (h + 1)], cwT[:, 128 * tt:128 * (tt + 1)],
                    b2s[:, 512 * h:512 * (h + 1)], start=True, stop=True),
                    reads=[cwT, b2s], writes=[psb], inc=(h == 1))
            p.op("sync", lambda e, tt=tt, b=b: e.dma_start(
                out=xts[b][:], in_=d["x_main"][128 * tt:128 * (tt + 1), :]),
                reads=[d["x_main"]], writes=[xts[b]], dma=True)
            p.op("vector", lambda e, tt=tt, b=b: e.tensor_scalar(
                xts[b][:], xts[b][:], st.rstd0[:, tt:tt + 1], st.nmr0[:, tt:tt + 1], ALU.mult, ALU.add),
                reads=[xts[b], st.rstd0, st.nmr0], writes=[xts[b]])
            p.op("vector", lambda e, tt=tt, b=b: e.tensor_add(
                g0[:, C * tt:C * (tt + 1)], xts[b][:], psb[:, :]),
                reads=[xts[b], psb], writes=[g0])

        W1s = sb("W1s", [128, NJ * HID], F16)
        W2s = sb("W2s", [128, 16 * C], F16)
        hT = sb("moe_hT", [128, 16 * 512], F16)
        b1s = sb("b1s", [128, 128], F32)
        ysc = sb("moe_ysc", [128, 1024], F16)
        p.op("sync", lambda e: e.dma_start(out=b1s[:], in_=d["b1_fm"]),
             reads=[d["b1_fm"]], writes=[b1s], dma=True)
        psh = [st.ps(f"moe_psh{i}", [128, 512], F32, scope=sc) for i in range(2)]
        psy = st.ps("moe_psy", [128, 1024], F32, scope=sc)
        w1r = W1s.ap().rearrange("p (j h) -> p j h", j=NJ)

        for e_ in range(NE0):
            p.op("gpsimd", lambda e, e_=e_: e.dma_start(
                out=W1s.ap().rearrange("p (j h) -> p j h", j=NJ),
                in_=d["W1"][e_].rearrange("(j p) h -> p j h", p=128)),
                reads=[d["W1"]], writes=[W1s], dma=True)
            p.op("gpsimd", lambda e, e_=e_: e.dma_start(
                out=W2s.ap().rearrange("p (h o) -> p h o", h=16),
                in_=d["W2"][e_].rearrange("(h p) o -> p h o", p=128)),
                reads=[d["W2"]], writes=[W2s], dma=True)
            for tc in range(4):
                for ht in range(16):
                    pb = ht % 2
                    for cj in range(NJ):
                        p.op("tensor", lambda e, tc=tc, ht=ht, cj=cj, pb=pb: e.matmul(
                            psh[pb][:, :], w1r[:, cj, 128 * ht:128 * (ht + 1)],
                            xns(cj, 512 * tc, 512 * (tc + 1)),
                            start=(cj == 0), stop=(cj == 7)),
                            reads=[W1s, st.xnT], writes=[psh[pb]], inc=(cj == 7))
                    p.op("scalar", lambda e, ht=ht, pb=pb, e_=e_: e.activation(
                        hT[:, 512 * ht:512 * (ht + 1)], psh[pb][:, :], AF.Relu,
                        bias=b1s[:, 16 * e_ + ht:16 * e_ + ht + 1]),
                        reads=[psh[pb], b1s], writes=[hT])
                for tt2 in range(4):
                    tt = 4 * tc + tt2
                    for ht in range(16):
                        for hh in range(2):
                            p.op("tensor", lambda e, tt2=tt2, ht=ht, hh=hh: e.matmul(
                                psy[:, 512 * hh:512 * (hh + 1)],
                                hT[:, 512 * ht + 128 * tt2:512 * ht + 128 * (tt2 + 1)],
                                W2s[:, C * ht + 512 * hh:C * ht + 512 * (hh + 1)],
                                start=(ht == 0), stop=(ht == 15)),
                                reads=[hT, W2s], writes=[psy],
                                inc=(ht == 15 and hh == 1))
                    p.op("scalar", lambda e, tt=tt, e_=e_: e.activation(
                        ysc[:, :], psy[:, :], AF.Copy, scale=cw[:, NE0 * tt + e_:NE0 * tt + e_ + 1]),
                        reads=[psy, cw], writes=[ysc])
                    p.op("vector", lambda e, tt=tt: e.tensor_add(
                        g0[:, C * tt:C * (tt + 1)], g0[:, C * tt:C * (tt + 1)], ysc[:, :]),
                        reads=[ysc, g0], writes=[g0])
        st.dbg_store("g0pre", g0[:])

        sg = sb("moe_sg", [128, C], F16)
        s0tl = [sb(f"moe_s0{i}", [128, C], F16) for i in range(2)]
        for tt in range(NT):
            b = tt % 2
            p.op("sync", lambda e, tt=tt, b=b: e.dma_start(
                out=s0tl[b][:], in_=d["s0_d"][128 * tt:128 * (tt + 1), :]),
                reads=[(d["s0_d"], tt)], writes=[s0tl[b]], dma=True)
            p.op("scalar", lambda e, tt=tt: e.activation(
                g0[:, C * tt:C * (tt + 1)], g0[:, C * tt:C * (tt + 1)], AF.Silu),
                reads=[g0], writes=[g0])
            p.op("vector", lambda e, tt=tt, b=b: e.tensor_mul(
                sg[:, :], s0tl[b][:], g0[:, C * tt:C * (tt + 1)]),
                reads=[s0tl[b], g0], writes=[sg])
            p.op("sync", lambda e, tt=tt, b=b: e.dma_start(
                out=xts[b][:], in_=d["x_main"][128 * tt:128 * (tt + 1), :]),
                reads=[d["x_main"]], writes=[xts[b]], dma=True)
            p.op("vector", lambda e, tt=tt, b=b: e.tensor_add(xts[b][:], xts[b][:], sg[:, :]),
                 reads=[sg, xts[b]], writes=[xts[b]])
            p.op("sync", lambda e, tt=tt, b=b: e.dma_start(
                out=d["x1_d"][128 * tt:128 * (tt + 1), :], in_=xts[b][:]),
                reads=[xts[b]], writes=[(d["x1_d"], tt)], dma=True)
        st.dbg_store("x1", d["x1_d"])
    st.barrier()



def emit_l1_qkv(st, scope_l1, scope_q, scope_x):
    p, d = st.p, st.d
    st.rstd1 = st.sb("rstd1", [128, 16], F32, scope=scope_l1)
    st.nmr1 = st.sb("nmr1", [128, 16], F32, scope=scope_l1)
    st.ctxT = st.sb("ctxT", [128, NJ * T_LOC], F16, scope=scope_l1, side="right")
    st.scoresT = st.sb("scoresT", [128, NE1 * T_LOC], F16, scope=scope_l1, side="right")
    st.xn1T = st.sb("xn1T", [128, NJ * T_LOC], F16, scope=scope_x)
    xnr1 = st.xn1T.ap().rearrange("p (j t) -> p j t", j=NJ)
    with ExitStack() as sc:
        sb = lambda nm, sh, dt=F32: st.sb(nm, sh, dt, scope=sc)

        def src(i):
            return d["x1_d"][128 * i:128 * (i + 1), :], 128

        with ExitStack() as sc_stats:
            _ln_stats(st, sc_stats, "ln1", src, 16, st.rstd1, st.nmr1)
            st.barrier()
        xt = [sb(f"p4_x{i}", [128, C], F32) for i in range(4)]
        xn16 = [sb(f"p4_n{i}", [128, C], F16) for i in range(2)]
        pst = [st.ps(f"p4_t{i}", [128, 8 * 128], F16, scope=sc) for i in range(2)]
        for i in range(NT):
            b = i % 4
            p.op("sync", lambda e, i=i, b=b: e.dma_start(
                out=xt[b][:], in_=d["x1_d"][128 * i:128 * (i + 1), :]),
                reads=[(d["x1_d"], i)], writes=[xt[b]], dma=True)
            b2 = i % 2
            p.op("vector", lambda e, b=b, b2=b2, i=i: e.tensor_scalar(
                xn16[b2][:], xt[b][:], st.rstd1[:, i:i + 1], st.nmr1[:, i:i + 1], ALU.mult, ALU.add),
                reads=[xt[b], st.rstd1, st.nmr1], writes=[xn16[b2]])
            for j in range(NJ):
                p.op("tensor", lambda e, b2=b2, j=j: e.transpose(
                    pst[b2][:, 128 * j:128 * (j + 1)], xn16[b2][:, 128 * j:128 * (j + 1)], st.ident[:]),
                    reads=[xn16[b2], st.ident], writes=[pst[b2]], inc=(j == NJ - 1))
            p.op("vector", lambda e, b2=b2, i=i: e.tensor_copy(
                xnr1[:, :, 128 * i:128 * (i + 1)], pst[b2].ap().rearrange("p (j t) -> p j t", j=NJ)),
                reads=[pst[b2]], writes=[st.xn1T])
        st.dbg_store("xn1T", st.xn1T[:])

        kvtile = [sb(f"p4_kv{i}", [128, 512], F16) for i in range(2)]
        bqs = sb("bqs", [128, 24], F32)
        for i, nm in enumerate(["bq_fm", "bk_fm", "bv_fm"]):
            p.op("sync", lambda e, nm=nm, i=i: e.dma_start(out=bqs[:, 8 * i:8 * (i + 1)], in_=d[nm]),
                 reads=[d[nm]], writes=[bqs], dma=True)
        wsb = [sb(f"p4_w{i}", [128, NJ * 128], F16) for i in range(2)]
        psq = [st.ps(f"p4_q{i}", [128, 512], F32, scope=sc) for i in range(2)]

        def proj(wname, dstT, dst_dram, bcol, j, slot):
            p.op("gpsimd", lambda e, wname=wname, j=j, slot=slot: e.dma_start(
                out=wsb[slot][:], in_=d[wname + "R"][j]),
                reads=[d[wname + "R"]], writes=[wsb[slot]], dma=True)
            dstr = dstT.ap().rearrange("p (j t) -> p j t", j=NJ) if dstT is not None else None
            for tc in range(4):
                pb = tc % 2
                for cj in range(NJ):
                    p.op("tensor", lambda e, tc=tc, cj=cj, pb=pb, slot=slot: e.matmul(
                        psq[pb][:, :], wsb[slot][:, 128 * cj:128 * (cj + 1)],
                        xnr1[:, cj, 512 * tc:512 * (tc + 1)], start=(cj == 0), stop=(cj == 7)),
                        reads=[wsb[slot], st.xn1T], writes=[psq[pb]], inc=(cj == 7))
                if dstT is not None:
                    p.op("vector", lambda e, j=j, tc=tc, pb=pb, bcol=bcol, dstr=dstr, dstT=dstT: e.tensor_scalar(
                        dstr[:, j, 512 * tc:512 * (tc + 1)], psq[pb][:, :],
                        bqs[:, bcol * 8 + j:bcol * 8 + j + 1], None, ALU.add),
                        reads=[psq[pb], bqs], writes=[dstT])
                else:
                    kb = tc % 2
                    p.op("vector", lambda e, j=j, tc=tc, pb=pb, kb=kb, bcol=bcol: e.tensor_scalar(
                        kvtile[kb][:, :], psq[pb][:, :],
                        bqs[:, bcol * 8 + j:bcol * 8 + j + 1], None, ALU.add),
                        reads=[psq[pb], bqs], writes=[kvtile[kb]])
                    p.op("sync", lambda e, j=j, tc=tc, kb=kb, dst_dram=dst_dram: e.dma_start(
                        out=dst_dram[128 * j:128 * (j + 1), 512 * tc:512 * (tc + 1)],
                        in_=kvtile[kb][:, :]),
                        reads=[kvtile[kb]], writes=[dst_dram], dma=True)

        for j in range(NJ):
            proj("wq", None, d["q_d"], 0, j, j % 2)
        for j in range(NJ):
            proj("wk", None, d["ccK_in"], 1, j, j % 2)
        for j in range(NJ):
            proj("wv", None, d["ccV_in"], 2, j, j % 2)
        p.op("gpsimd", lambda e: e.collective_compute(
            "AllGather", ALU.bypass, replica_groups=GROUPS,
            ins=[d["ccK_in"].opt()], outs=[d["ccK_out"].opt()]),
            reads=[d["ccK_in"]], writes=[d["ccK_out"]], q=st.cc_sems[1], inc_n=1)
        p.op("gpsimd", lambda e: e.collective_compute(
            "AllGather", ALU.bypass, replica_groups=GROUPS,
            ins=[d["ccV_in"].opt()], outs=[d["ccV_out"].opt()]),
            reads=[d["ccV_in"]], writes=[d["ccV_out"]], q=st.cc_sems[2], inc_n=1)
    st.barrier()



def emit_l1_attn(st, scope_q):
    p, d = st.p, st.d
    ctr = st.ctxT.ap().rearrange("p (j t) -> p j t", j=NJ)
    with ExitStack() as sc:
        sb = lambda nm, sh, dt=F32: st.sb(nm, sh, dt, scope=sc)
        KVH = [sb(f"a_kvh{i}", [128, S_FULL], F16) for i in range(4)]
        VH = [sb(f"a_vh{i}", [128, 64 * 65], F16) for i in range(4)]
        QR = [sb(f"a_qr{i}", [128, T_LOC], F16) for i in range(2)]
        PT = [sb(f"a_pt{i}", [128, 1024], F16) for i in range(2)]
        zrow = sb("a_z", [1, 512], F16)
        ups = sb("a_u", [64, 512], F16)
        ones1 = sb("a_ones", [1, 64], F16)
        psS = [st.ps(f"a_psS{i}", [128, 1024], F32, scope=sc) for i in range(2)]
        psC = [st.ps(f"a_psC{i}", [65, 512], F32, scope=sc) for i in range(2)]
        psT = [st.ps(f"a_psT{i}", [128, 512], F16, scope=sc) for i in range(2)]
        p.op("vector", lambda e: e.memset(ones1[:], 1.0), writes=[ones1])

        def qr_load(m):
            s = m % 2
            p.op("sync", lambda e, m=m, s=s: e.dma_start(
                out=QR[s][:], in_=d["q_d"][128 * m:128 * (m + 1), :]),
                reads=[d["q_d"]], writes=[QR[s]], dma=True)

        def kv_load(h):
            ss = h % 4
            prow = 64 * (h % 2)
            vrow = 64 - prow
            KH_h = KVH[ss][prow:prow + 64, :]
            VHT_h = KVH[ss][vrow:vrow + 64, :]
            for r in range(4):
                p.op("sync", lambda e, h=h, r=r, KH_h=KH_h, ss=ss: e.dma_start(
                    out=KH_h[:, T_LOC * r:T_LOC * (r + 1)],
                    in_=d["ccK_out"][C * r + HD * h:C * r + HD * (h + 1), :]),
                    reads=[d["ccK_out"]], writes=[KVH[ss]], dma=True)
                p.op("sync", lambda e, h=h, r=r, VHT_h=VHT_h, ss=ss: e.dma_start(
                    out=VHT_h[:, T_LOC * r:T_LOC * (r + 1)],
                    in_=d["ccV_out"][C * r + HD * h:C * r + HD * (h + 1), :]),
                    reads=[d["ccV_out"]], writes=[KVH[ss]], dma=True)

        def vh_build(h):
            ss = h % 4
            vrow = 64 - 64 * (h % 2)
            VHT_h = KVH[ss][vrow:vrow + 64, :]
            p.op("vector", lambda e, ss=ss: e.memset(VH[ss][:], 1.0),
                 reads=[(VH[ss], "t")], writes=[VH[ss]])
            for g2 in range(16):
                tb = psT[g2 % 2]
                for kb in range(4):
                    p.op("tensor", lambda e, ss=ss, g2=g2, kb=kb, VHT_h=VHT_h, vrow=vrow, tb=tb: e.transpose(
                        tb[:, 128 * kb:128 * kb + 64],
                        VHT_h[:, 128 * (4 * g2 + kb):128 * (4 * g2 + kb + 1)],
                        st.idblk[vrow:vrow + 64, :]),
                        reads=[KVH[ss], st.idblk], writes=[tb], inc=(kb == 3))
                p.op("vector", lambda e, ss=ss, g2=g2, tb=tb: e.tensor_copy(
                    VH[ss].ap().rearrange("p (kb x) -> p kb x", x=65)[:, 4 * g2:4 * (g2 + 1), 0:64],
                    tb.ap().rearrange("p (kb x) -> p kb x", x=128)[:, :, 0:64]),
                    reads=[tb, VH[ss]], writes=[(VH[ss], "t")])

        NG = 64

        def emit_qk(m, qt, gi):
            s = m % 2
            ss0, ss1 = (2 * m) % 4, (2 * m + 1) % 4
            pss = psS[gi % 2]
            p.op("tensor", lambda e, s=s, ss0=ss0, qt=qt, gi=gi, pss=pss: e.matmul(
                pss[:, 0:512],
                KVH[ss0][0:64, 128 * gi:128 * (gi + 1)],
                QR[s][0:64, 512 * qt:512 * (qt + 1)],
                start=True, stop=True, tile_position=(0, 0)),
                reads=[KVH[ss0], QR[s]], writes=[pss], inc=False)
            p.op("tensor", lambda e, s=s, ss1=ss1, qt=qt, gi=gi, pss=pss: e.matmul(
                pss[:, 512:1024],
                KVH[ss1][64:128, 128 * gi:128 * (gi + 1)],
                QR[s][64:128, 512 * qt:512 * (qt + 1)],
                start=True, stop=True, tile_position=(64, 0)),
                reads=[KVH[ss1], QR[s]], writes=[pss], inc=True)

        def emit_exp(gi):
            pb = gi % 2
            pss = psS[pb]
            p.op("scalar", lambda e, pb=pb, pss=pss: e.activation(
                PT[pb][:, :], pss[:, :], AF.Exp, scale=0.125),
                reads=[pss], writes=[PT[pb]])

        def emit_av(m, gi):
            pb = gi % 2
            last = (gi == NG - 1)
            for hh in range(2):
                ss = (2 * m + hh) % 4
                p.op("tensor", lambda e, ss=ss, pb=pb, gi=gi, hh=hh, last=last: e.matmul(
                    psC[hh][:, :], VH[ss][:, 65 * gi:65 * (gi + 1)],
                    PT[pb][:, 512 * hh:512 * (hh + 1)],
                    start=(gi == 0), stop=last),
                    reads=[VH[ss], (VH[ss], "t"), PT[pb]], writes=[psC[hh]],
                    inc=(last and hh == 1))

        qr_load(0)
        kv_load(0)
        vh_build(0)
        kv_load(1)
        vh_build(1)
        for m in range(NH // 2):
            h2, h3 = 2 * m + 2, 2 * m + 3
            if h2 < NH:
                kv_load(h2)
                kv_load(h3)
            if m + 1 < NH // 2:
                qr_load(m + 1)
            for qt in range(4):
                emit_qk(m, qt, 0)
                for gi in range(NG):
                    emit_exp(gi)
                    if gi >= 1:
                        emit_av(m, gi - 1)
                    if gi + 1 < NG:
                        emit_qk(m, qt, gi + 1)
                emit_av(m, NG - 1)
                for hh in range(2):
                    prow = 64 * hh
                    p.op("vector", lambda e, hh=hh: e.tensor_scalar(
                        ups[:, :], psC[hh][0:64, :], 1.0 / 4096.0, None, ALU.mult),
                        reads=[psC[hh]], writes=[ups])
                    p.op("vector", lambda e, hh=hh: e.reciprocal(psS[0][0:1, 0:512], psC[hh][64:65, :]),
                         reads=[psC[hh]], writes=[psS[0]])
                    p.op("vector", lambda e: e.tensor_scalar(
                        zrow[0:1, :], psS[0][0:1, 0:512], 4096.0, None, ALU.mult),
                        reads=[psS[0]], writes=[zrow])
                    p.op("tensor", lambda e, hh=hh: e.matmul(
                        psC[hh][0:64, :], ones1[0:1, :], zrow[0:1, :],
                        start=True, stop=True),
                        reads=[ones1, zrow], writes=[psC[hh]])
                    p.op("vector", lambda e, m=m, qt=qt, hh=hh, prow=prow: e.tensor_mul(
                        ctr[prow:prow + 64, m, 512 * qt:512 * (qt + 1)],
                        ups[:, :], psC[hh][0:64, :]),
                        reads=[ups, psC[hh]], writes=[st.ctxT])
                if qt == 1 and h2 < NH:
                    vh_build(h2)
                if qt == 2 and h2 < NH:
                    vh_build(h3)
        st.dbg_store("ctxT", st.ctxT[:])
    st.barrier()


def emit_l1_ffn_scores(st, scope):
    p, d = st.p, st.d
    xnr1 = st.xn1T.ap().rearrange("p (j t) -> p j t", j=NJ)
    scr = st.scoresT.ap().rearrange("p (E t) -> p E t", E=NE1)
    if True:
        sc = scope
        sb = lambda nm, sh, dt=F32: st.sb(nm, sh, dt, scope=sc)
        es16 = sb("es16", [16, C], F16)
        p.op("gpsimd", lambda e: e.dma_start(out=es16[:], in_=d["esel"]),
             reads=[d["esel"]], writes=[es16], dma=True)
        eselT = sb("eselT", [128, NJ * 16], F16)
        psE = st.ps("p6_psE", [128, 16], F16, scope=sc)
        for j in range(NJ):
            p.op("tensor", lambda e, j=j: e.transpose(
                psE[:, :], es16[:, 128 * j:128 * (j + 1)], st.ident[0:16, 0:16]),
                reads=[es16, st.ident], writes=[psE])
            p.op("vector", lambda e, j=j: e.tensor_copy(eselT[:, 16 * j:16 * (j + 1)], psE[:, :]),
                 reads=[psE], writes=[eselT])
        psSel = st.ps("p6_psSel", [16, 512], F32, scope=sc)
        selT = sb("selT", [16, T_LOC], F16)
        for tc in range(4):
            for cj in range(NJ):
                p.op("tensor", lambda e, tc=tc, cj=cj: e.matmul(
                    psSel[:, :], eselT[:, 16 * cj:16 * (cj + 1)],
                    xnr1[:, cj, 512 * tc:512 * (tc + 1)], start=(cj == 0), stop=(cj == 7)),
                    reads=[eselT, st.xn1T], writes=[psSel], inc=(cj == 7))
            p.op("scalar", lambda e, tc=tc: e.activation(
                selT[:, 512 * tc:512 * (tc + 1)], psSel[:, :], AF.Sigmoid),
                reads=[psSel], writes=[selT])

        keysb = [sb(f"p6_k{i}", [128, NJ * 128], F16) for i in range(2)]
        sel1h = sb("p6_1h", [16, NE1 * 128], F16)
        p.op("gpsimd", lambda e: e.dma_start(out=sel1h[:], in_=d["sel1h"]),
             reads=[d["sel1h"]], writes=[sel1h], dma=True)
        relscr = sb("p6_rel", [128, 512], F16)
        psSc = [st.ps(f"p6_psSc{i}", [128, 512], F32, scope=sc) for i in range(2)]
        psBc = st.ps("p6_psBc", [128, 512], F32, scope=sc)
        p.op("gpsimd", lambda e: e.dma_start(out=keysb[0][:], in_=d["keysR"][0]),
             reads=[d["keysR"]], writes=[keysb[0]], dma=True)
        for e_ in range(NE1):
            ks = e_ % 2
            if e_ + 1 < NE1:
                p.op("gpsimd", lambda e, e_=e_, ks=ks: e.dma_start(
                    out=keysb[1 - ks][:], in_=d["keysR"][e_ + 1]),
                    reads=[d["keysR"]], writes=[keysb[1 - ks]], dma=True)
            for tc in range(4):
                pb = tc % 2
                for cj in range(NJ):
                    p.op("tensor", lambda e, tc=tc, cj=cj, ks=ks, pb=pb: e.matmul(
                        psSc[pb][:, :], keysb[ks][:, 128 * cj:128 * (cj + 1)],
                        xnr1[:, cj, 512 * tc:512 * (tc + 1)], start=(cj == 0), stop=(cj == 7)),
                        reads=[keysb[ks], st.xn1T], writes=[psSc[pb]], inc=(cj == 7))
                p.op("scalar", lambda e, tc=tc, pb=pb: e.activation(
                    relscr[:, :], psSc[pb][:, :], AF.Relu),
                    reads=[psSc[pb]], writes=[relscr])
                p.op("tensor", lambda e, e_=e_, tc=tc: e.matmul(
                    psBc[:, :], sel1h[:, 128 * e_:128 * (e_ + 1)],
                    selT[0:16, 512 * tc:512 * (tc + 1)],
                    start=True, stop=True),
                    reads=[sel1h, selT], writes=[psBc])
                p.op("vector", lambda e, e_=e_, tc=tc: e.tensor_mul(
                    scr[:, e_, 512 * tc:512 * (tc + 1)], relscr[:, :], psBc[:, :]),
                    reads=[relscr, psBc], writes=[st.scoresT])


def emit_tail(st, scope_l1):
    p, d = st.p, st.d
    ctr = st.ctxT.ap().rearrange("p (j t) -> p j t", j=NJ)
    scr = st.scoresT.ap().rearrange("p (E t) -> p E t", E=NE1)
    with ExitStack() as sc:
        sb = lambda nm, sh, dt=F32: st.sb(nm, sh, dt, scope=sc)
        wos = sb("wos", [128, NJ * C], F16)
        p.op("gpsimd", lambda e: e.dma_start(
            out=wos.ap().rearrange("p (j o) -> p j o", j=NJ),
            in_=d["wo1"].rearrange("(j p) o -> p j o", p=128)),
            reads=[d["wo1"]], writes=[wos], dma=True)
        valsb = sb("t_vals", [128, NE1 * C], F16)
        for e_ in range(NE1):
            p.op("gpsimd", lambda e, e_=e_: e.dma_start(
                out=valsb[:, C * e_:C * (e_ + 1)], in_=d["values"][e_]),
                reads=[d["values"]], writes=[valsb], dma=True)
        Bo = sb("Bo", [128, C], F32)
        Gt = sb("Gt", [128, C], F32)
        Bt = sb("Bt", [128, C], F32)
        for nm, t in [("Bo_t", Bo), ("Gam_t", Gt), ("Bet_t", Bt)]:
            p.op("sync", lambda e, nm=nm, t=t: e.dma_start(out=t[:], in_=d[nm]),
                 reads=[d[nm]], writes=[t], dma=True)
        pso = [st.ps(f"t_pso{i}", [128, 1024], F32, scope=sc) for i in range(2)]
        psYs = [st.ps(f"t_psY{i}", [128, 1024], F32, scope=sc) for i in range(2)]
        xt = [sb(f"t_x{i}", [128, C], F32) for i in range(4)]
        lnin = sb("t_ln", [128, C], F32)
        sums = sb("t_s", [128, NT], F32)
        sqs = sb("t_q", [128, NT], F32)
        sqscr = sb("t_qs", [128, C], F32)
        g1 = [sb(f"t_g1{i}", [128, C], F16) for i in range(2)]
        s1 = sb("t_s1", [128, C], F16)
        sg = sb("t_sg", [128, C], F16)
        wor = wos.ap().rearrange("p (j o) -> p j o", j=NJ)
        for tt in range(NT):
            pb = tt % 2
            xb = tt % 4
            for hh in range(2):
                for cj in range(NJ):
                    p.op("tensor", lambda e, tt=tt, hh=hh, cj=cj, pb=pb: e.matmul(
                        pso[pb][:, 512 * hh:512 * (hh + 1)], ctr[:, cj, 128 * tt:128 * (tt + 1)],
                        wor[:, cj, 512 * hh:512 * (hh + 1)], start=(cj == 0), stop=(cj == 7)),
                        reads=[st.ctxT, wos], writes=[pso[pb]], inc=(cj == 7 and hh == 1))
            psY = psYs[tt % 2]
            for e_ in range(NE1):
                for hh in range(2):
                    p.op("tensor", lambda e, tt=tt, e_=e_, hh=hh, psY=psY: e.matmul(
                        psY[:, 512 * hh:512 * (hh + 1)],
                        scr[:, e_, 128 * tt:128 * (tt + 1)],
                        valsb[:, C * e_ + 512 * hh:C * e_ + 512 * (hh + 1)],
                        start=(e_ == 0), stop=(e_ == NE1 - 1)),
                        reads=[st.scoresT, valsb], writes=[psY],
                        inc=(e_ == NE1 - 1 and hh == 1))
            p.op("sync", lambda e, tt=tt, xb=xb: e.dma_start(
                out=xt[xb][:], in_=d["x1_d"][128 * tt:128 * (tt + 1), :]),
                reads=[(d["x1_d"], tt)], writes=[xt[xb]], dma=True)
            p.op("vector", lambda e, tt=tt, xb=xb: e.tensor_scalar(
                lnin[:], xt[xb][:], st.rstd1[:, tt:tt + 1], st.nmr1[:, tt:tt + 1], ALU.mult, ALU.add),
                reads=[xt[xb], st.rstd1, st.nmr1], writes=[lnin])
            p.op("vector", lambda e, tt=tt, pb=pb: e.tensor_add(lnin[:], lnin[:], pso[pb][:, :]),
                 reads=[lnin, pso[pb]], writes=[lnin])
            p.op("vector", lambda e: e.tensor_add(lnin[:], lnin[:], Bo[:]),
                 reads=[lnin, Bo], writes=[lnin])
            p.op("vector", lambda e, tt=tt: e.reduce_sum(sums[:, tt:tt + 1], lnin[:], axis=AX),
                 reads=[lnin], writes=[sums])
            p.op("scalar", lambda e, tt=tt: e.activation(sqscr[:], lnin[:], AF.Square,
                                                         accum_out=sqs[:, tt:tt + 1]),
                 reads=[lnin], writes=[sqscr, sqs])
            p.op("vector", lambda e, tt=tt: e.tensor_scalar(
                sums[:, tt:tt + 1], sums[:, tt:tt + 1], 1.0 / C, None, ALU.mult),
                reads=[sums], writes=[sums])
            p.op("vector", lambda e, tt=tt: e.scalar_tensor_tensor(
                sqscr[:, 0:1], sums[:, tt:tt + 1], -1.0, sums[:, tt:tt + 1], ALU.mult, ALU.mult),
                reads=[sums], writes=[sqscr])
            p.op("vector", lambda e, tt=tt: e.scalar_tensor_tensor(
                sqscr[:, 0:1], sqs[:, tt:tt + 1], 1.0 / C, sqscr[:, 0:1], ALU.mult, ALU.add),
                reads=[sqs, sqscr], writes=[sqscr])
            p.op("scalar", lambda e: e.activation(sqscr[:, 0:1], sqscr[:, 0:1], AF.Sqrt, bias=LN_EPS),
                 reads=[sqscr], writes=[sqscr])
            p.op("vector", lambda e: e.reciprocal(sqscr[:, 1:2], sqscr[:, 0:1]),
                 reads=[sqscr], writes=[sqscr])
            p.op("vector", lambda e, tt=tt: e.scalar_tensor_tensor(
                sqscr[:, 2:3], sums[:, tt:tt + 1], -1.0, sqscr[:, 1:2], ALU.mult, ALU.mult),
                reads=[sums, sqscr], writes=[sqscr])
            p.op("vector", lambda e: e.tensor_scalar(
                lnin[:], lnin[:], sqscr[:, 1:2], sqscr[:, 2:3], ALU.mult, ALU.add),
                reads=[lnin, sqscr], writes=[lnin])
            p.op("vector", lambda e: e.tensor_mul(lnin[:], lnin[:], Gt[:]),
                 reads=[lnin, Gt], writes=[lnin])
            p.op("vector", lambda e: e.tensor_add(s1[:], lnin[:], Bt[:]),
                 reads=[lnin, Bt], writes=[s1])
            gb = tt % 2
            p.op("scalar", lambda e, gb=gb, psY=psY: e.activation(g1[gb][:, :], psY[:, :], AF.Silu),
                 reads=[psY], writes=[g1[gb]])
            p.op("vector", lambda e, gb=gb: e.tensor_mul(sg[:, :], s1[:], g1[gb][:, :]),
                 reads=[s1, g1[gb]], writes=[sg])
            p.op("vector", lambda e, xb=xb: e.tensor_add(xt[xb][:], xt[xb][:], sg[:, :]),
                 reads=[sg, xt[xb]], writes=[xt[xb]])
            p.op("sync", lambda e, tt=tt, xb=xb: e.dma_start(
                out=d["out"][128 * tt:128 * (tt + 1), :], in_=xt[xb][:]),
                reads=[xt[xb]], writes=[(d["out"], tt)], dma=True)
        st.dbg_store("s1", s1[:])
    st.p.wait_all("sync")



ALL_PHASES = ("p0", "p1", "p2", "p4", "p5", "p5b", "p6")


def build_program(phases=ALL_PHASES, dbg=()):
    nc = bass.Bass("TRN2", target_bir_lowering=False)
    d = {}

    def din(name, shape, dt=F32):
        d[name] = nc.dram_tensor(name, shape, dt, kind="ExternalInput").ap()

    def dint(name, shape, dt=F32):
        d[name] = nc.dram_tensor(name, shape, dt, kind="Internal").ap()

    din("x_main", [T_LOC, C]); din("x_halo", [2, C])
    din("idn", [128, 128]); din("iota_t", [128, T_LOC]); din("iotar_t", [128, T_LOC])
    din("cst", [128, 6]); din("idn_blk", [128, 64])
    if "p1" in phases:
        for nm in ("Wk", "Wv", "Wr", "Wo"):
            din(nm, [C, C])
        for nm in ("mixk_fm", "mixv_fm", "mixr_fm", "sd_fm", "sf_fm"):
            din(nm, [128, 8])
        din("sd_row", [1, C]); din("qmask", [1, 4]); din("qcoef", [1, 4])
        dint("ccS_in", [128, 16]); dint("ccS_out", [512, 16]); dint("s0_d", [T_LOC, C], F16)
    if "p2" in phases:
        din("gws", [128, 64]); din("W1", [NE0, C, HID]); din("b1_fm", [128, 128])
        din("W2", [NE0, HID, C]); din("b2", [NE0, C])
        dint("x1_d", [T_LOC, C])
    if "p4" in phases:
        din("wqR", [NJ, 128, NJ * 128]); din("wkR", [NJ, 128, NJ * 128]); din("wvR", [NJ, 128, NJ * 128])
        din("bq_fm", [128, 8]); din("bk_fm", [128, 8]); din("bv_fm", [128, 8])
        dint("ccK_in", [C, T_LOC], F16); dint("ccK_out", [4 * C, T_LOC], F16)
        dint("ccV_in", [C, T_LOC], F16); dint("ccV_out", [4 * C, T_LOC], F16)
        dint("q_d", [C, T_LOC], F16)
    if "p5b" in phases:
        din("wo1", [C, C]); din("Bo_t", [128, C]); din("Gam_t", [128, C]); din("Bet_t", [128, C])
    if "p6" in phases:
        din("esel", [NE1, C]); din("keysR", [NE1, 128, NJ * 128]); din("values", [NE1, 128, C])
        din("sel1h", [NE1, NE1 * 128])
        dint("g1_d", [T_LOC, C], F16)
    d["out"] = nc.dram_tensor("out", [T_LOC, C], F32, kind="ExternalOutput").ap()

    dbg_aps = {}
    for name in dbg:
        shape, dt = DBG_SHAPES[name]
        dbg_aps[name] = nc.dram_tensor("dbg_" + name, shape, dt, kind="ExternalOutput").ap()

    with ExitStack() as ctx:
        p = Prog(nc, ctx)
        st = _St(nc, ctx, p, d, dbg_aps)
        p.blk = ctx.enter_context(nc.Block())
        emit_setup(st)
        with ExitStack() as scope_l0:
            emit_l0_alloc(st, scope_l0)
            if "p0" in phases:
                emit_ln0(st)
            if "p1" in phases:
                emit_l0_attn(st)
            if "p2" in phases:
                emit_moe(st)
        with ExitStack() as scope_l1:
            with ExitStack() as scope_q:
                with ExitStack() as scope_x:
                    if "p4" in phases:
                        emit_l1_qkv(st, scope_l1, scope_q, scope_x)
                    if "p6" in phases:
                        emit_l1_ffn_scores(st, scope_x)
                    st.barrier()
                if "p5" in phases:
                    emit_l1_attn(st, scope_q)
            if "p5b" in phases:
                emit_tail(st, scope_l1)
                p.flush()
        if "p6" not in phases:
            zo = st.sb("zo", [128, C], F32)
            p.op("vector", lambda e: e.memset(zo[:], 0.0), writes=[zo])
            for i in range(NT):
                p.op("sync", lambda e, i=i: e.dma_start(
                    out=d["out"][128 * i:128 * (i + 1), :], in_=zo[:]),
                    reads=[zo], writes=[d["out"]], dma=True)
            p.wait_all("sync")
        p.flush()
    return nc



def _fm(vec):
    return np.ascontiguousarray(np.asarray(vec, np.float32).reshape(8, 128).T)


def make_in_maps(inputs, phases=ALL_PHASES):
    ii = {k: np.asarray(v) for k, v in inputs.items()}
    x = ii["L0_attn_x"]
    maps = []
    iota = np.tile(np.arange(T_LOC, dtype=np.float32), (128, 1))
    iotar = (T_LOC - 1.0) - iota
    idn = np.eye(128, dtype=np.float32)
    for g in range(8):
        b, q = g // 4, g % 4
        t0 = T_LOC * q
        m = {}
        m["x_main"] = x[b, t0:t0 + T_LOC]
        halo = np.zeros((2, C), np.float32)
        if q > 0:
            halo[0] = x[b, t0 - 1]
        if q < 3:
            halo[1] = x[b, t0 + T_LOC]
        m["x_halo"] = halo
        m["idn"] = idn
        m["iota_t"] = iota
        m["iotar_t"] = iotar
        m["cst"] = np.tile(np.array([[1e-5, -1.0, 1.0 / S_FULL, 0.0, 1.0, 0.125]], np.float32), (128, 1))
        m["idn_blk"] = np.vstack([np.eye(64, dtype=np.float32)] * 2)
        if "p1" in phases:
            m["Wk"] = ii["L0_attn_Wk"]; m["Wv"] = ii["L0_attn_Wv"]
            m["Wr"] = ii["L0_attn_Wr"]; m["Wo"] = ii["L0_attn_Wo"]
            m["mixk_fm"] = _fm(ii["L0_attn_mix_k"]); m["mixv_fm"] = _fm(ii["L0_attn_mix_v"])
            m["mixr_fm"] = _fm(ii["L0_attn_mix_r"])
            m["sd_fm"] = _fm(ii["L0_attn_spatial_decay"])
            m["sf_fm"] = _fm(ii["L0_attn_spatial_first"])
            m["sd_row"] = ii["L0_attn_spatial_decay"].reshape(1, C)
            qmask = np.zeros((1, 4), np.float32); qmask[0, :q] = 1.0
            qcoef = np.zeros((1, 4), np.float32)
            for r in range(q):
                qcoef[0, r] = float(T_LOC * (q - 1 - r))
            m["qmask"] = qmask; m["qcoef"] = qcoef
        if "p2" in phases:
            gw = ii["L0_ffn_gate_W"]
            m["gws"] = gw.reshape(8, 128, 8).transpose(1, 0, 2).reshape(128, 64)
            m["W1"] = ii["L0_ffn_W1"]; m["W2"] = ii["L0_ffn_W2"]
            m["b1_fm"] = ii["L0_ffn_b1"].reshape(8, 16, 128).transpose(2, 0, 1).reshape(128, 128)
            m["b2"] = ii["L0_ffn_b2"]
        if "p4" in phases:
            def _wR(w):
                return np.ascontiguousarray(
                    np.asarray(w, np.float32).reshape(8, 128, 8, 128).transpose(2, 1, 0, 3).reshape(8, 128, 1024))
            m["wqR"] = _wR(ii["L1_attn_wq"]); m["wkR"] = _wR(ii["L1_attn_wk"]); m["wvR"] = _wR(ii["L1_attn_wv"])
            m["bq_fm"] = _fm(ii["L1_attn_bq"]); m["bk_fm"] = _fm(ii["L1_attn_bk"])
            m["bv_fm"] = _fm(ii["L1_attn_bv"])
        if "p5b" in phases:
            m["wo1"] = ii["L1_attn_wo"]
            m["Bo_t"] = np.tile(np.asarray(ii["L1_attn_bo"], np.float32).reshape(1, C), (128, 1))
            m["Gam_t"] = np.tile(np.asarray(ii["L1_attn_gamma"], np.float32).reshape(1, C), (128, 1))
            m["Bet_t"] = np.tile(np.asarray(ii["L1_attn_beta"], np.float32).reshape(1, C), (128, 1))
        if "p6" in phases:
            m["esel"] = ii["L1_ffn_expert_sel"]
            m["keysR"] = ii["L1_ffn_keys"].reshape(NE1, 8, 128, 128).transpose(0, 2, 1, 3).reshape(NE1, 128, 1024)
            m["values"] = ii["L1_ffn_values"]
            m["sel1h"] = np.repeat(np.eye(NE1, dtype=np.float32), 128, axis=1)
        maps.append({k: np.ascontiguousarray(np.asarray(v, np.float32)) for k, v in m.items()})
    return maps


def kernel(**inputs):
    nc = build_program(ALL_PHASES)
    in_maps = make_in_maps(inputs, ALL_PHASES)
    res = run_bass_kernel_spmd(nc, in_maps, core_ids=list(range(8)))
    out = np.zeros((2, S_FULL, C), np.float32)
    for g in range(8):
        b, q = g // 4, g % 4
        out[b, T_LOC * q:T_LOC * (q + 1)] = res.results[g]["out"]
    return out


# version 15 (final)
# speedup vs baseline: 1.1317x; 1.0085x over previous
import numpy as np
from contextlib import ExitStack

import concourse.bass as bass
import concourse.mybir as mybir
from concourse.bass_utils import run_bass_kernel_spmd

F32 = mybir.dt.float32
F16 = mybir.dt.float16
AF = mybir.ActivationFunctionType
ALU = mybir.AluOpType
AX = mybir.AxisListType.X

T_LOC = 2048
NT = 16
C = 1024
NJ = 8
HID = 2048
NE0 = 8
NE1 = 16
NH = 16
HD = 64
S_FULL = 8192
LN_EPS = 1e-5
XN_W = T_LOC + 2

ENGINES = ("sync", "gpsimd", "tensor", "vector", "scalar")
GROUPS = [[0, 1, 2, 3], [4, 5, 6, 7]]


def _tname(x):
    k = None
    if isinstance(x, tuple):
        x, k = x
    t = getattr(x, "tensor", x)
    return (getattr(t, "name", str(t)), k)


class Sem:
    __slots__ = ("h", "n", "name")

    def __init__(self, h, name):
        self.h, self.n, self.name = h, 0, name


class Prog:
    def __init__(self, nc, ctx):
        self.nc = nc
        self.ctx = ctx
        self.script = {e: [] for e in ENGINES}
        self.esem = {e: self.sem("sE_" + e) for e in ENGINES}
        self.dsem = {}
        self.wr = {}
        self.rd = {}
        self.pending = {e: [] for e in ENGINES}
        self.waited = {e: {} for e in ENGINES}
        self.all_sems = list(self.esem.values())

    def sem(self, name):
        h = self.ctx.enter_context(self.nc.semaphore(name))
        s = Sem(h, name)
        if hasattr(self, "all_sems"):
            self.all_sems.append(s)
        return s

    def dma_sem(self, key):
        if key not in self.dsem:
            self.dsem[key] = self.sem("q_" + str(key)[:12] + str(len(self.dsem)))
        return self.dsem[key]

    def op(self, eng, build, reads=(), writes=(), inc=True, dma=False, q=None,
           inc_n=None, extra=()):
        rkeys = [_tname(x) for x in reads]
        wkeys = [_tname(x) for x in writes]
        if q is not None:
            sem = q
        elif dma:
            nm0, k0 = wkeys[0]
            sem = self.dma_sem((nm0, k0) if nm0 in ("x1_d", "g1_d") else nm0)
        else:
            sem = self.esem[eng]
        if inc_n is None:
            inc_n = 16 if dma else 1
        waits = {}

        def add_wait(s, v):
            if v is None:
                raise RuntimeError(f"unresolved dep for {eng} op; producer must inc")
            if s.name.startswith("q_"):
                v = s.n
            if v <= 0:
                return
            if s is not self.esem[eng] and self.waited[eng].get(s.name, 0) >= v:
                return
            if s.name in waits:
                waits[s.name] = (s, max(waits[s.name][1], v))
            else:
                waits[s.name] = (s, v)

        def hit_keys(d, k):
            if not d:
                return []
            if k is None:
                return list(d.keys())
            return [kk for kk in (k, None) if kk in d]

        pe_skip = (eng == "tensor")

        def dep(s, v):
            if not (pe_skip and s is self.esem["tensor"]):
                add_wait(s, v)

        for name, k in rkeys:
            d = self.wr.get(name)
            for kk in hit_keys(d, k):
                dep(*d[kk])
        for name, k in wkeys:
            for peng, plist in self.pending.items():
                if peng == eng or (pe_skip and peng == "tensor"):
                    continue
                for _, pn, pk in plist:
                    if pn == name and (k is None or pk is None or pk == k):
                        raise RuntimeError(
                            f"write {name}[{k}] on {eng} conflicts with pending op on {peng}")
            d = self.wr.get(name)
            for kk in hit_keys(d, k):
                dep(*d[kk])
            rdn = self.rd.get(name)
            for kk in hit_keys(rdn, k):
                for s, v in rdn[kk].values():
                    dep(s, v)
        for s, v in extra:
            add_wait(s, v)

        if inc:
            sem.n += inc_n
            tgt = sem.n
        else:
            tgt = None
        wl = tuple(waits.values())
        for k2, (s, v) in waits.items():
            self.waited[eng][k2] = max(self.waited[eng].get(k2, 0), v)

        def f(e, build=build, wl=wl, sem=sem, tgt=tgt, inc=inc, inc_n=inc_n):
            for s, v in wl:
                e.wait_ge(s.h, v)
            ins = build(e)
            if inc:
                ins.then_inc(sem.h, inc_n)

        self.script[eng].append(f)

        def rec_read(name, k, s, v):
            self.rd.setdefault(name, {}).setdefault(k, {})[s.name] = (s, v)

        def rec_write(name, k, s, v):
            rdn = self.rd.get(name)
            if rdn is not None:
                if k is None:
                    self.rd.pop(name, None)
                else:
                    rdn.pop(k, None)
                    rdn.pop(None, None)
            if k is None:
                self.wr[name] = {None: (s, v)}
            else:
                dd = self.wr.setdefault(name, {})
                dd.pop(None, None)
                dd[k] = (s, v)

        if inc:
            if not dma and q is None:
                for kind, name, k in self.pending[eng]:
                    (rec_read if kind == "r" else rec_write)(name, k, sem, tgt)
                self.pending[eng] = []
            for name, k in rkeys:
                rec_read(name, k, sem, tgt)
            for name, k in wkeys:
                rec_write(name, k, sem, tgt)
        else:
            for name, k in rkeys:
                self.pending[eng].append(("r", name, k))
            for name, k in wkeys:
                self.pending[eng].append(("w", name, k))
                if k is None:
                    self.wr[name] = {None: (sem, None)}
                else:
                    dd = self.wr.setdefault(name, {})
                    dd.pop(None, None)
                    dd[k] = (sem, None)
        return tgt

    def wait_all(self, eng, sems=None):
        sems = sems if sems is not None else self.all_sems
        wl = []
        for s in sems:
            if s.n > 0 and self.waited[eng].get(s.name, 0) < s.n:
                wl.append((s, s.n))
                self.waited[eng][s.name] = s.n
        if not wl:
            return

        def f(e, wl=tuple(wl)):
            for s, v in wl:
                e.wait_ge(s.h, v)

        self.script[eng].append(f)

    def flush(self):
        for e in ENGINES:
            fns = self.script[e]
            if fns:
                getattr(self.blk, e)(lambda engobj, fns=fns: [f(engobj) for f in fns])
                self.script[e] = []


DBG_SHAPES = {
    "xnT": ([128, NJ * XN_W], F16),
    "srwkvT": ([128, NJ * T_LOC], F16),
    "s0": ([T_LOC, C], F16),
    "g0pre": ([128, NT * C], F16),
    "x1": ([T_LOC, C], F32),
    "QT": ([128, NJ * T_LOC], F16),
    "KT": ([128, NJ * T_LOC], F16),
    "VT": ([128, NJ * T_LOC], F16),
    "xn1T": ([128, NJ * T_LOC], F16),
    "ctxT": ([128, NJ * T_LOC], F16),
    "s1": ([128, NT * C], F16),
    "eku": ([128, NJ * T_LOC], F16),
    "Sab": ([128, 16], F32),
    "ainbin": ([128, 16], F32),
    "cw": ([128, NT * NE0], F32),
}


class _St:
    def __init__(self, nc, ctx, p, d, dbg):
        self.nc, self.ctx, self.p, self.d, self.dbg = nc, ctx, p, d, dbg
        self.cc_sems = [p.sem(f"sCC{i}") for i in range(3)]

    def sb(self, name, shape, dt=F32, scope=None, side=None):
        return (scope or self.ctx).enter_context(
            self.nc.sbuf_tensor(name, shape, dt, side=side))

    def ps(self, name, shape, dt=F32, scope=None):
        return (scope or self.ctx).enter_context(self.nc.psum_tensor(name, shape, dt))

    def barrier(self):
        sems = [s for s in self.p.all_sems if not s.name.startswith("sCC")]
        for e in ENGINES:
            assert not self.p.pending[e], f"dangling non-inc ops on {e}"
            self.p.wait_all(e, sems)
        self.p.flush()

    def dbg_store(self, name, src_ap):
        if name in self.dbg:
            self.p.op("sync", lambda e, ap=self.dbg[name], s=src_ap: e.dma_start(out=ap, in_=s),
                      reads=[src_ap], writes=[self.dbg[name]], dma=True)



CONST_VALS = [LN_EPS, -1.0, 1.0 / S_FULL, 0.0, 1.0, 0.125]


def emit_setup(st):
    p, d = st.p, st.d
    st.ident = st.sb("ident", [128, 128], F16)
    st.csts = st.sb("csts", [128, 8], F32)
    st.idblk = st.sb("idblk", [128, 64], F16)
    p.op("gpsimd", lambda e: e.dma_start(out=st.ident[:], in_=d["idn"]),
         reads=[d["idn"]], writes=[st.ident], dma=True)
    p.op("sync", lambda e: e.dma_start(out=st.csts[:, 0:len(CONST_VALS)], in_=d["cst"]),
         reads=[d["cst"]], writes=[st.csts], dma=True)
    p.op("gpsimd", lambda e: e.dma_start(out=st.idblk[:], in_=d["idn_blk"]),
         reads=[d["idn_blk"]], writes=[st.idblk], dma=True)
    for i, v in enumerate(CONST_VALS):
        st.nc.const_aps.register(F32, v, st.csts[:, i:i + 1])
    st.barrier()


def emit_l0_alloc(st, scope):
    p, d = st.p, st.d
    st.iota = st.sb("iota", [128, T_LOC], F16, scope=scope)
    st.iotar = st.sb("iotar", [128, T_LOC], F16, scope=scope)
    p.op("gpsimd", lambda e: e.dma_start(out=st.iota[:], in_=d["iota_t"]),
         reads=[d["iota_t"]], writes=[st.iota], dma=True)
    p.op("gpsimd", lambda e: e.dma_start(out=st.iotar[:], in_=d["iotar_t"]),
         reads=[d["iotar_t"]], writes=[st.iotar], dma=True)
    st.xnT = st.sb("xnT", [128, NJ * XN_W], F16, scope=scope)
    st.rstd0 = st.sb("rstd0", [128, 17], F32, scope=scope)
    st.nmr0 = st.sb("nmr0", [128, 17], F32, scope=scope)


def _ln_stats(st, sc, tag, src_fn, n_tiles, rstd, nmr):
    p = st.p
    sums = st.sb(tag + "_sums", [128, 17], F32, scope=sc)
    sqs = st.sb(tag + "_sqs", [128, 17], F32, scope=sc)
    xt = [st.sb(f"{tag}_x{i}", [128, C], F32, scope=sc) for i in range(4)]
    sqscr = [st.sb(f"{tag}_q{i}", [128, C], F32, scope=sc) for i in range(2)]
    p.op("vector", lambda e: e.memset(sums[:], 0.0), writes=[sums])
    p.op("vector", lambda e: e.memset(sqs[:], 0.0), writes=[sqs])
    for i in range(n_tiles):
        b = i % 4
        src, nr = src_fn(i)
        p.op("sync", lambda e, s=src, dd=xt[b], nr=nr: e.dma_start(out=dd[0:nr, :], in_=s),
             reads=[(src, i)], writes=[xt[b]], dma=True)
        p.op("vector", lambda e, b=b, i=i, nr=nr: e.reduce_sum(
            sums[0:nr, i:i + 1], xt[b][0:nr, :], axis=AX),
            reads=[xt[b]], writes=[sums])
        b2 = i % 2
        p.op("scalar", lambda e, b=b, b2=b2, i=i, nr=nr: e.activation(
            sqscr[b2][0:nr, :], xt[b][0:nr, :], AF.Square, accum_out=sqs[0:nr, i:i + 1]),
            reads=[xt[b]], writes=[sqscr[b2], sqs])
    n = n_tiles
    mean = st.sb(tag + "_mean", [128, 17], F32, scope=sc)
    var = st.sb(tag + "_var", [128, 17], F32, scope=sc)
    p.op("vector", lambda e, n=n: e.tensor_scalar(mean[:, 0:n], sums[:, 0:n], 1.0 / C, None, ALU.mult),
         reads=[sums], writes=[mean])
    p.op("vector", lambda e, n=n: e.scalar_tensor_tensor(var[:, 0:n], mean[:, 0:n], -1.0, mean[:, 0:n], ALU.mult, ALU.mult),
         reads=[mean], writes=[var])
    p.op("vector", lambda e, n=n: e.scalar_tensor_tensor(var[:, 0:n], sqs[:, 0:n], 1.0 / C, var[:, 0:n], ALU.mult, ALU.add),
         reads=[sqs, var], writes=[var])
    p.op("scalar", lambda e, n=n: e.activation(var[:, 0:n], var[:, 0:n], AF.Sqrt, bias=LN_EPS),
         reads=[var], writes=[var])
    p.op("vector", lambda e, n=n: e.reciprocal(rstd[:, 0:n], var[:, 0:n]),
         reads=[var], writes=[rstd])
    p.op("vector", lambda e, n=n: e.scalar_tensor_tensor(nmr[:, 0:n], mean[:, 0:n], -1.0, rstd[:, 0:n], ALU.mult, ALU.mult),
         reads=[mean, rstd], writes=[nmr])


def emit_ln0(st):
    p, d = st.p, st.d
    with ExitStack() as sc:
        def src(i):
            if i < 16:
                return d["x_main"][128 * i:128 * (i + 1), :], 128
            return d["x_halo"], 2

        _ln_stats(st, sc, "ln0", src, 17, st.rstd0, st.nmr0)
        xt = [st.sb(f"p0_x{i}", [128, C], F32, scope=sc) for i in range(4)]
        xn16 = [st.sb(f"p0_n{i}", [128, C], F16, scope=sc) for i in range(2)]
        pst = [st.ps(f"p0_t{i}", [128, 8 * 128], F16, scope=sc) for i in range(2)]
        xnr = st.xnT.ap().rearrange("p (j t) -> p j t", j=NJ)
        for i in range(17):
            b = i % 4
            sap, nr = src(i)
            p.op("sync", lambda e, s=sap, dd=xt[b], nr=nr: e.dma_start(out=dd[0:nr, :], in_=s),
                 reads=[sap], writes=[xt[b]], dma=True)
            b2 = i % 2
            p.op("vector", lambda e, b=b, b2=b2, i=i, nr=nr: e.tensor_scalar(
                xn16[b2][0:nr, :], xt[b][0:nr, :], st.rstd0[0:nr, i:i + 1], st.nmr0[0:nr, i:i + 1],
                ALU.mult, ALU.add),
                reads=[xt[b], st.rstd0, st.nmr0], writes=[xn16[b2]])
            for j in range(NJ):
                p.op("tensor", lambda e, b2=b2, j=j, nr=nr: e.transpose(
                    pst[b2][:, 128 * j:128 * j + nr], xn16[b2][0:nr, 128 * j:128 * (j + 1)],
                    st.ident[0:nr, 0:nr]),
                    reads=[xn16[b2], st.ident], writes=[pst[b2]], inc=(j == NJ - 1))
            if i < 16:
                dst = xnr[:, :, 1 + 128 * i: 1 + 128 * (i + 1)]
                srcap = pst[b2].ap().rearrange("p (j t) -> p j t", j=NJ)
            else:
                dst = xnr[:, :, 0:2050:2049]
                srcap = pst[b2].ap().rearrange("p (j t) -> p j t", j=NJ)[:, :, 0:2]
            p.op("vector", lambda e, dst=dst, srcap=srcap, b2=b2: e.tensor_copy(dst, srcap),
                 reads=[pst[b2]], writes=[st.xnT])
        st.dbg_store("xnT", st.xnT[:])
    st.barrier()


def _load_wT(st, bufs, wname, j, slot):
    p, d = st.p, st.d
    w16, pswt, wT = bufs
    p.op("gpsimd", lambda e, wname=wname, j=j, slot=slot: e.dma_start(
        out=w16[slot][:], in_=d[wname][128 * j:128 * (j + 1), :]),
        reads=[d[wname]], writes=[w16[slot]], dma=True)
    for j2 in range(NJ):
        p.op("tensor", lambda e, slot=slot, j2=j2: e.transpose(
            pswt[slot][:, 128 * j2:128 * (j2 + 1)], w16[slot][:, 128 * j2:128 * (j2 + 1)], st.ident[:]),
            reads=[w16[slot], st.ident], writes=[pswt[slot]], inc=(j2 == NJ - 1))
    p.op("vector", lambda e, slot=slot: e.tensor_copy(wT[slot][:], pswt[slot][:]),
         reads=[pswt[slot]], writes=[wT[slot]])
    return wT[slot]


def emit_l0_attn(st):
    p, d = st.p, st.d
    xn = st.xnT.ap().rearrange("p (j t) -> p j t", j=NJ)
    with ExitStack() as sc:
        sb = lambda nm, sh, dt=F32: st.sb(nm, sh, dt, scope=sc)
        vecs = sb("l0_vecs", [128, 40], F32)
        for i, nm in enumerate(["mixk_fm", "mixv_fm", "mixr_fm", "sd_fm", "sf_fm"]):
            p.op("sync", lambda e, nm=nm, i=i: e.dma_start(out=vecs[:, 8 * i:8 * (i + 1)], in_=d[nm]),
                 reads=[d[nm]], writes=[vecs], dma=True)
        mixk, mixv, mixr = vecs[:, 0:8], vecs[:, 8:16], vecs[:, 16:24]
        sdf, sff = vecs[:, 24:32], vecs[:, 32:40]
        aux = sb("l0_aux", [128, 64], F32)
        w8, u8, ew = aux[:, 0:8], aux[:, 8:16], aux[:, 16:24]
        expu, expnu = aux[:, 24:32], aux[:, 32:40]
        ain, binn = aux[:, 40:48], aux[:, 48:56]
        p.op("vector", lambda e: e.tensor_scalar(w8, sdf, 1.0 / S_FULL, None, ALU.mult),
             reads=[vecs], writes=[aux])
        p.op("vector", lambda e: e.tensor_scalar(u8, sff, 1.0 / S_FULL, None, ALU.mult),
             reads=[vecs, aux], writes=[aux])
        p.op("scalar", lambda e: e.activation(ew, w8, AF.Exp), reads=[aux], writes=[aux])
        p.op("scalar", lambda e: e.activation(expu, u8, AF.Exp), reads=[aux], writes=[aux])
        p.op("scalar", lambda e: e.activation(expnu, u8, AF.Exp, scale=-1.0),
             reads=[aux], writes=[aux])

        eku = sb("ekuT", [128, NJ * T_LOC], F16)
        vT = sb("vTbuf", [128, NJ * T_LOC], F16)
        srT = sb("srT", [128, NJ * T_LOC], F16)
        Sab = sb("Sab", [128, 16], F32)
        ekur = eku.ap().rearrange("p (j t) -> p j t", j=NJ)
        vTr = vT.ap().rearrange("p (j t) -> p j t", j=NJ)
        srr = srT.ap().rearrange("p (j t) -> p j t", j=NJ)
        wbufs = ([sb(f"l0_w16_{i}", [128, C], F16) for i in range(2)],
                 [st.ps(f"l0_pswt{i}", [128, NJ * 128], F16, scope=sc) for i in range(2)],
                 [sb(f"l0_wT{i}", [128, NJ * 128], F16) for i in range(2)])
        psk = st.ps("l0_psk", [128, 512], F32, scope=sc)
        psv = st.ps("l0_psv", [128, 512], F32, scope=sc)

        with ExitStack() as sc2:
            sb2 = lambda nm, sh, dt=F32: st.sb(nm, sh, dt, scope=sc2)
            mixT = st.sb("mixT", [128, NJ * T_LOC], F16, scope=sc2)
            mxr = mixT.ap().rearrange("p (j t) -> p j t", j=NJ)
            dscr = st.sb("l0_dscr", [128, T_LOC], F16, scope=sc2)

            def emit_mix(mixcol, dstT, dsc):
                dstr = dstT.ap().rearrange("p (j t) -> p j t", j=NJ)
                for j in range(NJ):
                    main = xn[:, j, 1:1 + T_LOC]
                    xx = xn[:, j, 0:T_LOC] if j < 2 else (xn[:, j, 2:2 + T_LOC] if j < 4 else main)
                    if j < 4:
                        p.op("vector", lambda e, main=main, xx=xx, dsc=dsc: e.tensor_sub(dsc[:], main, xx),
                             reads=[st.xnT], writes=[dsc])
                        p.op("vector", lambda e, j=j, xx=xx, mixcol=mixcol, dstr=dstr, dsc=dsc, dstT=dstT: e.scalar_tensor_tensor(
                            dstr[:, j, :], dsc[:], mixcol[:, j:j + 1], xx, ALU.mult, ALU.add),
                            reads=[dsc, vecs, st.xnT], writes=[dstT])
                    else:
                        p.op("vector", lambda e, j=j, main=main, dstr=dstr, dstT=dstT: e.tensor_copy(dstr[:, j, :], main),
                             reads=[st.xnT], writes=[dstT])

            def mm8(psum, wTbuf, rhsr, tc, extra_reads):
                for cj in range(8):
                    p.op("tensor", lambda e, cj=cj, tc=tc, psum=psum, wTbuf=wTbuf, rhsr=rhsr: e.matmul(
                        psum[:, :], wTbuf[:, 128 * cj:128 * (cj + 1)],
                        rhsr[:, cj, 512 * tc:512 * (tc + 1)], start=(cj == 0), stop=(cj == 7)),
                        reads=[wTbuf] + extra_reads, writes=[psum], inc=(cj == 7))

            emit_mix(mixk, mixT, dscr)
            wts = {0: _load_wT(st, wbufs, "Wk", 0, 0)}
            for j in range(NJ):
                wTk = wts.pop(j)
                if j + 1 < NJ:
                    wts[j + 1] = _load_wT(st, wbufs, "Wk", j + 1, (j + 1) % 2)
                for tc in range(4):
                    mm8(psk, wTk, mxr, tc, [mixT])
                    p.op("scalar", lambda e, j=j, tc=tc: e.activation(
                        ekur[:, j, 512 * tc:512 * (tc + 1)], psk[:, :], AF.Exp, bias=u8[:, j:j + 1]),
                        reads=[psk, aux], writes=[eku])
            dscr2 = st.sb("l0_dscr2", [128, T_LOC], F16, scope=sc2)
            emit_mix(mixv, srT, dscr2)
            srr_mix = srT.ap().rearrange("p (j t) -> p j t", j=NJ)
            wts = {0: _load_wT(st, wbufs, "Wv", 0, 0)}
            for j in range(NJ):
                wTv = wts.pop(j)
                if j + 1 < NJ:
                    wts[j + 1] = _load_wT(st, wbufs, "Wv", j + 1, (j + 1) % 2)
                for tc in range(4):
                    mm8(psv, wTv, srr_mix, tc, [srT])
                    p.op("vector", lambda e, j=j, tc=tc: e.tensor_copy(
                        vTr[:, j, 512 * tc:512 * (tc + 1)], psv[:, :]),
                        reads=[psv], writes=[vT])
            ekt = sb2("l0_ek", [128, T_LOC], F16)
            ppt = sb2("l0_pp", [128, T_LOC], F16)
            dmat = sb2("l0_dm16", [128, T_LOC], F16)
            junk = dscr
            for j in range(NJ):
                p.op("vector", lambda e, j=j: e.tensor_scalar(
                    ekt[:], ekur[:, j, :], expnu[:, j:j + 1], None, ALU.mult),
                    reads=[eku, aux], writes=[ekt])
                p.op("vector", lambda e, j=j: e.tensor_mul(ppt[:], ekt[:], vTr[:, j, :]),
                     reads=[ekt, vT], writes=[ppt])
                p.op("scalar", lambda e, j=j: e.activation(dmat[:], st.iotar[:], AF.Exp, scale=w8[:, j:j + 1]),
                     reads=[st.iotar, aux], writes=[dmat])
                p.op("vector", lambda e, j=j: e.scalar_tensor_tensor(
                    junk[:], ppt[:], 1.0, dmat[:], ALU.mult, ALU.mult, accum_out=Sab[:, j:j + 1]),
                    reads=[ppt, dmat], writes=[junk, Sab])
                p.op("vector", lambda e, j=j: e.scalar_tensor_tensor(
                    junk[:], ekt[:], 1.0, dmat[:], ALU.mult, ALU.mult, accum_out=Sab[:, 8 + j:9 + j]),
                    reads=[ekt, dmat], writes=[junk, Sab])
            st.dbg_store("eku", eku[:])
            st.dbg_store("Sab", Sab[:])

            p.op("sync", lambda e: e.dma_start(out=d["ccS_in"], in_=Sab[:]),
                 reads=[Sab], writes=[d["ccS_in"]], dma=True)
            p.op("gpsimd", lambda e: e.collective_compute(
                "AllGather", ALU.bypass, replica_groups=GROUPS,
                ins=[d["ccS_in"].opt()], outs=[d["ccS_out"].opt()]),
                reads=[d["ccS_in"]], writes=[d["ccS_out"]], q=st.cc_sems[0], inc_n=1)
            Sgf = sb2("Sgf", [128, 64], F32)
            for r in range(4):
                p.op("gpsimd", lambda e, r=r: e.dma_start(
                    out=Sgf[:, 16 * r:16 * (r + 1)], in_=d["ccS_out"][128 * r:128 * (r + 1), :]),
                    reads=[d["ccS_out"]], writes=[Sgf], dma=True)
            qrow = sb2("qrow", [1, 8], F16)
            p.op("gpsimd", lambda e: e.dma_start(out=qrow[0:1, 0:4], in_=d["qmask"]),
                 reads=[d["qmask"]], writes=[qrow], dma=True)
            p.op("gpsimd", lambda e: e.dma_start(out=qrow[0:1, 4:8], in_=d["qcoef"]),
                 reads=[d["qcoef"]], writes=[qrow], dma=True)
            ones128 = sb2("ones128", [1, 128], F16)
            p.op("vector", lambda e: e.memset(ones128[:], 1.0), writes=[ones128])
            psq8 = st.ps("l0_psq8", [128, 8], F32, scope=sc2)
            p.op("tensor", lambda e: e.matmul(psq8[:, :], ones128[0:1, :], qrow[0:1, :],
                                              start=True, stop=True),
                 reads=[ones128, qrow], writes=[psq8])
            qmb = aux[:, 56:60]
            qcb = aux[:, 60:64]
            p.op("vector", lambda e: e.tensor_copy(aux[:, 56:64], psq8[:, :]),
                 reads=[psq8], writes=[aux])
            sc8 = sb2("l0_sc8", [128, 8], F32)
            tm8 = sb2("l0_tm8", [128, 8], F32)
            p.op("vector", lambda e: e.memset(aux[:, 40:56], 0.0), writes=[aux])
            for r in range(4):
                p.op("vector", lambda e, r=r: e.tensor_scalar(
                    sc8[:], w8, qcb[:, r:r + 1], None, ALU.mult),
                    reads=[aux], writes=[sc8])
                p.op("scalar", lambda e: e.activation(sc8[:], sc8[:], AF.Exp),
                     reads=[sc8], writes=[sc8])
                p.op("vector", lambda e, r=r: e.tensor_scalar(
                    sc8[:], sc8[:], qmb[:, r:r + 1], None, ALU.mult),
                    reads=[sc8, aux], writes=[sc8])
                p.op("vector", lambda e, r=r: e.tensor_mul(tm8[:], Sgf[:, 16 * r:16 * r + 8], sc8[:]),
                     reads=[Sgf, sc8], writes=[tm8])
                p.op("vector", lambda e: e.tensor_add(ain, ain, tm8[:]),
                     reads=[tm8, aux], writes=[aux])
                p.op("vector", lambda e, r=r: e.tensor_mul(tm8[:], Sgf[:, 16 * r + 8:16 * r + 16], sc8[:]),
                     reads=[Sgf, sc8], writes=[tm8])
                p.op("vector", lambda e: e.tensor_add(binn, binn, tm8[:]),
                     reads=[tm8, aux], writes=[aux])
            st.dbg_store("ainbin", aux[:, 40:56])

            emit_mix(mixr, mixT, dscr)
            wts = {0: _load_wT(st, wbufs, "Wr", 0, 0)}
            for j in range(NJ):
                wTr = wts.pop(j)
                if j + 1 < NJ:
                    wts[j + 1] = _load_wT(st, wbufs, "Wr", j + 1, (j + 1) % 2)
                for tc in range(4):
                    mm8(psk, wTr, mxr, tc, [mixT])
                    p.op("scalar", lambda e, j=j, tc=tc: e.activation(
                        srr[:, j, 512 * tc:512 * (tc + 1)], psk[:, :], AF.Sigmoid),
                        reads=[psk], writes=[srT])

        st.barrier()
        ekt = sb("l02_ek", [128, T_LOC], F16)
        ppt = sb("l02_pp", [128, T_LOC], F16)
        dmat = sb("l02_dm", [128, T_LOC], F32)
        Abuf = sb("Abuf", [128, T_LOC + 1], F32)
        Bbuf = sb("Bbuf", [128, T_LOC + 1], F32)
        ewb = sb("ewb", [128, T_LOC], F32)
        numb = sb("numb", [128, T_LOC], F32)
        denb = sb("denb", [128, T_LOC], F32)
        p.op("vector", lambda e: e.memset(Abuf[:, 0:1], 0.0), writes=[Abuf])
        p.op("vector", lambda e: e.memset(Bbuf[:, 0:1], 0.0), reads=[Abuf], writes=[Bbuf])
        for j in range(NJ):
            p.op("vector", lambda e, j=j: e.tensor_scalar(
                ekt[:], ekur[:, j, :], expnu[:, j:j + 1], None, ALU.mult),
                reads=[eku, aux], writes=[ekt])
            p.op("vector", lambda e, j=j: e.tensor_mul(ppt[:], ekt[:], vTr[:, j, :]),
                 reads=[ekt, vT], writes=[ppt])
            p.op("vector", lambda e, j=j: e.tensor_scalar(
                ewb[:], st.iota[:], 0.0, ew[:, j:j + 1], ALU.mult, ALU.add),
                reads=[st.iota, aux], writes=[ewb])
            p.op("vector", lambda e: e.tensor_tensor_scan(
                Abuf[:, 1:], ewb[:], ppt[:], 0.0, ALU.mult, ALU.add),
                reads=[ewb, ppt], writes=[Abuf])
            p.op("vector", lambda e: e.tensor_tensor_scan(
                Bbuf[:, 1:], ewb[:], ekt[:], 0.0, ALU.mult, ALU.add),
                reads=[ewb, ekt], writes=[Bbuf])
            p.op("scalar", lambda e, j=j: e.activation(dmat[:], st.iota[:], AF.Exp, scale=w8[:, j:j + 1]),
                 reads=[st.iota, aux], writes=[dmat])
            p.op("vector", lambda e, j=j: e.scalar_tensor_tensor(
                numb[:], dmat[:], ain[:, j:j + 1], Abuf[:, 0:T_LOC], ALU.mult, ALU.add),
                reads=[dmat, aux, Abuf], writes=[numb])
            p.op("vector", lambda e, j=j: e.scalar_tensor_tensor(
                numb[:], ppt[:], expu[:, j:j + 1], numb[:], ALU.mult, ALU.add),
                reads=[ppt, aux, numb], writes=[numb])
            p.op("vector", lambda e, j=j: e.scalar_tensor_tensor(
                denb[:], dmat[:], binn[:, j:j + 1], Bbuf[:, 0:T_LOC], ALU.mult, ALU.add),
                reads=[dmat, aux, Bbuf], writes=[denb])
            p.op("vector", lambda e, j=j: e.scalar_tensor_tensor(
                denb[:], ekt[:], expu[:, j:j + 1], denb[:], ALU.mult, ALU.add),
                reads=[ekt, aux, denb], writes=[denb])
            p.op("scalar", lambda e: e.activation(dmat[:], denb[:], AF.Ln),
                 reads=[denb], writes=[dmat])
            p.op("scalar", lambda e: e.activation(dmat[:], dmat[:], AF.Exp, scale=-1.0),
                 reads=[dmat], writes=[dmat])
            p.op("vector", lambda e: e.tensor_mul(numb[:], numb[:], dmat[:]),
                 reads=[numb, dmat], writes=[numb])
            p.op("vector", lambda e, j=j: e.tensor_mul(vTr[:, j, :], numb[:], srr[:, j, :]),
                 reads=[numb, srT], writes=[vT])
        st.dbg_store("srwkvT", vT[:])

        s0T = eku
        s0r = s0T.ap().rearrange("p (j t) -> p j t", j=NJ)
        wts = {0: _load_wT(st, wbufs, "Wo", 0, 0)}
        for j in range(NJ):
            wTo = wts.pop(j)
            if j + 1 < NJ:
                wts[j + 1] = _load_wT(st, wbufs, "Wo", j + 1, (j + 1) % 2)
            for tc in range(4):
                for cj in range(8):
                    p.op("tensor", lambda e, cj=cj, tc=tc, wTo=wTo: e.matmul(
                        psv[:, :], wTo[:, 128 * cj:128 * (cj + 1)],
                        vTr[:, cj, 512 * tc:512 * (tc + 1)], start=(cj == 0), stop=(cj == 7)),
                        reads=[wTo, vT], writes=[psv], inc=(cj == 7))
                p.op("vector", lambda e, j=j, tc=tc: e.tensor_copy(
                    s0r[:, j, 512 * tc:512 * (tc + 1)], psv[:, :]),
                    reads=[psv], writes=[s0T])
        pst2 = st.ps("l0_pst2", [128, 1024], F16, scope=sc)
        s0tile = [sb(f"s0t{i}", [128, C], F16) for i in range(2)]
        for i in range(NT):
            b = i % 2
            for j in range(NJ):
                p.op("tensor", lambda e, i=i, j=j: e.transpose(
                    pst2[:, 128 * j:128 * (j + 1)], s0r[:, j, 128 * i:128 * (i + 1)], st.ident[:]),
                    reads=[s0T, st.ident], writes=[pst2], inc=(j == NJ - 1))
            p.op("vector", lambda e, i=i, b=b: e.tensor_copy(s0tile[b][:], pst2[:]),
                 reads=[pst2], writes=[s0tile[b]])
            p.op("sync", lambda e, i=i, b=b: e.dma_start(
                out=d["s0_d"][128 * i:128 * (i + 1), :], in_=s0tile[b][:]),
                reads=[s0tile[b]], writes=[(d["s0_d"], i)], dma=True)
        st.dbg_store("s0", d["s0_d"])
    st.barrier()



def emit_moe(st):
    p, d = st.p, st.d
    xnr = st.xnT.ap().rearrange("p (j t) -> p j t", j=NJ)

    def xns(j, t0, t1):
        return xnr[:, j, 1 + t0:1 + t1]

    with ExitStack() as sc:
        sb = lambda nm, sh, dt=F32: st.sb(nm, sh, dt, scope=sc)
        gws = sb("gws16", [128, 64], F16)
        p.op("gpsimd", lambda e: e.dma_start(out=gws[:], in_=d["gws"]),
             reads=[d["gws"]], writes=[gws], dma=True)
        psl = st.ps("moe_psl", [128, 8], F32, scope=sc)
        cw = sb("cw", [128, NT * NE0], F32)
        Escr = sb("moe_E", [128, 8], F32)
        m1s = sb("moe_m1", [128, NT], F32)
        m2s = sb("moe_m2", [128, NT], F32)
        zs = sb("moe_z", [128, NT], F32)
        nmax = sb("moe_nm", [128, 1], F32)
        msk = sb("moe_msk", [128, 8], F32)
        gm = sb("moe_gm", [128, 8], F32)
        den = sb("moe_den", [128, NT], F32)
        cwT = sb("cwT", [8, T_LOC], F16)
        cw16 = sb("cw16", [128, 8], F16)
        pscT = st.ps("moe_pscT", [8, 128], F16, scope=sc)
        for tt in range(NT):
            for cj in range(NJ):
                p.op("tensor", lambda e, tt=tt, cj=cj: e.matmul(
                    psl[:, :], xns(cj, 128 * tt, 128 * (tt + 1)), gws[:, 8 * cj:8 * (cj + 1)],
                    start=(cj == 0), stop=(cj == 7)),
                    reads=[st.xnT, gws], writes=[psl], inc=(cj == 7))
            p.op("vector", lambda e: e.reduce_max(nmax[:, 0:1], psl[:, :], axis=AX, negate=True),
                 reads=[psl], writes=[nmax])
            p.op("scalar", lambda e: e.activation(Escr[:, :], psl[:, :], AF.Exp, bias=nmax[:, 0:1]),
                 reads=[psl, nmax], writes=[Escr])
            p.op("vector", lambda e, tt=tt: e.reduce_sum(zs[:, tt:tt + 1], Escr[:, :], axis=AX),
                 reads=[Escr], writes=[zs])
            p.op("vector", lambda e, tt=tt: e.reduce_max(m1s[:, tt:tt + 1], Escr[:, :], axis=AX),
                 reads=[Escr], writes=[m1s])
            p.op("vector", lambda e, tt=tt: e.tensor_scalar(
                msk[:, :], Escr[:, :], m1s[:, tt:tt + 1], None, ALU.is_ge),
                reads=[Escr, m1s], writes=[msk])
            p.op("vector", lambda e: e.scalar_tensor_tensor(
                gm[:, :], msk[:, :], -1e30, Escr[:, :], ALU.mult, ALU.add),
                reads=[msk, Escr], writes=[gm])
            p.op("vector", lambda e, tt=tt: e.reduce_max(m2s[:, tt:tt + 1], gm[:, :], axis=AX),
                 reads=[gm], writes=[m2s])
            p.op("vector", lambda e, tt=tt: e.tensor_scalar(
                msk[:, :], Escr[:, :], m2s[:, tt:tt + 1], None, ALU.is_ge),
                reads=[Escr, m2s], writes=[msk])
            p.op("vector", lambda e, tt=tt: e.tensor_mul(
                cw[:, NE0 * tt:NE0 * (tt + 1)], Escr[:, :], msk[:, :]),
                reads=[Escr, msk], writes=[cw])
        p.op("vector", lambda e: e.tensor_add(den[:, :], m1s[:, :], m2s[:, :]),
             reads=[m1s, m2s], writes=[den])
        p.op("vector", lambda e: e.scalar_tensor_tensor(den[:, :], zs[:, :], 1e-8, den[:, :], ALU.mult, ALU.add),
             reads=[zs, den], writes=[den])
        p.op("vector", lambda e: e.reciprocal(den[:, :], den[:, :]), reads=[den], writes=[den])
        for tt in range(NT):
            p.op("vector", lambda e, tt=tt: e.tensor_scalar(
                cw[:, NE0 * tt:NE0 * (tt + 1)], cw[:, NE0 * tt:NE0 * (tt + 1)], den[:, tt:tt + 1],
                None, ALU.mult),
                reads=[cw, den], writes=[cw])
            p.op("vector", lambda e, tt=tt: e.tensor_copy(cw16[:, :], cw[:, NE0 * tt:NE0 * (tt + 1)]),
                 reads=[cw], writes=[cw16])
            p.op("tensor", lambda e: e.transpose(pscT[:, :], cw16[:, :], st.ident[:]),
                 reads=[cw16, st.ident], writes=[pscT])
            p.op("vector", lambda e, tt=tt: e.tensor_copy(cwT[:, 128 * tt:128 * (tt + 1)], pscT[0:8, :]),
                 reads=[pscT], writes=[cwT])
        st.dbg_store("cw", cw[:])

        g0 = sb("g0acc", [128, NT * C], F16)
        b2s = sb("b2s", [8, C], F16)
        p.op("gpsimd", lambda e: e.dma_start(out=b2s[:], in_=d["b2"]),
             reads=[d["b2"]], writes=[b2s], dma=True)
        psb = st.ps("moe_psb", [128, 1024], F32, scope=sc)
        xts = [sb(f"moe_x{i}", [128, C], F32) for i in range(2)]
        for tt in range(NT):
            b = tt % 2
            for h in range(2):
                p.op("tensor", lambda e, tt=tt, h=h: e.matmul(
                    psb[:, 512 * h:512 * (h + 1)], cwT[:, 128 * tt:128 * (tt + 1)],
                    b2s[:, 512 * h:512 * (h + 1)], start=True, stop=True),
                    reads=[cwT, b2s], writes=[psb], inc=(h == 1))
            p.op("sync", lambda e, tt=tt, b=b: e.dma_start(
                out=xts[b][:], in_=d["x_main"][128 * tt:128 * (tt + 1), :]),
                reads=[d["x_main"]], writes=[xts[b]], dma=True)
            p.op("vector", lambda e, tt=tt, b=b: e.tensor_scalar(
                xts[b][:], xts[b][:], st.rstd0[:, tt:tt + 1], st.nmr0[:, tt:tt + 1], ALU.mult, ALU.add),
                reads=[xts[b], st.rstd0, st.nmr0], writes=[xts[b]])
            p.op("vector", lambda e, tt=tt, b=b: e.tensor_add(
                g0[:, C * tt:C * (tt + 1)], xts[b][:], psb[:, :]),
                reads=[xts[b], psb], writes=[g0])

        W1s = sb("W1s", [128, NJ * HID], F16)
        W2s = sb("W2s", [128, 16 * C], F16)
        hT = sb("moe_hT", [128, 16 * 512], F16)
        b1s = sb("b1s", [128, 128], F32)
        ysc = sb("moe_ysc", [128, 1024], F16)
        p.op("sync", lambda e: e.dma_start(out=b1s[:], in_=d["b1_fm"]),
             reads=[d["b1_fm"]], writes=[b1s], dma=True)
        psh = [st.ps(f"moe_psh{i}", [128, 512], F32, scope=sc) for i in range(2)]
        psy = st.ps("moe_psy", [128, 1024], F32, scope=sc)
        w1r = W1s.ap().rearrange("p (j h) -> p j h", j=NJ)

        for e_ in range(NE0):
            p.op("gpsimd", lambda e, e_=e_: e.dma_start(
                out=W1s.ap().rearrange("p (j h) -> p j h", j=NJ),
                in_=d["W1"][e_].rearrange("(j p) h -> p j h", p=128)),
                reads=[d["W1"]], writes=[W1s], dma=True)
            p.op("gpsimd", lambda e, e_=e_: e.dma_start(
                out=W2s.ap().rearrange("p (h o) -> p h o", h=16),
                in_=d["W2"][e_].rearrange("(h p) o -> p h o", p=128)),
                reads=[d["W2"]], writes=[W2s], dma=True)
            for tc in range(4):
                for ht in range(16):
                    pb = ht % 2
                    for cj in range(NJ):
                        p.op("tensor", lambda e, tc=tc, ht=ht, cj=cj, pb=pb: e.matmul(
                            psh[pb][:, :], w1r[:, cj, 128 * ht:128 * (ht + 1)],
                            xns(cj, 512 * tc, 512 * (tc + 1)),
                            start=(cj == 0), stop=(cj == 7)),
                            reads=[W1s, st.xnT], writes=[psh[pb]], inc=(cj == 7))
                    p.op("scalar", lambda e, ht=ht, pb=pb, e_=e_: e.activation(
                        hT[:, 512 * ht:512 * (ht + 1)], psh[pb][:, :], AF.Relu,
                        bias=b1s[:, 16 * e_ + ht:16 * e_ + ht + 1]),
                        reads=[psh[pb], b1s], writes=[hT])
                for tt2 in range(4):
                    tt = 4 * tc + tt2
                    for ht in range(16):
                        for hh in range(2):
                            p.op("tensor", lambda e, tt2=tt2, ht=ht, hh=hh: e.matmul(
                                psy[:, 512 * hh:512 * (hh + 1)],
                                hT[:, 512 * ht + 128 * tt2:512 * ht + 128 * (tt2 + 1)],
                                W2s[:, C * ht + 512 * hh:C * ht + 512 * (hh + 1)],
                                start=(ht == 0), stop=(ht == 15)),
                                reads=[hT, W2s], writes=[psy],
                                inc=(ht == 15 and hh == 1))
                    p.op("scalar", lambda e, tt=tt, e_=e_: e.activation(
                        ysc[:, :], psy[:, :], AF.Copy, scale=cw[:, NE0 * tt + e_:NE0 * tt + e_ + 1]),
                        reads=[psy, cw], writes=[ysc])
                    p.op("vector", lambda e, tt=tt: e.tensor_add(
                        g0[:, C * tt:C * (tt + 1)], g0[:, C * tt:C * (tt + 1)], ysc[:, :]),
                        reads=[ysc, g0], writes=[g0])
        st.dbg_store("g0pre", g0[:])

        sg = sb("moe_sg", [128, C], F16)
        s0tl = [sb(f"moe_s0{i}", [128, C], F16) for i in range(2)]
        for tt in range(NT):
            b = tt % 2
            p.op("sync", lambda e, tt=tt, b=b: e.dma_start(
                out=s0tl[b][:], in_=d["s0_d"][128 * tt:128 * (tt + 1), :]),
                reads=[(d["s0_d"], tt)], writes=[s0tl[b]], dma=True)
            p.op("scalar", lambda e, tt=tt: e.activation(
                g0[:, C * tt:C * (tt + 1)], g0[:, C * tt:C * (tt + 1)], AF.Silu),
                reads=[g0], writes=[g0])
            p.op("vector", lambda e, tt=tt, b=b: e.tensor_mul(
                sg[:, :], s0tl[b][:], g0[:, C * tt:C * (tt + 1)]),
                reads=[s0tl[b], g0], writes=[sg])
            p.op("sync", lambda e, tt=tt, b=b: e.dma_start(
                out=xts[b][:], in_=d["x_main"][128 * tt:128 * (tt + 1), :]),
                reads=[d["x_main"]], writes=[xts[b]], dma=True)
            p.op("vector", lambda e, tt=tt, b=b: e.tensor_add(xts[b][:], xts[b][:], sg[:, :]),
                 reads=[sg, xts[b]], writes=[xts[b]])
            p.op("sync", lambda e, tt=tt, b=b: e.dma_start(
                out=d["x1_d"][128 * tt:128 * (tt + 1), :], in_=xts[b][:]),
                reads=[xts[b]], writes=[(d["x1_d"], tt)], dma=True)
        st.dbg_store("x1", d["x1_d"])
    st.barrier()



def emit_l1_qkv(st, scope_l1, scope_q, scope_x):
    p, d = st.p, st.d
    st.rstd1 = st.sb("rstd1", [128, 16], F32, scope=scope_l1)
    st.nmr1 = st.sb("nmr1", [128, 16], F32, scope=scope_l1)
    st.ctxT = st.sb("ctxT", [128, NJ * T_LOC], F16, scope=scope_l1, side="right")
    st.scoresT = st.sb("scoresT", [128, NE1 * T_LOC], F16, scope=scope_l1, side="right")
    st.xn1T = st.sb("xn1T", [128, NJ * T_LOC], F16, scope=scope_x)
    xnr1 = st.xn1T.ap().rearrange("p (j t) -> p j t", j=NJ)
    with ExitStack() as sc:
        sb = lambda nm, sh, dt=F32: st.sb(nm, sh, dt, scope=sc)

        sums = sb("ln1_sums", [128, 17])
        sqs = sb("ln1_sqs", [128, 17])
        sxt = [sb(f"ln1_x{i}", [128, C], F32) for i in range(4)]
        sqscr = [sb(f"ln1_q{i}", [128, C], F32) for i in range(2)]
        mean = sb("ln1_mean", [128, 17])
        var = sb("ln1_var", [128, 17])
        p.op("vector", lambda e: e.memset(sums[:], 0.0), writes=[sums])
        p.op("vector", lambda e: e.memset(sqs[:], 0.0), writes=[sqs])
        rstd, nmr = st.rstd1, st.nmr1

        def stat_tile(i):
            b = i % 4
            p.op("sync", lambda e, i=i, b=b: e.dma_start(
                out=sxt[b][:], in_=d["x1_d"][128 * i:128 * (i + 1), :]),
                reads=[(d["x1_d"], i)], writes=[sxt[b]], dma=True)
            p.op("vector", lambda e, b=b, i=i: e.reduce_sum(
                sums[:, i:i + 1], sxt[b][:], axis=AX),
                reads=[sxt[b]], writes=[sums])
            b2 = i % 2
            p.op("scalar", lambda e, b=b, b2=b2, i=i: e.activation(
                sqscr[b2][:], sxt[b][:], AF.Square, accum_out=sqs[:, i:i + 1]),
                reads=[sxt[b]], writes=[sqscr[b2], sqs])

        def stat_math(c0, c1):
            s_ = slice(c0, c1)
            p.op("vector", lambda e, s_=s_: e.tensor_scalar(
                mean[:, s_], sums[:, s_], 1.0 / C, None, ALU.mult),
                reads=[sums], writes=[mean])
            p.op("vector", lambda e, s_=s_: e.scalar_tensor_tensor(
                var[:, s_], mean[:, s_], -1.0, mean[:, s_], ALU.mult, ALU.mult),
                reads=[mean], writes=[var])
            p.op("vector", lambda e, s_=s_: e.scalar_tensor_tensor(
                var[:, s_], sqs[:, s_], 1.0 / C, var[:, s_], ALU.mult, ALU.add),
                reads=[sqs, var], writes=[var])
            p.op("scalar", lambda e, s_=s_: e.activation(
                var[:, s_], var[:, s_], AF.Sqrt, bias=LN_EPS),
                reads=[var], writes=[var])
            p.op("vector", lambda e, s_=s_: e.reciprocal(rstd[:, s_], var[:, s_]),
                 reads=[var], writes=[rstd])
            p.op("vector", lambda e, s_=s_: e.scalar_tensor_tensor(
                nmr[:, s_], mean[:, s_], -1.0, rstd[:, s_], ALU.mult, ALU.mult),
                reads=[mean, rstd], writes=[nmr])

        xt = [sb(f"p4_x{i}", [128, C], F32) for i in range(4)]
        xn16 = [sb(f"p4_n{i}", [128, C], F16) for i in range(2)]
        pst = [st.ps(f"p4_t{i}", [128, 8 * 128], F16, scope=sc) for i in range(2)]

        def rebuild_tile(i):
            b = i % 4
            p.op("sync", lambda e, i=i, b=b: e.dma_start(
                out=xt[b][:], in_=d["x1_d"][128 * i:128 * (i + 1), :]),
                reads=[(d["x1_d"], i)], writes=[xt[b]], dma=True)
            b2 = i % 2
            p.op("vector", lambda e, b=b, b2=b2, i=i: e.tensor_scalar(
                xn16[b2][:], xt[b][:], st.rstd1[:, i:i + 1], st.nmr1[:, i:i + 1], ALU.mult, ALU.add),
                reads=[xt[b], st.rstd1, st.nmr1], writes=[xn16[b2]])
            for j in range(NJ):
                p.op("tensor", lambda e, b2=b2, j=j: e.transpose(
                    pst[b2][:, 128 * j:128 * (j + 1)], xn16[b2][:, 128 * j:128 * (j + 1)], st.ident[:]),
                    reads=[xn16[b2], st.ident], writes=[pst[b2]], inc=(j == NJ - 1))
            p.op("vector", lambda e, b2=b2, i=i: e.tensor_copy(
                xnr1[:, :, 128 * i:128 * (i + 1)], pst[b2].ap().rearrange("p (j t) -> p j t", j=NJ)),
                reads=[pst[b2]], writes=[st.xn1T])

        for i in range(8):
            stat_tile(i)
        stat_math(0, 8)
        for i in range(8, NT):
            stat_tile(i)
        for i in range(8):
            rebuild_tile(i)
        stat_math(8, NT)
        for i in range(8, NT):
            rebuild_tile(i)
        st.dbg_store("xn1T", st.xn1T[:])

        kvtile = [sb(f"p4_kv{i}", [128, 512], F16) for i in range(2)]
        bqs = sb("bqs", [128, 24], F32)
        for i, nm in enumerate(["bq_fm", "bk_fm", "bv_fm"]):
            p.op("sync", lambda e, nm=nm, i=i: e.dma_start(out=bqs[:, 8 * i:8 * (i + 1)], in_=d[nm]),
                 reads=[d[nm]], writes=[bqs], dma=True)
        wsb = [sb(f"p4_w{i}", [128, NJ * 128], F16) for i in range(2)]
        psq = [st.ps(f"p4_q{i}", [128, 512], F32, scope=sc) for i in range(2)]

        def proj(wname, dstT, dst_dram, bcol, j, slot):
            p.op("gpsimd", lambda e, wname=wname, j=j, slot=slot: e.dma_start(
                out=wsb[slot][:], in_=d[wname + "R"][j]),
                reads=[d[wname + "R"]], writes=[wsb[slot]], dma=True)
            dstr = dstT.ap().rearrange("p (j t) -> p j t", j=NJ) if dstT is not None else None
            for tc in range(4):
                pb = tc % 2
                for cj in range(NJ):
                    p.op("tensor", lambda e, tc=tc, cj=cj, pb=pb, slot=slot: e.matmul(
                        psq[pb][:, :], wsb[slot][:, 128 * cj:128 * (cj + 1)],
                        xnr1[:, cj, 512 * tc:512 * (tc + 1)], start=(cj == 0), stop=(cj == 7)),
                        reads=[wsb[slot], st.xn1T], writes=[psq[pb]], inc=(cj == 7))
                if dstT is not None:
                    p.op("vector", lambda e, j=j, tc=tc, pb=pb, bcol=bcol, dstr=dstr, dstT=dstT: e.tensor_scalar(
                        dstr[:, j, 512 * tc:512 * (tc + 1)], psq[pb][:, :],
                        bqs[:, bcol * 8 + j:bcol * 8 + j + 1], None, ALU.add),
                        reads=[psq[pb], bqs], writes=[dstT])
                else:
                    kb = tc % 2
                    p.op("vector", lambda e, j=j, tc=tc, pb=pb, kb=kb, bcol=bcol: e.tensor_scalar(
                        kvtile[kb][:, :], psq[pb][:, :],
                        bqs[:, bcol * 8 + j:bcol * 8 + j + 1], None, ALU.add),
                        reads=[psq[pb], bqs], writes=[kvtile[kb]])
                    p.op("sync", lambda e, j=j, tc=tc, kb=kb, dst_dram=dst_dram: e.dma_start(
                        out=dst_dram[128 * j:128 * (j + 1), 512 * tc:512 * (tc + 1)],
                        in_=kvtile[kb][:, :]),
                        reads=[kvtile[kb]], writes=[dst_dram], dma=True)

        for j in range(NJ):
            proj("wq", None, d["q_d"], 0, j, j % 2)
        for j in range(NJ):
            proj("wk", None, d["ccK_in"], 1, j, j % 2)
        for j in range(NJ):
            proj("wv", None, d["ccV_in"], 2, j, j % 2)
        p.op("gpsimd", lambda e: e.collective_compute(
            "AllGather", ALU.bypass, replica_groups=GROUPS,
            ins=[d["ccK_in"].opt()], outs=[d["ccK_out"].opt()]),
            reads=[d["ccK_in"]], writes=[d["ccK_out"]], q=st.cc_sems[1], inc_n=1)
        p.op("gpsimd", lambda e: e.collective_compute(
            "AllGather", ALU.bypass, replica_groups=GROUPS,
            ins=[d["ccV_in"].opt()], outs=[d["ccV_out"].opt()]),
            reads=[d["ccV_in"]], writes=[d["ccV_out"]], q=st.cc_sems[2], inc_n=1)
    st.barrier()



def emit_l1_attn(st, scope_q):
    p, d = st.p, st.d
    ctr = st.ctxT.ap().rearrange("p (j t) -> p j t", j=NJ)
    with ExitStack() as sc:
        sb = lambda nm, sh, dt=F32: st.sb(nm, sh, dt, scope=sc)
        KVH = [sb(f"a_kvh{i}", [128, S_FULL], F16) for i in range(4)]
        VH = [sb(f"a_vh{i}", [128, 64 * 65], F16) for i in range(4)]
        QR = [sb(f"a_qr{i}", [128, T_LOC], F16) for i in range(2)]
        PT = [sb(f"a_pt{i}", [128, 1024], F16) for i in range(2)]
        zrow = sb("a_z", [1, 512], F16)
        ups = sb("a_u", [64, 512], F16)
        ones1 = sb("a_ones", [1, 64], F16)
        psS = [st.ps(f"a_psS{i}", [128, 1024], F32, scope=sc) for i in range(2)]
        psC = [st.ps(f"a_psC{i}", [65, 512], F32, scope=sc) for i in range(2)]
        psT = [st.ps(f"a_psT{i}", [128, 512], F16, scope=sc) for i in range(2)]
        p.op("vector", lambda e: e.memset(ones1[:], 1.0), writes=[ones1])

        def qr_load(m):
            s = m % 2
            p.op("sync", lambda e, m=m, s=s: e.dma_start(
                out=QR[s][:], in_=d["q_d"][128 * m:128 * (m + 1), :]),
                reads=[d["q_d"]], writes=[QR[s]], dma=True)

        def kv_load(h):
            ss = h % 4
            prow = 64 * (h % 2)
            vrow = 64 - prow
            KH_h = KVH[ss][prow:prow + 64, :]
            VHT_h = KVH[ss][vrow:vrow + 64, :]
            for r in range(4):
                p.op("sync", lambda e, h=h, r=r, KH_h=KH_h, ss=ss: e.dma_start(
                    out=KH_h[:, T_LOC * r:T_LOC * (r + 1)],
                    in_=d["ccK_out"][C * r + HD * h:C * r + HD * (h + 1), :]),
                    reads=[d["ccK_out"]], writes=[KVH[ss]], dma=True)
                p.op("sync", lambda e, h=h, r=r, VHT_h=VHT_h, ss=ss: e.dma_start(
                    out=VHT_h[:, T_LOC * r:T_LOC * (r + 1)],
                    in_=d["ccV_out"][C * r + HD * h:C * r + HD * (h + 1), :]),
                    reads=[d["ccV_out"]], writes=[KVH[ss]], dma=True)

        def vh_build(h):
            ss = h % 4
            vrow = 64 - 64 * (h % 2)
            VHT_h = KVH[ss][vrow:vrow + 64, :]
            p.op("vector", lambda e, ss=ss: e.memset(VH[ss][:], 1.0),
                 reads=[(VH[ss], "t")], writes=[VH[ss]])
            for g2 in range(16):
                tb = psT[g2 % 2]
                for kb in range(4):
                    p.op("tensor", lambda e, ss=ss, g2=g2, kb=kb, VHT_h=VHT_h, vrow=vrow, tb=tb: e.transpose(
                        tb[:, 128 * kb:128 * kb + 64],
                        VHT_h[:, 128 * (4 * g2 + kb):128 * (4 * g2 + kb + 1)],
                        st.idblk[vrow:vrow + 64, :]),
                        reads=[KVH[ss], st.idblk], writes=[tb], inc=(kb == 3))
                p.op("vector", lambda e, ss=ss, g2=g2, tb=tb: e.tensor_copy(
                    VH[ss].ap().rearrange("p (kb x) -> p kb x", x=65)[:, 4 * g2:4 * (g2 + 1), 0:64],
                    tb.ap().rearrange("p (kb x) -> p kb x", x=128)[:, :, 0:64]),
                    reads=[tb, VH[ss]], writes=[(VH[ss], "t")])

        NG = 64

        def emit_qk(m, qt, gi):
            s = m % 2
            ss0, ss1 = (2 * m) % 4, (2 * m + 1) % 4
            pss = psS[gi % 2]
            p.op("tensor", lambda e, s=s, ss0=ss0, qt=qt, gi=gi, pss=pss: e.matmul(
                pss[:, 0:512],
                KVH[ss0][0:64, 128 * gi:128 * (gi + 1)],
                QR[s][0:64, 512 * qt:512 * (qt + 1)],
                start=True, stop=True, tile_position=(0, 0)),
                reads=[KVH[ss0], QR[s]], writes=[pss], inc=False)
            p.op("tensor", lambda e, s=s, ss1=ss1, qt=qt, gi=gi, pss=pss: e.matmul(
                pss[:, 512:1024],
                KVH[ss1][64:128, 128 * gi:128 * (gi + 1)],
                QR[s][64:128, 512 * qt:512 * (qt + 1)],
                start=True, stop=True, tile_position=(64, 0)),
                reads=[KVH[ss1], QR[s]], writes=[pss], inc=True)

        def emit_exp(gi):
            pb = gi % 2
            pss = psS[pb]
            p.op("scalar", lambda e, pb=pb, pss=pss: e.activation(
                PT[pb][:, :], pss[:, :], AF.Exp, scale=0.125),
                reads=[pss], writes=[PT[pb]])

        def emit_av(m, gi):
            pb = gi % 2
            last = (gi == NG - 1)
            for hh in range(2):
                ss = (2 * m + hh) % 4
                p.op("tensor", lambda e, ss=ss, pb=pb, gi=gi, hh=hh, last=last: e.matmul(
                    psC[hh][:, :], VH[ss][:, 65 * gi:65 * (gi + 1)],
                    PT[pb][:, 512 * hh:512 * (hh + 1)],
                    start=(gi == 0), stop=last),
                    reads=[VH[ss], (VH[ss], "t"), PT[pb]], writes=[psC[hh]],
                    inc=(last and hh == 1))

        qr_load(0)
        kv_load(0)
        vh_build(0)
        kv_load(1)
        vh_build(1)
        for m in range(NH // 2):
            h2, h3 = 2 * m + 2, 2 * m + 3
            if h2 < NH:
                kv_load(h2)
                kv_load(h3)
            if m + 1 < NH // 2:
                qr_load(m + 1)
            for qt in range(4):
                emit_qk(m, qt, 0)
                for gi in range(NG):
                    emit_exp(gi)
                    if gi >= 1:
                        emit_av(m, gi - 1)
                    if gi + 1 < NG:
                        emit_qk(m, qt, gi + 1)
                emit_av(m, NG - 1)
                for hh in range(2):
                    prow = 64 * hh
                    p.op("vector", lambda e, hh=hh: e.tensor_scalar(
                        ups[:, :], psC[hh][0:64, :], 1.0 / 4096.0, None, ALU.mult),
                        reads=[psC[hh]], writes=[ups])
                    p.op("vector", lambda e, hh=hh: e.reciprocal(psS[0][0:1, 0:512], psC[hh][64:65, :]),
                         reads=[psC[hh]], writes=[psS[0]])
                    p.op("vector", lambda e: e.tensor_scalar(
                        zrow[0:1, :], psS[0][0:1, 0:512], 4096.0, None, ALU.mult),
                        reads=[psS[0]], writes=[zrow])
                    p.op("tensor", lambda e, hh=hh: e.matmul(
                        psC[hh][0:64, :], ones1[0:1, :], zrow[0:1, :],
                        start=True, stop=True),
                        reads=[ones1, zrow], writes=[psC[hh]])
                    p.op("vector", lambda e, m=m, qt=qt, hh=hh, prow=prow: e.tensor_mul(
                        ctr[prow:prow + 64, m, 512 * qt:512 * (qt + 1)],
                        ups[:, :], psC[hh][0:64, :]),
                        reads=[ups, psC[hh]], writes=[st.ctxT])
                if qt == 1 and h2 < NH:
                    vh_build(h2)
                if qt == 2 and h2 < NH:
                    vh_build(h3)
        st.dbg_store("ctxT", st.ctxT[:])
    st.barrier()


def emit_l1_ffn_scores(st, scope):
    p, d = st.p, st.d
    xnr1 = st.xn1T.ap().rearrange("p (j t) -> p j t", j=NJ)
    scr = st.scoresT.ap().rearrange("p (E t) -> p E t", E=NE1)
    if True:
        sc = scope
        sb = lambda nm, sh, dt=F32: st.sb(nm, sh, dt, scope=sc)
        es16 = sb("es16", [16, C], F16)
        p.op("gpsimd", lambda e: e.dma_start(out=es16[:], in_=d["esel"]),
             reads=[d["esel"]], writes=[es16], dma=True)
        eselT = sb("eselT", [128, NJ * 16], F16)
        psE = st.ps("p6_psE", [128, 16], F16, scope=sc)
        for j in range(NJ):
            p.op("tensor", lambda e, j=j: e.transpose(
                psE[:, :], es16[:, 128 * j:128 * (j + 1)], st.ident[0:16, 0:16]),
                reads=[es16, st.ident], writes=[psE])
            p.op("vector", lambda e, j=j: e.tensor_copy(eselT[:, 16 * j:16 * (j + 1)], psE[:, :]),
                 reads=[psE], writes=[eselT])
        psSel = st.ps("p6_psSel", [16, 512], F32, scope=sc)
        selT = sb("selT", [16, T_LOC], F16)
        for tc in range(4):
            for cj in range(NJ):
                p.op("tensor", lambda e, tc=tc, cj=cj: e.matmul(
                    psSel[:, :], eselT[:, 16 * cj:16 * (cj + 1)],
                    xnr1[:, cj, 512 * tc:512 * (tc + 1)], start=(cj == 0), stop=(cj == 7)),
                    reads=[eselT, st.xn1T], writes=[psSel], inc=(cj == 7))
            p.op("scalar", lambda e, tc=tc: e.activation(
                selT[:, 512 * tc:512 * (tc + 1)], psSel[:, :], AF.Sigmoid),
                reads=[psSel], writes=[selT])

        keysb = [sb(f"p6_k{i}", [128, NJ * 128], F16) for i in range(2)]
        sel1h = sb("p6_1h", [16, NE1 * 128], F16)
        p.op("gpsimd", lambda e: e.dma_start(out=sel1h[:], in_=d["sel1h"]),
             reads=[d["sel1h"]], writes=[sel1h], dma=True)
        relscr = sb("p6_rel", [128, 512], F16)
        psSc = [st.ps(f"p6_psSc{i}", [128, 512], F32, scope=sc) for i in range(2)]
        psBc = st.ps("p6_psBc", [128, 512], F32, scope=sc)
        p.op("gpsimd", lambda e: e.dma_start(out=keysb[0][:], in_=d["keysR"][0]),
             reads=[d["keysR"]], writes=[keysb[0]], dma=True)
        for e_ in range(NE1):
            ks = e_ % 2
            if e_ + 1 < NE1:
                p.op("gpsimd", lambda e, e_=e_, ks=ks: e.dma_start(
                    out=keysb[1 - ks][:], in_=d["keysR"][e_ + 1]),
                    reads=[d["keysR"]], writes=[keysb[1 - ks]], dma=True)
            for tc in range(4):
                pb = tc % 2
                for cj in range(NJ):
                    p.op("tensor", lambda e, tc=tc, cj=cj, ks=ks, pb=pb: e.matmul(
                        psSc[pb][:, :], keysb[ks][:, 128 * cj:128 * (cj + 1)],
                        xnr1[:, cj, 512 * tc:512 * (tc + 1)], start=(cj == 0), stop=(cj == 7)),
                        reads=[keysb[ks], st.xn1T], writes=[psSc[pb]], inc=(cj == 7))
                p.op("scalar", lambda e, tc=tc, pb=pb: e.activation(
                    relscr[:, :], psSc[pb][:, :], AF.Relu),
                    reads=[psSc[pb]], writes=[relscr])
                p.op("tensor", lambda e, e_=e_, tc=tc: e.matmul(
                    psBc[:, :], sel1h[:, 128 * e_:128 * (e_ + 1)],
                    selT[0:16, 512 * tc:512 * (tc + 1)],
                    start=True, stop=True),
                    reads=[sel1h, selT], writes=[psBc])
                p.op("vector", lambda e, e_=e_, tc=tc: e.tensor_mul(
                    scr[:, e_, 512 * tc:512 * (tc + 1)], relscr[:, :], psBc[:, :]),
                    reads=[relscr, psBc], writes=[st.scoresT])


def emit_tail(st, scope_l1):
    p, d = st.p, st.d
    ctr = st.ctxT.ap().rearrange("p (j t) -> p j t", j=NJ)
    scr = st.scoresT.ap().rearrange("p (E t) -> p E t", E=NE1)
    with ExitStack() as sc:
        sb = lambda nm, sh, dt=F32: st.sb(nm, sh, dt, scope=sc)
        wos = sb("wos", [128, NJ * C], F16)
        p.op("gpsimd", lambda e: e.dma_start(
            out=wos.ap().rearrange("p (j o) -> p j o", j=NJ),
            in_=d["wo1"].rearrange("(j p) o -> p j o", p=128)),
            reads=[d["wo1"]], writes=[wos], dma=True)
        valsb = sb("t_vals", [128, NE1 * C], F16)
        for e_ in range(NE1):
            p.op("gpsimd", lambda e, e_=e_: e.dma_start(
                out=valsb[:, C * e_:C * (e_ + 1)], in_=d["values"][e_]),
                reads=[d["values"]], writes=[valsb], dma=True)
        Bo = sb("Bo", [128, C], F32)
        Gt = sb("Gt", [128, C], F32)
        Bt = sb("Bt", [128, C], F32)
        for nm, t in [("Bo_t", Bo), ("Gam_t", Gt), ("Bet_t", Bt)]:
            p.op("sync", lambda e, nm=nm, t=t: e.dma_start(out=t[:], in_=d[nm]),
                 reads=[d[nm]], writes=[t], dma=True)
        pso = [st.ps(f"t_pso{i}", [128, 1024], F32, scope=sc) for i in range(2)]
        psYs = [st.ps(f"t_psY{i}", [128, 1024], F32, scope=sc) for i in range(2)]
        xt = [sb(f"t_x{i}", [128, C], F32) for i in range(4)]
        lnin = sb("t_ln", [128, C], F32)
        sums = sb("t_s", [128, NT], F32)
        sqs = sb("t_q", [128, NT], F32)
        sqscr = sb("t_qs", [128, C], F32)
        g1 = [sb(f"t_g1{i}", [128, C], F16) for i in range(2)]
        s1 = sb("t_s1", [128, C], F16)
        sg = sb("t_sg", [128, C], F16)
        wor = wos.ap().rearrange("p (j o) -> p j o", j=NJ)
        for tt in range(NT):
            pb = tt % 2
            xb = tt % 4
            for hh in range(2):
                for cj in range(NJ):
                    p.op("tensor", lambda e, tt=tt, hh=hh, cj=cj, pb=pb: e.matmul(
                        pso[pb][:, 512 * hh:512 * (hh + 1)], ctr[:, cj, 128 * tt:128 * (tt + 1)],
                        wor[:, cj, 512 * hh:512 * (hh + 1)], start=(cj == 0), stop=(cj == 7)),
                        reads=[st.ctxT, wos], writes=[pso[pb]], inc=(cj == 7 and hh == 1))
            psY = psYs[tt % 2]
            for e_ in range(NE1):
                for hh in range(2):
                    p.op("tensor", lambda e, tt=tt, e_=e_, hh=hh, psY=psY: e.matmul(
                        psY[:, 512 * hh:512 * (hh + 1)],
                        scr[:, e_, 128 * tt:128 * (tt + 1)],
                        valsb[:, C * e_ + 512 * hh:C * e_ + 512 * (hh + 1)],
                        start=(e_ == 0), stop=(e_ == NE1 - 1)),
                        reads=[st.scoresT, valsb], writes=[psY],
                        inc=(e_ == NE1 - 1 and hh == 1))
            p.op("sync", lambda e, tt=tt, xb=xb: e.dma_start(
                out=xt[xb][:], in_=d["x1_d"][128 * tt:128 * (tt + 1), :]),
                reads=[(d["x1_d"], tt)], writes=[xt[xb]], dma=True)
            p.op("vector", lambda e, tt=tt, xb=xb: e.tensor_scalar(
                lnin[:], xt[xb][:], st.rstd1[:, tt:tt + 1], st.nmr1[:, tt:tt + 1], ALU.mult, ALU.add),
                reads=[xt[xb], st.rstd1, st.nmr1], writes=[lnin])
            p.op("vector", lambda e, tt=tt, pb=pb: e.tensor_add(lnin[:], lnin[:], pso[pb][:, :]),
                 reads=[lnin, pso[pb]], writes=[lnin])
            p.op("vector", lambda e: e.tensor_add(lnin[:], lnin[:], Bo[:]),
                 reads=[lnin, Bo], writes=[lnin])
            p.op("vector", lambda e, tt=tt: e.reduce_sum(sums[:, tt:tt + 1], lnin[:], axis=AX),
                 reads=[lnin], writes=[sums])
            p.op("scalar", lambda e, tt=tt: e.activation(sqscr[:], lnin[:], AF.Square,
                                                         accum_out=sqs[:, tt:tt + 1]),
                 reads=[lnin], writes=[sqscr, sqs])
            p.op("vector", lambda e, tt=tt: e.tensor_scalar(
                sums[:, tt:tt + 1], sums[:, tt:tt + 1], 1.0 / C, None, ALU.mult),
                reads=[sums], writes=[sums])
            p.op("vector", lambda e, tt=tt: e.scalar_tensor_tensor(
                sqscr[:, 0:1], sums[:, tt:tt + 1], -1.0, sums[:, tt:tt + 1], ALU.mult, ALU.mult),
                reads=[sums], writes=[sqscr])
            p.op("vector", lambda e, tt=tt: e.scalar_tensor_tensor(
                sqscr[:, 0:1], sqs[:, tt:tt + 1], 1.0 / C, sqscr[:, 0:1], ALU.mult, ALU.add),
                reads=[sqs, sqscr], writes=[sqscr])
            p.op("scalar", lambda e: e.activation(sqscr[:, 0:1], sqscr[:, 0:1], AF.Sqrt, bias=LN_EPS),
                 reads=[sqscr], writes=[sqscr])
            p.op("vector", lambda e: e.reciprocal(sqscr[:, 1:2], sqscr[:, 0:1]),
                 reads=[sqscr], writes=[sqscr])
            p.op("vector", lambda e, tt=tt: e.scalar_tensor_tensor(
                sqscr[:, 2:3], sums[:, tt:tt + 1], -1.0, sqscr[:, 1:2], ALU.mult, ALU.mult),
                reads=[sums, sqscr], writes=[sqscr])
            p.op("vector", lambda e: e.tensor_scalar(
                lnin[:], lnin[:], sqscr[:, 1:2], sqscr[:, 2:3], ALU.mult, ALU.add),
                reads=[lnin, sqscr], writes=[lnin])
            p.op("vector", lambda e: e.tensor_mul(lnin[:], lnin[:], Gt[:]),
                 reads=[lnin, Gt], writes=[lnin])
            p.op("vector", lambda e: e.tensor_add(s1[:], lnin[:], Bt[:]),
                 reads=[lnin, Bt], writes=[s1])
            gb = tt % 2
            p.op("scalar", lambda e, gb=gb, psY=psY: e.activation(g1[gb][:, :], psY[:, :], AF.Silu),
                 reads=[psY], writes=[g1[gb]])
            p.op("vector", lambda e, gb=gb: e.tensor_mul(sg[:, :], s1[:], g1[gb][:, :]),
                 reads=[s1, g1[gb]], writes=[sg])
            p.op("vector", lambda e, xb=xb: e.tensor_add(xt[xb][:], xt[xb][:], sg[:, :]),
                 reads=[sg, xt[xb]], writes=[xt[xb]])
            p.op("sync", lambda e, tt=tt, xb=xb: e.dma_start(
                out=d["out"][128 * tt:128 * (tt + 1), :], in_=xt[xb][:]),
                reads=[xt[xb]], writes=[(d["out"], tt)], dma=True)
        st.dbg_store("s1", s1[:])
    st.p.wait_all("sync")



ALL_PHASES = ("p0", "p1", "p2", "p4", "p5", "p5b", "p6")


def build_program(phases=ALL_PHASES, dbg=()):
    nc = bass.Bass("TRN2", target_bir_lowering=False)
    d = {}

    def din(name, shape, dt=F32):
        d[name] = nc.dram_tensor(name, shape, dt, kind="ExternalInput").ap()

    def dint(name, shape, dt=F32):
        d[name] = nc.dram_tensor(name, shape, dt, kind="Internal").ap()

    din("x_main", [T_LOC, C]); din("x_halo", [2, C])
    din("idn", [128, 128]); din("iota_t", [128, T_LOC]); din("iotar_t", [128, T_LOC])
    din("cst", [128, 6]); din("idn_blk", [128, 64])
    if "p1" in phases:
        for nm in ("Wk", "Wv", "Wr", "Wo"):
            din(nm, [C, C])
        for nm in ("mixk_fm", "mixv_fm", "mixr_fm", "sd_fm", "sf_fm"):
            din(nm, [128, 8])
        din("sd_row", [1, C]); din("qmask", [1, 4]); din("qcoef", [1, 4])
        dint("ccS_in", [128, 16]); dint("ccS_out", [512, 16]); dint("s0_d", [T_LOC, C], F16)
    if "p2" in phases:
        din("gws", [128, 64]); din("W1", [NE0, C, HID]); din("b1_fm", [128, 128])
        din("W2", [NE0, HID, C]); din("b2", [NE0, C])
        dint("x1_d", [T_LOC, C])
    if "p4" in phases:
        din("wqR", [NJ, 128, NJ * 128]); din("wkR", [NJ, 128, NJ * 128]); din("wvR", [NJ, 128, NJ * 128])
        din("bq_fm", [128, 8]); din("bk_fm", [128, 8]); din("bv_fm", [128, 8])
        dint("ccK_in", [C, T_LOC], F16); dint("ccK_out", [4 * C, T_LOC], F16)
        dint("ccV_in", [C, T_LOC], F16); dint("ccV_out", [4 * C, T_LOC], F16)
        dint("q_d", [C, T_LOC], F16)
    if "p5b" in phases:
        din("wo1", [C, C]); din("Bo_t", [128, C]); din("Gam_t", [128, C]); din("Bet_t", [128, C])
    if "p6" in phases:
        din("esel", [NE1, C]); din("keysR", [NE1, 128, NJ * 128]); din("values", [NE1, 128, C])
        din("sel1h", [NE1, NE1 * 128])
        dint("g1_d", [T_LOC, C], F16)
    d["out"] = nc.dram_tensor("out", [T_LOC, C], F32, kind="ExternalOutput").ap()

    dbg_aps = {}
    for name in dbg:
        shape, dt = DBG_SHAPES[name]
        dbg_aps[name] = nc.dram_tensor("dbg_" + name, shape, dt, kind="ExternalOutput").ap()

    with ExitStack() as ctx:
        p = Prog(nc, ctx)
        st = _St(nc, ctx, p, d, dbg_aps)
        p.blk = ctx.enter_context(nc.Block())
        emit_setup(st)
        with ExitStack() as scope_l0:
            emit_l0_alloc(st, scope_l0)
            if "p0" in phases:
                emit_ln0(st)
            if "p1" in phases:
                emit_l0_attn(st)
            if "p2" in phases:
                emit_moe(st)
        with ExitStack() as scope_l1:
            with ExitStack() as scope_q:
                with ExitStack() as scope_x:
                    if "p4" in phases:
                        emit_l1_qkv(st, scope_l1, scope_q, scope_x)
                    if "p6" in phases:
                        emit_l1_ffn_scores(st, scope_x)
                    st.barrier()
                if "p5" in phases:
                    emit_l1_attn(st, scope_q)
            if "p5b" in phases:
                emit_tail(st, scope_l1)
                p.flush()
        if "p6" not in phases:
            zo = st.sb("zo", [128, C], F32)
            p.op("vector", lambda e: e.memset(zo[:], 0.0), writes=[zo])
            for i in range(NT):
                p.op("sync", lambda e, i=i: e.dma_start(
                    out=d["out"][128 * i:128 * (i + 1), :], in_=zo[:]),
                    reads=[zo], writes=[d["out"]], dma=True)
            p.wait_all("sync")
        p.flush()
    return nc



def _fm(vec):
    return np.ascontiguousarray(np.asarray(vec, np.float32).reshape(8, 128).T)


def make_in_maps(inputs, phases=ALL_PHASES):
    ii = {k: np.asarray(v) for k, v in inputs.items()}
    x = ii["L0_attn_x"]
    maps = []
    iota = np.tile(np.arange(T_LOC, dtype=np.float32), (128, 1))
    iotar = (T_LOC - 1.0) - iota
    idn = np.eye(128, dtype=np.float32)
    for g in range(8):
        b, q = g // 4, g % 4
        t0 = T_LOC * q
        m = {}
        m["x_main"] = x[b, t0:t0 + T_LOC]
        halo = np.zeros((2, C), np.float32)
        if q > 0:
            halo[0] = x[b, t0 - 1]
        if q < 3:
            halo[1] = x[b, t0 + T_LOC]
        m["x_halo"] = halo
        m["idn"] = idn
        m["iota_t"] = iota
        m["iotar_t"] = iotar
        m["cst"] = np.tile(np.array([[1e-5, -1.0, 1.0 / S_FULL, 0.0, 1.0, 0.125]], np.float32), (128, 1))
        m["idn_blk"] = np.vstack([np.eye(64, dtype=np.float32)] * 2)
        if "p1" in phases:
            m["Wk"] = ii["L0_attn_Wk"]; m["Wv"] = ii["L0_attn_Wv"]
            m["Wr"] = ii["L0_attn_Wr"]; m["Wo"] = ii["L0_attn_Wo"]
            m["mixk_fm"] = _fm(ii["L0_attn_mix_k"]); m["mixv_fm"] = _fm(ii["L0_attn_mix_v"])
            m["mixr_fm"] = _fm(ii["L0_attn_mix_r"])
            m["sd_fm"] = _fm(ii["L0_attn_spatial_decay"])
            m["sf_fm"] = _fm(ii["L0_attn_spatial_first"])
            m["sd_row"] = ii["L0_attn_spatial_decay"].reshape(1, C)
            qmask = np.zeros((1, 4), np.float32); qmask[0, :q] = 1.0
            qcoef = np.zeros((1, 4), np.float32)
            for r in range(q):
                qcoef[0, r] = float(T_LOC * (q - 1 - r))
            m["qmask"] = qmask; m["qcoef"] = qcoef
        if "p2" in phases:
            gw = ii["L0_ffn_gate_W"]
            m["gws"] = gw.reshape(8, 128, 8).transpose(1, 0, 2).reshape(128, 64)
            m["W1"] = ii["L0_ffn_W1"]; m["W2"] = ii["L0_ffn_W2"]
            m["b1_fm"] = ii["L0_ffn_b1"].reshape(8, 16, 128).transpose(2, 0, 1).reshape(128, 128)
            m["b2"] = ii["L0_ffn_b2"]
        if "p4" in phases:
            def _wR(w):
                return np.ascontiguousarray(
                    np.asarray(w, np.float32).reshape(8, 128, 8, 128).transpose(2, 1, 0, 3).reshape(8, 128, 1024))
            m["wqR"] = _wR(ii["L1_attn_wq"]); m["wkR"] = _wR(ii["L1_attn_wk"]); m["wvR"] = _wR(ii["L1_attn_wv"])
            m["bq_fm"] = _fm(ii["L1_attn_bq"]); m["bk_fm"] = _fm(ii["L1_attn_bk"])
            m["bv_fm"] = _fm(ii["L1_attn_bv"])
        if "p5b" in phases:
            m["wo1"] = ii["L1_attn_wo"]
            m["Bo_t"] = np.tile(np.asarray(ii["L1_attn_bo"], np.float32).reshape(1, C), (128, 1))
            m["Gam_t"] = np.tile(np.asarray(ii["L1_attn_gamma"], np.float32).reshape(1, C), (128, 1))
            m["Bet_t"] = np.tile(np.asarray(ii["L1_attn_beta"], np.float32).reshape(1, C), (128, 1))
        if "p6" in phases:
            m["esel"] = ii["L1_ffn_expert_sel"]
            m["keysR"] = ii["L1_ffn_keys"].reshape(NE1, 8, 128, 128).transpose(0, 2, 1, 3).reshape(NE1, 128, 1024)
            m["values"] = ii["L1_ffn_values"]
            m["sel1h"] = np.repeat(np.eye(NE1, dtype=np.float32), 128, axis=1)
        maps.append({k: np.ascontiguousarray(np.asarray(v, np.float32)) for k, v in m.items()})
    return maps


def kernel(**inputs):
    nc = build_program(ALL_PHASES)
    in_maps = make_in_maps(inputs, ALL_PHASES)
    res = run_bass_kernel_spmd(nc, in_maps, core_ids=list(range(8)))
    out = np.zeros((2, S_FULL, C), np.float32)
    for g in range(8):
        b, q = g // 4, g % 4
        out[b, T_LOC * q:T_LOC * (q + 1)] = res.results[g]["out"]
    return out
